# version 10
# speedup vs baseline: 1.5827x; 1.0483x over previous
import sys
import numpy as np
import ml_dtypes

import concourse.bass as bass
import concourse.mybir as mybir
from concourse.alu_op_type import AluOpType
from concourse.bass_utils import run_bass_kernel_spmd

F32 = mybir.dt.float32
BF16 = mybir.dt.bfloat16
I32 = mybir.dt.int32
I16 = mybir.dt.int16
U16 = mybir.dt.uint16
AF = mybir.ActivationFunctionType
AX = mybir.AxisListType
OP = AluOpType

D = 1024
FF = 3072
NH = 16
HD = 64
NE = 8
EH = 256
RK = 64
NCORES = 8

ENGINES = ('sync', 'gpsimd', 'vector', 'scalar', 'tensor')


class RecInst:
    __slots__ = ('incs',)
    def __init__(self):
        self.incs = []
    def then_inc(self, sem, n=1):
        self.incs.append((sem, n))
        return self


class Rec:
    _CTRL = frozenset(('wait_ge', 'sem_inc', 'nop'))
    def __init__(self, q):
        self.__dict__['_q'] = q
        self.__dict__['last'] = None
    def __getattr__(self, name):
        q = self._q
        ctrl = name in Rec._CTRL
        rec = self
        def call(*a, **k):
            ri = RecInst()
            q.append((name, a, k, ri))
            if not ctrl:
                rec.__dict__['last'] = (name, ri)
            return ri
        return call


class Ctx:
    def __init__(self, nc, seq, ext_in, ext_out, n_cores=NCORES):
        self.nc = nc
        self.seq = seq
        self.n_cores = n_cores
        self.ntile = seq // 128
        self.nqg = seq // 512
        self.q = {e: [] for e in ENGINES}
        self.rec = {e: Rec(self.q[e]) for e in ENGINES}
        self.cnt = {}
        self.aux = {}
        self._sems = {}
        self._seminfo = {}
        self._clock = 0
        self._semidx = 0
        self.T = {}
        self.P = {}
        self.Dr = {}
        self.ext_in = ext_in
        self.ext_out = ext_out
        self.consts = {}
        self._nbar = 0

    @property
    def sy(self): return self.rec['sync']
    @property
    def g(self): return self.rec['gpsimd']
    @property
    def v(self): return self.rec['vector']
    @property
    def s(self): return self.rec['scalar']
    @property
    def t(self): return self.rec['tensor']


    def sem(self, name):
        info = self._seminfo.setdefault(name, dict(first=None, last=None, dma=False))
        self._use(name)
        return ('SEM', name)

    def _use(self, name):
        self._clock += 1
        info = self._seminfo[name]
        if info['first'] is None:
            info['first'] = self._clock
        info['last'] = self._clock

    def tick(self, ri, name, n=1):
        self.sem(name)
        if n >= 16:
            self._seminfo[name]['dma'] = True
        ri.then_inc(('SEM', name), n)
        self.cnt[name] = self.cnt.get(name, 0) + n
        return self.cnt[name]

    def n(self, name):
        self.sem(name)
        return self.cnt.get(name, 0)

    def _resolve_sems(self):
        names = sorted(self._seminfo, key=lambda nm: self._seminfo[nm]['first'])
        phys = []
        n2h, n2off = {}, {}
        for nm in names:
            info = self._seminfo[nm]
            total = self.cnt.get(nm, 0)
            if info['dma']:
                h = self.nc.semaphore(f'sd_{nm}').__enter__()
                n2h[nm], n2off[nm] = h, 0
                continue
            slot = None
            for p in phys:
                if p['last'] < info['first']:
                    slot = p
                    break
            if slot is None:
                slot = dict(handle=self.nc.semaphore(f'sp{len(phys)}').__enter__(),
                            last=-1, total=0)
                phys.append(slot)
            n2h[nm] = slot['handle']
            n2off[nm] = slot['total']
            slot['total'] += total
            slot['last'] = info['last']
        return n2h, n2off

    def barrier(self, engines=ENGINES):
        self._nbar += 1
        snap = getattr(self, '_bar_snap', {})
        diff = [(nm, v) for nm, v in self.cnt.items()
                if v > 0 and snap.get(nm, -1) != v]
        self._bar_snap = dict(self.cnt)
        for e in engines:
            for nm, v in diff:
                self.sem(nm)
                self.rec[e].wait_ge(('SEM', nm), v)

    VIEW_MAP = {
        'F0':      ('SC0', 0, 2048, 'f32'),
        'F1':      ('SC0', 2048, 4096, 'f32'),
        'F2':      ('SC1', 0, 2048, 'f32'),
        'F3':      ('SC1', 2048, 4096, 'f32'),
        'I0':      ('SC1', 4096, 6144, 'i32'),
        'p3_i16':  ('SC0', 4096, 5120, 'i16'),
        'p3_u16':  ('SC0', 5120, 6144, 'u16'),
        'p4_i16b': ('SC0', 6144, 7168, 'i16'),
        'p4b_x':   ('SC0', 7168, 8192, 'f32'),
        'B3':      ('SC0', 4096, 8192, 'bf16'),
    }
    _VDT = {'f32': F32, 'i32': I32, 'i16': I16, 'u16': U16, 'bf16': BF16}

    def base_tile(self, name):
        if name not in self.T:
            self.T[name] = self.nc.alloc_sbuf_tensor(f't_{name}', [128, 8192 if name == 'SC0' else 6144], F32)
        return self.T[name]

    def tile(self, name, shape, dtype):
        vm = self.VIEW_MAP.get(name)
        if vm is not None:
            base, c0, c1, dts = vm
            bt = self.base_tile(base)
            ap = bt[:, c0:c1]
            tgt = self._VDT[dts]
            if tgt != F32:
                ap = ap.bitcast(tgt)
            return ap
        if name not in self.T:
            self.T[name] = self.nc.alloc_sbuf_tensor(f't_{name}', list(shape), dtype)
        return self.T[name]

    def psum(self, name, shape, dtype=F32):
        if name not in self.P:
            self.P[name] = self.nc.alloc_psum_tensor(f'p_{name}', list(shape), dtype)
        return self.P[name]

    def dram(self, name, shape, dtype):
        if name not in self.Dr:
            if name in self.ext_in:
                kind = 'ExternalInput'
            elif name in self.ext_out:
                kind = 'ExternalOutput'
            else:
                kind = 'Internal'
            self.Dr[name] = self.nc.dram_tensor(name, list(shape), dtype, kind=kind)
        return self.Dr[name]

    def const(self, name, arr):
        if name not in self.consts:
            self.consts[name] = self.nc.inline_tensor(np.ascontiguousarray(arr), f'c_{name}')
        return self.consts[name]

    KCONST_VALS = (0.5, 1e-5, 1.0, -0.5)

    def kbias(self, val):
        t = self.tile('kconst', [128, 8], F32)
        i = self.KCONST_VALS.index(val)
        return t[:, i:i + 1]

    def prologue(self):
        t = self.tile('kconst', [128, 8], F32)
        arr = np.zeros((128, 8), np.float32)
        for i, v in enumerate(self.KCONST_VALS):
            arr[:, i] = v
        h = self.const('kconst', arr)
        r = self.sy.dma_start(out=t[:], in_=h.ap())
        self.tick(r, 'kc', 16)
        for e in ENGINES:
            self.rec[e].wait_ge(self.sem('kc'), 16)

    def replay(self):
        n2h, n2off = self._resolve_sems()

        def xl(v):
            if isinstance(v, tuple) and len(v) == 2 and v[0] == 'SEM':
                return n2h[v[1]]
            return v

        with self.nc.Block() as block:
            for ename in ENGINES:
                qq = self.q[ename]
                def mk(qq):
                    def body(eng):
                        for name, a, k, ri in qq:
                            if name == 'wait_ge':
                                semref, val = a[0], a[1]
                                nm = semref[1]
                                eng.wait_ge(n2h[nm], val + n2off[nm])
                                continue
                            a = tuple(xl(x) for x in a)
                            k = {kk: xl(vv) for kk, vv in k.items()}
                            r = getattr(eng, name)(*a, **k)
                            for semref, n in ri.incs:
                                r.then_inc(xl(semref), n)
                    return body
                getattr(block, ename)(mk(qq))



def tile_b1(c):
    return c.tile('B1', [128, max(c.ntile * 1040, 32768)], BF16)


def ident_bf16(c):
    return c.const('id_bf16', np.eye(128, dtype=ml_dtypes.bfloat16))

def ident_f32(c):
    return c.const('id_f32', np.eye(128, dtype=np.float32))

def ones_row(c):
    return c.const('ones_row', np.ones((1, 128), dtype=np.float32))


def qdq_round(c, pre, u_tile, i32_tile, cfp_tile, m_tile, out_tile, src_ap, rs_ap,
              vsem, psem, asem, clip_lo=None, clip_hi=None, out_scale=None,
              out_slice=None, extra_u_bias=0.5):
    v, s, g = c.v, c.s, c.g
    r = v.tensor_scalar(out=u_tile, in0=src_ap, scalar1=rs_ap, scalar2=extra_u_bias,
                        op0=OP.mult, op1=OP.add)
    nu = c.tick(r, vsem)
    s.wait_ge(c.sem(vsem), nu)
    r = s.activation(out=i32_tile, in_=u_tile, func=AF.Identity)
    na = c.tick(r, asem)
    g.wait_ge(c.sem(asem), na)
    r = g.tensor_copy(out=cfp_tile, in_=i32_tile)
    np_ = c.tick(r, psem)
    v.wait_ge(c.sem(psem), np_)
    r = v.tensor_tensor(out=m_tile, in0=cfp_tile, in1=u_tile, op=OP.is_gt)
    nu = c.tick(r, vsem)
    v.wait_ge(c.sem(vsem), nu)
    tgt = out_tile if out_slice is None else out_slice
    if clip_lo is None and out_scale is None:
        r = v.scalar_tensor_tensor(out=tgt, in0=m_tile, scalar=-1.0, in1=cfp_tile,
                                   op0=OP.mult, op1=OP.add)
        nu = c.tick(r, vsem)
        return nu
    r = v.scalar_tensor_tensor(out=m_tile, in0=m_tile, scalar=-1.0, in1=cfp_tile,
                               op0=OP.mult, op1=OP.add)
    nu = c.tick(r, vsem)
    v.wait_ge(c.sem(vsem), nu)
    if clip_lo is not None and out_scale is None:
        r = v.tensor_scalar(out=tgt, in0=m_tile, scalar1=clip_lo, scalar2=clip_hi,
                            op0=OP.max, op1=OP.min)
        nu = c.tick(r, vsem)
        return nu
    if clip_lo is not None:
        r = v.tensor_scalar(out=m_tile, in0=m_tile, scalar1=clip_lo, scalar2=clip_hi,
                            op0=OP.max, op1=OP.min)
        nu = c.tick(r, vsem)
        v.wait_ge(c.sem(vsem), nu)
    r = v.tensor_scalar(out=tgt, in0=m_tile, scalar1=out_scale, scalar2=None,
                        op0=OP.mult)
    nu = c.tick(r, vsem)
    return nu



def phase_w0(c):
    nc = c.nc
    names = [('L0_attn_Wq', 'wT_l0q'), ('L0_attn_Wk', 'wT_l0k'),
             ('L0_attn_Wv', 'wT_l0v'), ('L0_attn_Wo', 'wT_l0o'),
             ('L1_attn_Wq', 'wT_l1q'), ('L1_attn_Wk', 'wT_l1k'),
             ('L1_attn_Wv', 'wT_l1v'), ('L1_attn_Wfc', 'wT_l1fc')]
    b0 = c.tile('B0', [128, 32768], BF16)
    wbf = b0[:, 0:8192]
    ev = b0[:, 8192:16384]
    svb = c.tile('w0_svb', [128, 16], F32)
    sv_row = c.tile('w0_svr', [1, 16], F32)
    scr1 = c.tile('w0_s1', [1, 16], F32)
    idb = ident_bf16(c)
    onr = ones_row(c)
    idt = c.tile('w0_id', [128, 128], BF16)
    pT = c.psum('pA', [128, 2048], F32)
    pB = c.psum('pB', [128, 1024], F32)
    nct = c.dram('L0_attn_norm_const', [1, 16], F32)

    seqc = c.const('seqval', np.full((1, 1), float(c.seq), np.float32))
    seqt = c.tile('w0_seq', [1, 1], F32)
    lnt = c.tile('w0_ln', [1, 1], F32)
    onrt = c.tile('w0_onr', [1, 128], F32)
    r = c.sy.dma_start(out=idt[:], in_=idb.ap()); c.tick(r, 'w0_ld', 16)
    r = c.sy.dma_start(out=sv_row[:], in_=nct.ap()); c.tick(r, 'w0_ld', 16)
    r = c.sy.dma_start(out=seqt[:], in_=seqc.ap()); c.tick(r, 'w0_ld', 16)
    r = c.sy.dma_start(out=onrt[:], in_=onr.ap()); c.tick(r, 'w0_ld', 16)
    c.s.wait_ge(c.sem('w0_ld'), 64)
    r = c.s.activation(out=scr1[:], in_=sv_row[:], func=AF.Sigmoid)
    c.tick(r, 'w0_a')
    r = c.s.activation(out=lnt[:], in_=seqt[:], func=AF.Ln)
    c.tick(r, 'w0_a')
    c.v.wait_ge(c.sem('w0_a'), 2)
    r = c.v.tensor_scalar(out=scr1[:], in0=scr1[:], scalar1=lnt[0:1, 0:1], scalar2=-1.0,
                          op0=OP.mult, op1=OP.mult)
    c.tick(r, 'w0_v')
    c.s.wait_ge(c.sem('w0_v'), 1)
    r = c.s.activation(out=sv_row[:], in_=scr1[:], func=AF.Exp)
    c.tick(r, 'w0_a')
    c.t.wait_ge(c.sem('w0_a'), 3)
    c.t.wait_ge(c.sem('w0_ld'), 64)
    r = c.t.matmul(out=pB[0:128, 0:16], lhsT=onrt[:], rhs=sv_row[:], start=True, stop=True)
    c.tick(r, 'w0_t')
    c.v.wait_ge(c.sem('w0_t'), 1)
    r = c.v.tensor_copy(out=svb[:], in_=pB[0:128, 0:16])
    c.tick(r, 'w0_v')

    for wi, (src, dst) in enumerate(names):
        wd = c.dram(src, [D, D], F32)
        wt = c.dram(dst, [D, D], BF16)
        r = c.g.dma_start(out=wbf.rearrange('p (a d) -> p a d', a=8), in_=wd.ap().rearrange('(a p) d -> p a d', p=128))
        c.tick(r, 'w0_gld', 16)
        c.v.wait_ge(c.sem('w0_st'), 16 * wi)
        c.t.wait_ge(c.sem('w0_gld'), c.n('w0_gld'))
        if src == 'L0_attn_Wv':
            c.v.wait_ge(c.sem('w0_gld'), c.n('w0_gld'))
            for a in range(8):
                for hh in range(2):
                    h = 2 * a + hh
                    r = c.v.tensor_scalar(
                        out=wbf[hh * 64:(hh + 1) * 64, a * 1024:(a + 1) * 1024],
                        in0=wbf[hh * 64:(hh + 1) * 64, a * 1024:(a + 1) * 1024],
                        scalar1=svb[hh * 64:(hh + 1) * 64, h:h + 1], scalar2=None,
                        op0=OP.mult)
                    c.tick(r, 'w0_v')
            c.t.wait_ge(c.sem('w0_v'), c.n('w0_v'))
        for a in range(8):
            for b in range(8):
                g_idx = wi * 64 + a * 8 + b
                bank = g_idx % 2
                pv_view = pT[:, bank * 1024:bank * 1024 + 128].bitcast(BF16)[:, 0:128]
                if g_idx >= 2:
                    c.t.wait_ge(c.sem('w0_ev'), c.n('w0_ev') - 1)
                r = c.t.transpose(out=pv_view, in_=wbf[:, a * 1024 + b * 128: a * 1024 + b * 128 + 128],
                                  identity=idt[:])
                nt = c.tick(r, 'w0_t')
                c.v.wait_ge(c.sem('w0_t'), nt)
                r = c.v.tensor_copy(out=ev[:, b * 1024 + a * 128: b * 1024 + a * 128 + 128],
                                    in_=pv_view)
                c.tick(r, 'w0_ev')
        c.sy.wait_ge(c.sem('w0_ev'), c.n('w0_ev'))
        r = c.sy.dma_start(out=wt.ap().rearrange('(b p) o -> p b o', p=128), in_=ev.rearrange('p (b o) -> p b o', b=8))
        c.tick(r, 'w0_st', 16)
        c.g.wait_ge(c.sem('w0_t'), c.n('w0_t'))
    c.sy.wait_ge(c.sem('w0_st'), c.n('w0_st'))



def phase_w1(c):
    f0 = c.tile('F0', [128, 2048], F32)
    f1 = c.tile('F1', [128, 2048], F32)
    f2 = c.tile('F2', [128, 2048], F32)
    f3 = c.tile('F3', [128, 2048], F32)
    i0 = c.tile('I0', [128, 2048], I32)
    b0 = c.tile('B0', [128, 32768], BF16)
    dq = b0[:, 0:8192]
    ev = b0[:, 8192:16384]
    idt = c.tile('w0_id', [128, 128], BF16)
    onrt = c.tile('w0_onr', [1, 128], F32)
    bm = c.tile('w1_bm', [128, 96], F32)
    sblk = c.tile('w1_s', [128, 96], F32)
    rsblk = c.tile('w1_rs', [128, 96], F32)
    sc7 = c.tile('w1_7', [1, 1], F32)
    r7 = c.tile('w1_r7', [1, 1], F32)
    r7b = c.tile('w1_r7b', [128, 1], F32)
    pT = c.psum('pA', [128, 2048], F32)
    pB = c.psum('pB', [128, 1024], F32)

    c7 = c.const('c7', np.full((1, 1), 7.0, np.float32))
    r = c.sy.dma_start(out=sc7[:], in_=c7.ap()); c.tick(r, 'w1_ld', 16)
    r = c.sy.dma_start(out=idt[:], in_=ident_bf16(c).ap()); c.tick(r, 'w1_ld', 16)
    r = c.sy.dma_start(out=onrt[:], in_=ones_row(c).ap()); c.tick(r, 'w1_ld', 16)
    c.v.wait_ge(c.sem('w1_ld'), 48)
    r = c.v.reciprocal(out=r7[:], in_=sc7[:]); c.tick(r, 'w1_v')
    c.t.wait_ge(c.sem('w1_v'), c.n('w1_v'))
    r = c.t.matmul(out=pB[0:128, 0:1], lhsT=onrt[:], rhs=r7[:], start=True, stop=True)
    c.tick(r, 'w1_t')
    c.v.wait_ge(c.sem('w1_t'), c.n('w1_t'))
    r = c.v.tensor_copy(out=r7b[:], in_=pB[0:128, 0:1]); c.tick(r, 'w1_v')

    specs = [('L0_ffn_w_up', 'wT_up', FF, D), ('L0_ffn_w_gate', 'wT_gate', FF, D),
             ('L0_ffn_w_down', 'wT_down', D, FF)]
    for src, dst, O, I in specs:
        wd = c.dram(src, [O, I], F32)
        wt = c.dram(dst, [I, O], BF16)
        nob = O // 128
        for ti in range(nob):
            ics = [(0, 2048), (2048, 1024)] if I == 3072 else [(0, 1024)]
            for (i_off, i_w) in ics:
                nb = i_w // 32
                c.sy.wait_ge(c.sem('w1_v'), c.n('w1_v'))
                r = c.sy.dma_start(out=f0[:, 0:i_w],
                                   in_=wd.ap()[ti * 128:(ti + 1) * 128, i_off:i_off + i_w])
                nld = c.tick(r, 'w1_ld', 16)
                c.v.wait_ge(c.sem('w1_ld'), nld)
                r = c.v.reduce_max(out=bm[:, 0:nb], in_=f0[:, 0:i_w].rearrange('p (b k) -> p b k', k=32),
                                   axis=AX.X, apply_absolute_value=True)
                nv = c.tick(r, 'w1_v')
                c.v.wait_ge(c.sem('w1_v'), nv)
                r = c.v.tensor_scalar(out=sblk[:, 0:nb], in0=bm[:, 0:nb], scalar1=r7b[:, 0:1],
                                      scalar2=1e-12, op0=OP.mult, op1=OP.max)
                nv = c.tick(r, 'w1_v')
                c.v.wait_ge(c.sem('w1_v'), nv)
                r = c.v.reciprocal(out=rsblk[:, 0:nb], in_=sblk[:, 0:nb])
                nv = c.tick(r, 'w1_v')
                c.v.wait_ge(c.sem('w1_v'), nv)
                r = c.v.tensor_tensor(out=f1[:, 0:i_w].rearrange('p (b k) -> p b k', k=32),
                                      in0=f0[:, 0:i_w].rearrange('p (b k) -> p b k', k=32),
                                      in1=rsblk[:, 0:nb].unsqueeze(2).broadcast_to([128, nb, 32]),
                                      op=OP.mult)
                nv = c.tick(r, 'w1_v')
                c.s.wait_ge(c.sem('w1_v'), nv)
                r = c.s.activation(out=i0[:, 0:i_w], in_=f1[:, 0:i_w], func=AF.Identity, bias=c.kbias(0.5))
                na = c.tick(r, 'w1_a')
                c.g.wait_ge(c.sem('w1_a'), na)
                r = c.g.tensor_copy(out=f2[:, 0:i_w], in_=i0[:, 0:i_w])
                npl = c.tick(r, 'w1_p')
                c.v.wait_ge(c.sem('w1_p'), npl)
                r = c.v.scalar_tensor_tensor(out=f3[:, 0:i_w], in0=f2[:, 0:i_w], scalar=-0.5,
                                             in1=f1[:, 0:i_w], op0=OP.add, op1=OP.is_gt)
                nv = c.tick(r, 'w1_v')
                c.v.wait_ge(c.sem('w1_v'), nv)
                r = c.v.scalar_tensor_tensor(out=f1[:, 0:i_w], in0=f3[:, 0:i_w], scalar=-1.0,
                                             in1=f2[:, 0:i_w], op0=OP.mult, op1=OP.add)
                nv = c.tick(r, 'w1_v')
                c.v.wait_ge(c.sem('w1_v'), nv)
                r = c.v.tensor_scalar(out=f1[:, 0:i_w], in0=f1[:, 0:i_w], scalar1=-8.0,
                                      scalar2=7.0, op0=OP.max, op1=OP.min)
                nv = c.tick(r, 'w1_v')
                c.v.wait_ge(c.sem('w1_v'), nv)
                r = c.v.tensor_tensor(out=dq[:, i_off:i_off + i_w].rearrange('p (b k) -> p b k', k=32),
                                      in0=f1[:, 0:i_w].rearrange('p (b k) -> p b k', k=32),
                                      in1=sblk[:, 0:nb].unsqueeze(2).broadcast_to([128, nb, 32]),
                                      op=OP.mult)
                c.tick(r, 'w1_v')
            c.t.wait_ge(c.sem('w1_v'), c.n('w1_v'))
            for b in range(I // 128):
                g_idx = c.aux.get('w1_tt', 0)
                c.aux['w1_tt'] = g_idx + 1
                bank = g_idx % 2
                pv = pT[:, bank * 1024:bank * 1024 + 128].bitcast(BF16)[:, 0:128]
                if g_idx >= 2:
                    c.t.wait_ge(c.sem('w1_ev'), c.n('w1_ev') - 1)
                r = c.t.transpose(out=pv, in_=dq[:, b * 128:(b + 1) * 128], identity=idt[:])
                nt = c.tick(r, 'w1_t')
                c.v.wait_ge(c.sem('w1_t'), nt)
                r = c.v.tensor_copy(out=ev[:, b * 128:(b + 1) * 128], in_=pv)
                c.tick(r, 'w1_ev')
            c.g.wait_ge(c.sem('w1_ev'), c.n('w1_ev'))
            r = c.g.dma_start(out=wt.ap().rearrange('(b p) o -> p b o', p=128)[:, :, ti * 128:(ti + 1) * 128],
                              in_=ev[:, 0:I].rearrange('p (b o) -> p b o', o=128))
            nst = c.tick(r, 'w1_st', 16)
            c.v.wait_ge(c.sem('w1_st'), nst)
    c.g.wait_ge(c.sem('w1_st'), c.n('w1_st'))



def layernorm_tiles(c, pre, x_dram, xn_tok_dram, xn_fm_dram, amax_tile=None):
    f0 = c.tile('F0', [128, 2048], F32)
    f1 = c.tile('F1', [128, 2048], F32)
    f2 = c.tile('F2', [128, 2048], F32)
    st = c.tile('p0_st', [128, 16], F32)
    xnb = c.tile('B3', None, None)[:, 0:4096]
    stg = c.tile('B2', [128, 4096], BF16)
    idt = c.tile('w0_id', [128, 128], BF16)
    pT = c.psum('pA', [128, 2048], F32)
    ld, st_s, vs, as_, ts_, evs, gs = (pre + x for x in ('_ld', '_st', '_v', '_a', '_t', '_ev', '_g'))

    r = c.sy.dma_start(out=idt[:], in_=ident_bf16(c).ap()); c.tick(r, ld, 16)
    if amax_tile is not None:
        r = c.v.memset(amax_tile, 0.0); c.tick(r, vs)
    H = dict(nld={}, nsub={}, nsq={}, nsqrt={}, nmul={}, nbf={}, ntr={}, nst={})

    def stage_a(i):
        s = i % 2
        xa = f0[:, s * 1024:(s + 1) * 1024]
        if i >= 2:
            c.sy.wait_ge(c.sem(vs), H['nsub'][i - 2])
        r = c.sy.dma_start(out=xa, in_=x_dram.ap().rearrange('(t p) d -> t p d', p=128)[i])
        H['nld'][i] = nld = c.tick(r, ld + str(s), 16)
        c.v.wait_ge(c.sem(ld + str(s)), nld)
        r = c.v.reduce_sum(out=st[:, s * 8:s * 8 + 1], in_=xa, axis=AX.X); nv = c.tick(r, vs)
        c.v.wait_ge(c.sem(vs), nv)
        r = c.v.tensor_scalar(out=st[:, s * 8 + 1:s * 8 + 2], in0=st[:, s * 8:s * 8 + 1],
                              scalar1=1.0 / 1024, scalar2=None, op0=OP.mult); nv = c.tick(r, vs)
        c.v.wait_ge(c.sem(vs), nv)
        if i >= 2:
            c.v.wait_ge(c.sem(as_), H['nsq'][i - 2])
        r = c.v.tensor_scalar(out=f1[:, s * 1024:(s + 1) * 1024], in0=xa,
                              scalar1=st[:, s * 8 + 1:s * 8 + 2], scalar2=None,
                              op0=OP.subtract)
        H['nsub'][i] = c.tick(r, vs)
        c.s.wait_ge(c.sem(vs), H['nsub'][i])
        if xn_tok_dram is not None and i >= 2:
            c.s.wait_ge(c.sem(st_s + str(s)), H['nst'][i - 2])
        r = c.s.activation(out=f2[:, s * 1024:(s + 1) * 1024], in_=f1[:, s * 1024:(s + 1) * 1024],
                           func=AF.Square, accum_out=st[:, s * 8 + 2:s * 8 + 3])
        H['nsq'][i] = na = c.tick(r, as_)
        c.s.wait_ge(c.sem(as_), na)
        r = c.s.activation(out=st[:, s * 8 + 3:s * 8 + 4], in_=st[:, s * 8 + 2:s * 8 + 3],
                           func=AF.Sqrt, scale=1.0 / 1024, bias=c.kbias(1e-5))
        H['nsqrt'][i] = c.tick(r, as_)

    def stage_b(i):
        s = i % 2
        c.v.wait_ge(c.sem(as_), H['nsqrt'][i])
        r = c.v.reciprocal(out=st[:, s * 8 + 4:s * 8 + 5], in_=st[:, s * 8 + 3:s * 8 + 4])
        nv = c.tick(r, vs)
        c.v.wait_ge(c.sem(vs), nv)
        r = c.v.tensor_scalar(out=f2[:, s * 1024:(s + 1) * 1024],
                              in0=f1[:, s * 1024:(s + 1) * 1024],
                              scalar1=st[:, s * 8 + 4:s * 8 + 5], scalar2=None, op0=OP.mult)
        H['nmul'][i] = nv = c.tick(r, vs)
        if xn_tok_dram is not None:
            c.sy.wait_ge(c.sem(vs), nv)
            r = c.sy.dma_start(out=xn_tok_dram.ap().rearrange('(t p) d -> t p d', p=128)[i],
                               in_=f2[:, s * 1024:(s + 1) * 1024])
            H['nst'][i] = c.tick(r, st_s + str(s), 16)
        if amax_tile is not None:
            c.v.wait_ge(c.sem(vs), nv)
            r = c.v.reduce_max(out=st[:, s * 8 + 5:s * 8 + 6], in_=f2[:, s * 1024:(s + 1) * 1024],
                               axis=AX.X, apply_absolute_value=True); nv = c.tick(r, vs)
            c.v.wait_ge(c.sem(vs), nv)
            r = c.v.tensor_tensor(out=amax_tile, in0=amax_tile, in1=st[:, s * 8 + 5:s * 8 + 6],
                                  op=OP.max)
            nv = c.tick(r, vs)
        c.v.wait_ge(c.sem(vs), nv)
        if i >= 2:
            c.v.wait_ge(c.sem(ts_), H['ntr'][i - 2])
        r = c.v.tensor_copy(out=xnb[:, s * 1024:(s + 1) * 1024],
                            in_=f2[:, s * 1024:(s + 1) * 1024])
        H['nbf'][i] = nv = c.tick(r, vs)
        c.t.wait_ge(c.sem(vs), nv)
        if i == 0:
            c.t.wait_ge(c.sem(ld), 16)
        col = (i % 4) * 128
        for b in range(8):
            g_idx = c.aux.get(pre + '_tt', 0)
            c.aux[pre + '_tt'] = g_idx + 1
            bank = g_idx % 2
            pv = pT[:, bank * 1024:bank * 1024 + 128].bitcast(BF16)[:, 0:128]
            if g_idx >= 2:
                c.t.wait_ge(c.sem(evs), c.n(evs) - 1)
            r = c.t.transpose(out=pv, in_=xnb[:, s * 1024 + b * 128: s * 1024 + (b + 1) * 128],
                              identity=idt[:])
            H['ntr'][i] = nt = c.tick(r, ts_)
            c.v.wait_ge(c.sem(ts_), nt)
            if i % 4 == 0 and b == 0:
                c.v.wait_ge(c.sem(gs), c.n(gs))
            r = c.v.tensor_copy(out=stg[:, b * 512 + col: b * 512 + col + 128], in_=pv)
            c.tick(r, evs)
        if i % 4 == 3:
            c.g.wait_ge(c.sem(evs), c.n(evs))
            j = i // 4
            r = c.g.dma_start(
                out=xn_fm_dram.ap().rearrange('(a p) s -> p a s', p=128)[:, :, j * 512:(j + 1) * 512],
                in_=stg[:, 0:4096].rearrange('p (a s) -> p a s', s=512))
            c.tick(r, gs, 16)

    for i in range(c.ntile + 1):
        if i < c.ntile:
            stage_a(i)
        if i >= 1:
            stage_b(i - 1)
    c.g.wait_ge(c.sem(gs), c.n(gs))
    if xn_tok_dram is not None:
        c.sy.wait_ge(c.sem(st_s + '0'), c.n(st_s + '0'))
        c.sy.wait_ge(c.sem(st_s + '1'), c.n(st_s + '1'))


def global_absmax_to_scale(c, pre, amax_tile, qmax, s_out, rs_out):
    onrt = c.tile('w0_onr', [1, 128], F32)
    idf = c.tile('p0_idf', [128, 128], F32)
    rowm = c.tile('p0_rowm', [1, 128], F32)
    sc = c.tile('p0_sc', [1, 8], F32)
    qm = c.tile('p0_qm', [1, 1], F32)
    rqm = c.tile('p0_rqm', [1, 1], F32)
    pB = c.psum('pB', [128, 1024], F32)
    cc_in = c.dram(pre + '_ccin', [1, 8], F32)
    cc_out = c.dram(pre + '_ccout', [1, 8], F32)

    r = c.sy.dma_start(out=idf[:], in_=ident_f32(c).ap()); c.tick(r, pre + '_ld2', 16)
    qmc = c.const('qmax_%d' % int(qmax), np.full((1, 1), float(qmax), np.float32))
    r = c.sy.dma_start(out=qm[:], in_=qmc.ap()); c.tick(r, pre + '_ld2', 16)
    r = c.sy.dma_start(out=onrt[:], in_=ones_row(c).ap()); c.tick(r, pre + '_ld2', 16)
    c.t.wait_ge(c.sem(pre + '_ld2'), 48)
    r = c.t.matmul(out=pB[0:1, 0:128], lhsT=amax_tile, rhs=idf[:], start=True, stop=True)
    nt = c.tick(r, pre + '_t2')
    c.v.wait_ge(c.sem(pre + '_t2'), nt)
    r = c.v.reduce_max(out=rowm[0:1, 0:1], in_=pB[0:1, 0:128], axis=AX.X)
    nv = c.tick(r, pre + '_v2')
    c.v.wait_ge(c.sem(pre + '_v2'), nv)
    r = c.v.tensor_copy(out=sc[:], in_=rowm[0:1, 0:1].broadcast_to([1, 8]))
    nv = c.tick(r, pre + '_v2')
    c.g.wait_ge(c.sem(pre + '_v2'), nv)
    r = c.g.dma_start(out=cc_in.ap(), in_=sc[:]); c.tick(r, pre + '_g2', 16)
    c.g.wait_ge(c.sem(pre + '_g2'), c.n(pre + '_g2'))
    r = c.g.collective_compute('AllReduce', OP.max,
                               replica_groups=[list(range(c.n_cores))],
                               ins=[cc_in.ap()], outs=[cc_out.ap()])
    c.tick(r, pre + '_cc')
    c._seminfo[pre + '_cc']['dma'] = True
    c.g.wait_ge(c.sem(pre + '_cc'), 1)
    r = c.g.dma_start(out=sc[:], in_=cc_out.ap()); c.tick(r, pre + '_g2', 16)
    c.v.wait_ge(c.sem(pre + '_g2'), c.n(pre + '_g2'))
    r = c.v.reciprocal(out=rqm[:], in_=qm[:]); nv = c.tick(r, pre + '_v2')
    c.v.wait_ge(c.sem(pre + '_v2'), nv)
    r = c.v.tensor_scalar(out=sc[0:1, 1:2], in0=sc[0:1, 0:1], scalar1=rqm[0:1, 0:1],
                          scalar2=1e-12, op0=OP.mult, op1=OP.max)
    nv = c.tick(r, pre + '_v2')
    c.v.wait_ge(c.sem(pre + '_v2'), nv)
    r = c.v.reciprocal(out=sc[0:1, 2:3], in_=sc[0:1, 1:2]); nv = c.tick(r, pre + '_v2')
    c.t.wait_ge(c.sem(pre + '_v2'), nv)
    r = c.t.matmul(out=pB[0:128, 8:10], lhsT=onrt[:], rhs=sc[0:1, 1:3], start=True, stop=True)
    nt = c.tick(r, pre + '_t2')
    c.v.wait_ge(c.sem(pre + '_t2'), nt)
    r = c.v.tensor_copy(out=s_out, in_=pB[0:128, 8:9]); c.tick(r, pre + '_v2')
    r = c.v.tensor_copy(out=rs_out, in_=pB[0:128, 9:10]); nv = c.tick(r, pre + '_v2')
    return nv


def phase_p0(c):
    x_dram = c.dram('x0', [c.seq, D], F32)
    xn_tok = c.dram('xn_tok_l0', [c.seq, D], F32)
    xn_fm = c.dram('xn_fm_l0', [D, c.seq], BF16)
    xq_fm = c.dram('xq_fm', [D, c.seq], BF16)
    amax = c.tile('p0_amax', [128, 1], F32)
    layernorm_tiles(c, 'p0', x_dram, xn_tok, xn_fm, amax_tile=amax[:])
    c.barrier()
    s1b = c.tile('p0_s1b', [128, 1], F32)
    rs1b = c.tile('p0_rs1b', [128, 1], F32)
    global_absmax_to_scale(c, 'p0c', amax[:], 32767.0, s1b[:], rs1b[:])
    c.barrier()
    f0 = c.tile('F0', [128, 2048], F32)
    f1 = c.tile('F1', [128, 2048], F32)
    f2 = c.tile('F2', [128, 2048], F32)
    f3 = c.tile('F3', [128, 2048], F32)
    i0 = c.tile('I0', [128, 2048], I32)
    xnb = c.tile('B3', None, None)[:, 0:4096]
    stg = c.tile('B2', [128, 4096], BF16)
    idt = c.tile('w0_id', [128, 128], BF16)
    pT = c.psum('pA', [128, 2048], F32)
    Hq = dict(nld={}, nts={}, na={}, np={}, nstt={}, nsc={}, ntr={})

    def qstage_a(i):
        s = i % 2
        if i >= 2:
            c.sy.wait_ge(c.sem('q0_v'), Hq['nts'][i - 2])
        r = c.sy.dma_start(out=f0[:, s * 1024:(s + 1) * 1024],
                           in_=xn_tok.ap().rearrange('(t p) d -> t p d', p=128)[i])
        Hq['nld'][i] = nld = c.tick(r, 'q0_l%d' % s, 16)
        c.v.wait_ge(c.sem('q0_l%d' % s), nld)
        if i >= 2:
            c.v.wait_ge(c.sem('q0_a'), Hq['na'][i - 2])
        r = c.v.tensor_scalar(out=f1[:, s * 1024:(s + 1) * 1024],
                              in0=f0[:, s * 1024:(s + 1) * 1024], scalar1=rs1b[:, 0:1],
                              scalar2=None, op0=OP.mult)
        Hq['nts'][i] = nv = c.tick(r, 'q0_v')
        c.s.wait_ge(c.sem('q0_v'), nv)
        if i >= 2:
            c.s.wait_ge(c.sem('q0_p'), Hq['np'][i - 2])
        r = c.s.activation(out=i0[:, s * 1024:(s + 1) * 1024],
                           in_=f1[:, s * 1024:(s + 1) * 1024], func=AF.Identity, bias=c.kbias(0.5))
        Hq['na'][i] = na = c.tick(r, 'q0_a')
        c.g.wait_ge(c.sem('q0_a'), na)
        if i >= 2:
            c.g.wait_ge(c.sem('q0_v'), Hq['nstt'][i - 2])
        r = c.g.tensor_copy(out=f2[:, s * 1024:(s + 1) * 1024], in_=i0[:, s * 1024:(s + 1) * 1024])
        Hq['np'][i] = c.tick(r, 'q0_p')

    def qstage_b(i):
        s = i % 2
        c.v.wait_ge(c.sem('q0_p'), Hq['np'][i])
        r = c.v.scalar_tensor_tensor(out=f3[:, s * 1024:(s + 1) * 1024],
                                     in0=f2[:, s * 1024:(s + 1) * 1024], scalar=-0.5,
                                     in1=f1[:, s * 1024:(s + 1) * 1024], op0=OP.add, op1=OP.is_gt)
        nv = c.tick(r, 'q0_v')
        c.v.wait_ge(c.sem('q0_v'), nv)
        r = c.v.scalar_tensor_tensor(out=f1[:, s * 1024:(s + 1) * 1024],
                                     in0=f3[:, s * 1024:(s + 1) * 1024], scalar=-1.0,
                                     in1=f2[:, s * 1024:(s + 1) * 1024], op0=OP.mult, op1=OP.add)
        Hq['nstt'][i] = nv = c.tick(r, 'q0_v')
        c.v.wait_ge(c.sem('q0_v'), nv)
        if i >= 2:
            c.v.wait_ge(c.sem('q0_t'), Hq['ntr'][i - 2])
        r = c.v.tensor_scalar(out=xnb[:, s * 1024:(s + 1) * 1024],
                              in0=f1[:, s * 1024:(s + 1) * 1024], scalar1=s1b[:, 0:1],
                              scalar2=None, op0=OP.mult)
        Hq['nsc'][i] = nv = c.tick(r, 'q0_v')
        c.t.wait_ge(c.sem('q0_v'), nv)
        col = (i % 4) * 128
        for b in range(8):
            g_idx = c.aux.get('q0_tt', 0)
            c.aux['q0_tt'] = g_idx + 1
            bank = g_idx % 2
            pv = pT[:, bank * 1024:bank * 1024 + 128].bitcast(BF16)[:, 0:128]
            if g_idx >= 2:
                c.t.wait_ge(c.sem('q0_ev'), c.n('q0_ev') - 1)
            r = c.t.transpose(out=pv, in_=xnb[:, s * 1024 + b * 128: s * 1024 + (b + 1) * 128],
                              identity=idt[:])
            Hq['ntr'][i] = nt = c.tick(r, 'q0_t')
            c.v.wait_ge(c.sem('q0_t'), nt)
            if i % 4 == 0 and b == 0:
                c.v.wait_ge(c.sem('q0_g'), c.n('q0_g'))
            r = c.v.tensor_copy(out=stg[:, b * 512 + col: b * 512 + col + 128], in_=pv)
            c.tick(r, 'q0_ev')
        if i % 4 == 3:
            c.g.wait_ge(c.sem('q0_ev'), c.n('q0_ev'))
            j = i // 4
            r = c.g.dma_start(
                out=xq_fm.ap().rearrange('(a p) s -> p a s', p=128)[:, :, j * 512:(j + 1) * 512],
                in_=stg[:, 0:4096].rearrange('p (a s) -> p a s', s=512))
            c.tick(r, 'q0_g', 16)

    for i in range(c.ntile + 1):
        if i < c.ntile:
            qstage_a(i)
        if i >= 1:
            qstage_b(i - 1)
    c.g.wait_ge(c.sem('q0_g'), c.n('q0_g'))



def _proj_tok(c, pre, xfm, w_sb, a_cols, out_ps, nmm_sem):
    n = 0
    for half in range(2):
        for a in range(8):
            r = c.t.matmul(out=out_ps[:, half * 512:(half + 1) * 512],
                           lhsT=xfm[:, a * 128:(a + 1) * 128],
                           rhs=w_sb[:, a_cols + a * 1024 + half * 512: a_cols + a * 1024 + half * 512 + 512],
                           start=(a == 0), stop=(a == 7))
            n = c.tick(r, nmm_sem)
    return n


def _l2norm_tok(c, pre, src_f32, dst_bf, st, vsem, asem):
    r = c.v.reduce_sum(out=st[:, 0:16], in_=src_f32.rearrange('p (h e) -> p h e', e=HD),
                       axis=AX.X, apply_absolute_value=False)
    return r


def phase_p1(c):
    seq = c.seq
    x0 = c.dram('x0', [seq, D], F32)
    x1 = c.dram('x1', [seq, D], F32)
    xn_fm = c.dram('xn_fm_l0', [D, seq], BF16)
    wk = c.dram('wT_l0k', [D, D], BF16)
    wv = c.dram('wT_l0v', [D, D], BF16)
    wq = c.dram('wT_l0q', [D, D], BF16)
    wo = c.dram('wT_l0o', [D, D], BF16)

    w_sb = c.tile('B0', [128, 32768], BF16)
    sc1_ = c.base_tile('SC1')
    f3bf = sc1_[:, 2048:4096].bitcast(BF16)
    i0bf = sc1_[:, 4096:6144].bitcast(BF16)
    xfm = f3bf[:, 0:1024]
    KB = f3bf[:, 1024:2048]
    VB = f3bf[:, 2048:3072]
    afm = f3bf[:, 3072:4096]
    kvb = i0bf[:, 0:1024]
    f0 = c.tile('F0', [128, 2048], F32)
    f1 = c.tile('F1', [128, 2048], F32)
    f2 = c.tile('F2', [128, 2048], F32)
    st = c.tile('p0_st', [128, 8], F32)
    stq = c.tile('p1_stq', [128, 48], F32)
    idt = c.tile('w0_id', [128, 128], BF16)
    pA = c.psum('pA', [128, 2048], F32)
    pB = c.psum('pB', [128, 1024], F32)
    pC = c.psum('pC', [128, 1024], F32)

    r = c.sy.dma_start(out=idt[:], in_=ident_bf16(c).ap()); c.tick(r, 'p1_w', 16)
    r = c.sy.dma_start(out=w_sb[:, 0:8192].rearrange('p (a o) -> p a o', a=8),
                       in_=wk.ap().rearrange('(a p) o -> p a o', p=128))
    c.tick(r, 'p1_w', 16)
    r = c.sy.dma_start(out=w_sb[:, 8192:16384].rearrange('p (a o) -> p a o', a=8),
                       in_=wv.ap().rearrange('(a p) o -> p a o', p=128))
    c.tick(r, 'p1_w', 16)
    c.t.wait_ge(c.sem('p1_w'), 48)

    def sq_norm_scale(src_f32, dst_bf, col0):
        ss, no, rn = (stq[:, col0:col0 + 16], stq[:, col0 + 16:col0 + 32],
                      stq[:, col0 + 32:col0 + 48])
        c.s.wait_ge(c.sem('p1_v'), c.n('p1_v'))
        c.s.wait_ge(c.sem('p1_g'), c.n('p1_g'))
        r = c.s.activation(out=f1[:, 0:1024], in_=src_f32, func=AF.Square)
        na = c.tick(r, 'p1_a')
        c.v.wait_ge(c.sem('p1_a'), na)
        r = c.v.reduce_sum(out=ss, in_=f1[:, 0:1024].rearrange('p (h e) -> p h e', e=HD),
                           axis=AX.X)
        nv = c.tick(r, 'p1_v')
        c.s.wait_ge(c.sem('p1_v'), nv)
        r = c.s.activation(out=no, in_=ss, func=AF.Sqrt)
        na = c.tick(r, 'p1_a')
        c.v.wait_ge(c.sem('p1_a'), na)
        r = c.v.tensor_scalar(out=no, in0=no, scalar1=1e-12, scalar2=None, op0=OP.max)
        nv = c.tick(r, 'p1_v')
        c.v.wait_ge(c.sem('p1_v'), nv)
        r = c.v.reciprocal(out=rn, in_=no)
        nv = c.tick(r, 'p1_v')
        c.v.wait_ge(c.sem('p1_v'), nv)
        nv = 0
        for h in range(NH):
            r = c.v.tensor_scalar(out=dst_bf[:, h * HD:(h + 1) * HD],
                                  in0=src_f32[:, h * HD:(h + 1) * HD],
                                  scalar1=rn[:, h:h + 1], scalar2=None, op0=OP.mult)
            nv = c.tick(r, 'p1_v')
        return nv

    for i in range(c.ntile):
        c.sy.wait_ge(c.sem('p1_t'), c.n('p1_t'))
        r = c.sy.dma_start(out=xfm.rearrange('p (a w) -> p a w', a=8),
                           in_=xn_fm.ap().rearrange('(a p) s -> p a s', p=128)[:, :, i * 128:(i + 1) * 128])
        nld = c.tick(r, 'p1_ld', 16)
        c.t.wait_ge(c.sem('p1_ld'), nld)
        if i > 0:
            c.t.wait_ge(c.sem('p1_v'), c.n('p1_v'))
            c.t.wait_ge(c.sem('p1_a'), c.n('p1_a'))
        _proj_tok(c, 'p1', xfm, w_sb, 0, pA[:, 0:1024], 'p1_t')
        nmmv = _proj_tok(c, 'p1', xfm, w_sb, 8192, pA[:, 1024:2048], 'p1_t')
        c.v.wait_ge(c.sem('p1_t'), nmmv)
        r = c.v.tensor_copy(out=f0[:, 0:1024], in_=pA[:, 0:1024])
        c.tick(r, 'p1_v')
        c.s.wait_ge(c.sem('p1_t'), nmmv)
        r = c.s.activation(out=VB, in_=pA[:, 1024:2048], func=AF.Identity)
        nva = c.tick(r, 'p1_a')
        c.v.wait_ge(c.sem('p1_v'), c.n('p1_v'))
        nkn = sq_norm_scale(f0[:, 0:1024], KB, 0)
        c.t.wait_ge(c.sem('p1_v'), nkn)
        c.t.wait_ge(c.sem('p1_a'), nva)
        for h in range(NH):
            par = h % 2
            r = c.t.matmul(out=pB[par * 64:par * 64 + 64,
                                  par * 512 + (h // 2) * 64: par * 512 + (h // 2) * 64 + 64],
                           lhsT=KB[:, h * 64:(h + 1) * 64], rhs=VB[:, h * 64:(h + 1) * 64],
                           start=(i == 0 and h in (0, 1)),
                           stop=(i == c.ntile - 1 and h in (14, 15)),
                           tile_position=(0, par * 64))
            c.tick(r, 'p1_t')

    c.v.wait_ge(c.sem('p1_t'), c.n('p1_t'))
    r = c.v.tensor_copy(out=kvb[0:64, 0:512], in_=pB[0:64, 0:512])
    c.tick(r, 'p1_v')
    r = c.v.tensor_copy(out=kvb[64:128, 512:1024], in_=pB[64:128, 512:1024])
    nkv = c.tick(r, 'p1_v')
    c.sy.wait_ge(c.sem('p1_t'), c.n('p1_t'))
    r = c.sy.dma_start(out=w_sb[:, 0:8192].rearrange('p (a o) -> p a o', a=8),
                       in_=wq.ap().rearrange('(a p) o -> p a o', p=128))
    c.tick(r, 'p1_w', 16)
    r = c.sy.dma_start(out=w_sb[:, 8192:16384].rearrange('p (a o) -> p a o', a=8),
                       in_=wo.ap().rearrange('(a p) o -> p a o', p=128))
    nld2 = c.tick(r, 'p1_w', 16)
    c.t.wait_ge(c.sem('p1_w'), nld2)
    c.t.wait_ge(c.sem('p1_v'), nkv)

    for i in range(c.ntile):
        c.sy.wait_ge(c.sem('p1_t'), c.n('p1_t'))
        r = c.sy.dma_start(out=xfm.rearrange('p (a w) -> p a w', a=8),
                           in_=xn_fm.ap().rearrange('(a p) s -> p a s', p=128)[:, :, i * 128:(i + 1) * 128])
        nld = c.tick(r, 'p1_ld', 16)
        c.sy.wait_ge(c.sem('p1_v'), c.n('p1_v'))
        r = c.sy.dma_start(out=f2[:, 0:1024],
                           in_=x0.ap().rearrange('(t p) d -> t p d', p=128)[i])
        nldx = c.tick(r, 'p1_ld', 16)
        c.t.wait_ge(c.sem('p1_ld'), nldx)
        if i > 0:
            c.t.wait_ge(c.sem('p1_v'), c.n('p1_v'))
            c.t.wait_ge(c.sem('p1_a'), c.n('p1_a'))
        nmm = _proj_tok(c, 'p1', xfm, w_sb, 0, pA[:, 0:1024], 'p1_t')
        c.v.wait_ge(c.sem('p1_t'), nmm)
        r = c.v.tensor_copy(out=f0[:, 0:1024], in_=pA[:, 0:1024])
        c.tick(r, 'p1_v')
        c.v.wait_ge(c.sem('p1_v'), c.n('p1_v'))
        nqn = sq_norm_scale(f0[:, 0:1024], KB, 0)
        c.t.wait_ge(c.sem('p1_v'), nqn)
        for b in range(8):
            g_idx = c.aux.get('p1_tt', 0); c.aux['p1_tt'] = g_idx + 1
            bank = g_idx % 2
            pv = pC[:, bank * 512:bank * 512 + 128].bitcast(BF16)[:, 0:128]
            if g_idx >= 2:
                c.t.wait_ge(c.sem('p1_ev'), c.n('p1_ev') - 1)
            r = c.t.transpose(out=pv, in_=KB[:, b * 128:(b + 1) * 128], identity=idt[:])
            nt = c.tick(r, 'p1_t')
            c.v.wait_ge(c.sem('p1_t'), nt)
            r = c.v.tensor_copy(out=afm[:, b * 128:(b + 1) * 128], in_=pv)
            c.tick(r, 'p1_ev')
        c.t.wait_ge(c.sem('p1_ev'), c.n('p1_ev'))
        for h in range(NH):
            par = h % 2
            lhs = kvb[par * 64:par * 64 + 64,
                      par * 512 + (h // 2) * 64: par * 512 + (h // 2) * 64 + 64]
            rhs = afm[(h % 2) * 64:(h % 2) * 64 + 64, (h // 2) * 128:(h // 2) * 128 + 128]
            r = c.t.matmul(out=pA[(h % 2) * 64:(h % 2) * 64 + 64,
                                  1024 + (h // 2) * 128:1024 + (h // 2) * 128 + 128],
                           lhsT=lhs, rhs=rhs, start=True, stop=True,
                           tile_position=((h % 2) * 64, (h % 2) * 64))
            natt = c.tick(r, 'p1_t')
        c.v.wait_ge(c.sem('p1_t'), natt)
        r = c.v.tensor_copy(out=afm, in_=pA[:, 1024:2048])
        nev = c.tick(r, 'p1_v')
        c.t.wait_ge(c.sem('p1_v'), nev)
        for half in range(2):
            for a in range(8):
                r = c.t.matmul(out=pB[:, half * 512:(half + 1) * 512],
                               lhsT=afm[:, a * 128:(a + 1) * 128],
                               rhs=w_sb[:, 8192 + a * 1024 + half * 512: 8192 + a * 1024 + half * 512 + 512],
                               start=(a == 0), stop=(a == 7))
                no = c.tick(r, 'p1_t')
        c.v.wait_ge(c.sem('p1_t'), no)
        c.v.wait_ge(c.sem('p1_a'), c.n('p1_a'))
        r = c.v.tensor_tensor(out=f1[:, 0:1024], in0=f2[:, 0:1024], in1=pB[:, 0:1024],
                              op=OP.add)
        nx1 = c.tick(r, 'p1_v')
        c.g.wait_ge(c.sem('p1_v'), nx1)
        r = c.g.dma_start(out=x1.ap().rearrange('(t p) d -> t p d', p=128)[i],
                          in_=f1[:, 0:1024])
        c.tick(r, 'p1_g', 16)
        c.v.wait_ge(c.sem('p1_g'), c.n('p1_g'))
    c.g.wait_ge(c.sem('p1_g'), c.n('p1_g'))



def _row_to_scal(c, pre, vals_and_dsts):
    pass


def _coll_max(c, pre, src_row_ap, n_items, dst_row_ap):
    cc_in = c.dram(pre + '_ccin', [1, 8], F32)
    cc_out = c.dram(pre + '_ccout', [1, 8], F32)
    r = c.g.dma_start(out=cc_in.ap(), in_=src_row_ap)
    c.tick(r, pre + '_g', 16)
    c.g.wait_ge(c.sem(pre + '_g'), c.n(pre + '_g'))
    r = c.g.collective_compute('AllReduce', OP.max,
                               replica_groups=[list(range(c.n_cores))],
                               ins=[cc_in.ap()], outs=[cc_out.ap()])
    c.tick(r, pre + '_cc')
    c._seminfo[pre + '_cc']['dma'] = True
    c.g.wait_ge(c.sem(pre + '_cc'), 1)
    r = c.g.dma_start(out=dst_row_ap, in_=cc_out.ap())
    return c.tick(r, pre + '_g', 16)


def _xpart_max(c, pre, amax_col_ap, dst_row_ap):
    idf = c.tile('p0_idf', [128, 128], F32)
    pB = c.psum('pB', [128, 1024], F32)
    if not c.aux.get(pre + '_idf'):
        c.aux[pre + '_idf'] = True
        r = c.sy.dma_start(out=idf[:], in_=ident_f32(c).ap()); c.tick(r, pre + '_ld', 16)
    c.t.wait_ge(c.sem(pre + '_ld'), c.n(pre + '_ld'))
    c.t.wait_ge(c.sem(pre + '_v'), c.n(pre + '_v'))
    r = c.t.matmul(out=pB[0:1, 0:128], lhsT=amax_col_ap, rhs=idf[:], start=True, stop=True)
    nt = c.tick(r, pre + '_t')
    c.v.wait_ge(c.sem(pre + '_t'), nt)
    r = c.v.reduce_max(out=dst_row_ap, in_=pB[0:1, 0:128], axis=AX.X)
    return c.tick(r, pre + '_v')


def _bcast_cols(c, pre, src_row_ap, n, dst_tile_ap, psum_off=512):
    onrt = c.tile('w0_onr', [1, 128], F32)
    pB = c.psum('pB', [128, 1024], F32)
    c.t.wait_ge(c.sem(pre + '_v'), c.n(pre + '_v'))
    r = c.t.matmul(out=pB[0:128, psum_off:psum_off + n], lhsT=onrt[:], rhs=src_row_ap,
                   start=True, stop=True)
    nt = c.tick(r, pre + '_t')
    c.v.wait_ge(c.sem(pre + '_t'), nt)
    r = c.v.tensor_copy(out=dst_tile_ap, in_=pB[0:128, psum_off:psum_off + n])
    return c.tick(r, pre + '_v')


def phase_p2(c):
    seq = c.seq
    ncc = seq // 512
    xq_fm = c.dram('xq_fm', [D, seq], BF16)
    scalA = c.dram('scalA', [1, 8], F32)
    w_sb = c.tile('B0', [128, 32768], BF16)
    xqs = c.tile('B2', [128, 4096], BF16)
    am = c.tile('p2_am', [128, 2], F32)
    rc = c.tile('p2_rc', [128, 1], F32)
    row = c.tile('p2_row', [1, 8], F32)
    pA = c.psum('pA', [128, 2048], F32)

    r = c.v.memset(am[:], 0.0); c.tick(r, 'p2_v')
    r = c.v.memset(row[:], 0.0); c.tick(r, 'p2_v')
    nred = {}
    for wi, wname in enumerate(['wT_up', 'wT_gate']):
        wd = c.dram(wname, [D, FF], BF16)
        c.sy.wait_ge(c.sem('p2_t'), c.n('p2_t'))
        r = c.sy.dma_start(out=w_sb[:, 0:8 * FF].rearrange('p (a o) -> p a o', a=8),
                           in_=wd.ap().rearrange('(a p) o -> p a o', p=128))
        nw = c.tick(r, 'p2_w', 16)
        for cc in range(ncc):
            c.sy.wait_ge(c.sem('p2_t'), c.n('p2_t'))
            r = c.sy.dma_start(out=xqs[:].rearrange('p (a s) -> p a s', a=8),
                               in_=xq_fm.ap().rearrange('(a p) s -> p a s', p=128)[:, :, cc * 512:(cc + 1) * 512])
            nld = c.tick(r, 'p2_ld', 16)
            c.t.wait_ge(c.sem('p2_ld'), nld)
            c.t.wait_ge(c.sem('p2_w'), nw)
            for t in range(FF // 128):
                blk = (wi * ncc + cc) * (FF // 128) + t
                bank = blk % 2
                if blk >= 2:
                    c.t.wait_ge(c.sem('p2_v'), nred[blk - 2])
                for a in range(8):
                    r = c.t.matmul(out=pA[:, bank * 512:(bank + 1) * 512],
                                   lhsT=w_sb[:, a * FF + t * 128: a * FF + t * 128 + 128],
                                   rhs=xqs[:, a * 512:(a + 1) * 512],
                                   start=(a == 0), stop=(a == 7))
                    nmm = c.tick(r, 'p2_t')
                c.v.wait_ge(c.sem('p2_t'), nmm)
                c.v.wait_ge(c.sem('p2_v'), c.n('p2_v'))
                r = c.v.reduce_max(out=rc[:], in_=pA[:, bank * 512:(bank + 1) * 512],
                                   axis=AX.X, apply_absolute_value=True)
                nv = c.tick(r, 'p2_v')
                c.v.wait_ge(c.sem('p2_v'), nv)
                r = c.v.tensor_tensor(out=am[:, wi:wi + 1], in0=am[:, wi:wi + 1],
                                      in1=rc[:], op=OP.max)
                nred[blk] = c.tick(r, 'p2_v')
    c.t.wait_ge(c.sem('p2_v'), c.n('p2_v'))
    _xpart_max(c, 'p2x', am[:, 0:1], row[0:1, 0:1])
    _xpart_max(c, 'p2x', am[:, 1:2], row[0:1, 1:2])
    nv = c.n('p2x_v')
    c.g.wait_ge(c.sem('p2x_v'), nv)
    ng = _coll_max(c, 'p2c', row[:], 2, row[:])
    c.g.wait_ge(c.sem('p2c_g'), ng)
    r = c.g.dma_start(out=scalA.ap(), in_=row[:])
    c.tick(r, 'p2c_g', 16)
    c.g.wait_ge(c.sem('p2c_g'), c.n('p2c_g'))


def _ffn_scales(c, pre, upto):
    sc = c.tile(pre + '_sc', [1, 16], F32)
    qm = c.tile('p0_qm', [1, 1], F32)
    rqm = c.tile('p0_rqm', [1, 1], F32)
    out = {}
    scalA = c.dram('scalA', [1, 8], F32)
    r = c.sy.dma_start(out=sc[0:1, 0:8], in_=scalA.ap()); c.tick(r, pre + '_sld', 16)
    qmc = c.const('qmax_32767', np.full((1, 1), 32767.0, np.float32))
    r = c.sy.dma_start(out=qm[:], in_=qmc.ap()); c.tick(r, pre + '_sld', 16)
    onrt = c.tile('w0_onr', [1, 128], F32)
    r = c.sy.dma_start(out=onrt[:], in_=ones_row(c).ap()); c.tick(r, pre + '_sld', 16)
    if upto >= 2:
        scalB = c.dram('scalB', [1, 8], F32)
        r = c.sy.dma_start(out=sc[0:1, 8:10], in_=scalB.ap()[0:1, 0:2]); c.tick(r, pre + '_sld', 16)
    if upto >= 3:
        scalC = c.dram('scalC', [1, 8], F32)
        r = c.sy.dma_start(out=sc[0:1, 10:12], in_=scalC.ap()[0:1, 0:2]); c.tick(r, pre + '_sld', 16)
    c.v.wait_ge(c.sem(pre + '_sld'), c.n(pre + '_sld'))
    r = c.v.reciprocal(out=rqm[:], in_=qm[:]); nv = c.tick(r, pre + '_v')
    c.v.wait_ge(c.sem(pre + '_v'), nv)
    W_ = sc
    def vop(fn):
        nonlocal nv
        r = fn()
        nv = c.tick(r, pre + '_v')
        c.v.wait_ge(c.sem(pre + '_v'), nv)
    vop(lambda: c.v.tensor_scalar(out=W_[0:1, 12:13], in0=W_[0:1, 0:1], scalar1=rqm[0:1, 0:1],
                                  scalar2=1e-12, op0=OP.mult, op1=OP.max))
    vop(lambda: c.v.reciprocal(out=W_[0:1, 13:14], in_=W_[0:1, 12:13]))
    vop(lambda: c.v.tensor_scalar(out=W_[0:1, 14:15], in0=W_[0:1, 1:2], scalar1=rqm[0:1, 0:1],
                                  scalar2=1e-12, op0=OP.mult, op1=OP.max))
    vop(lambda: c.v.reciprocal(out=W_[0:1, 15:16], in_=W_[0:1, 14:15]))
    tiles = {}
    def bc(name, src_ap):
        t = c.tile(pre + '_' + name, [128, 1], F32)
        c.t.wait_ge(c.sem(pre + '_v'), c.n(pre + '_v'))
        _bcast_cols(c, pre + 'b', src_ap, 1, t[:])
        tiles[name] = t
    if upto == 1:
        bc('rs2b', W_[0:1, 13:14])
        bc('rs3b', W_[0:1, 15:16])
        bc('s3b', W_[0:1, 14:15])
        return tiles
    vop(lambda: c.v.tensor_scalar(out=W_[0:1, 2:3], in0=W_[0:1, 14:15], scalar1=1.0 / 65536.0,
                                  scalar2=None, op0=OP.mult))
    vop(lambda: c.v.tensor_tensor(out=W_[0:1, 3:4], in0=W_[0:1, 8:9], in1=W_[0:1, 2:3], op=OP.mult))
    vop(lambda: c.v.tensor_scalar(out=W_[0:1, 4:5], in0=W_[0:1, 3:4], scalar1=rqm[0:1, 0:1],
                                  scalar2=1e-12, op0=OP.mult, op1=OP.max))
    vop(lambda: c.v.reciprocal(out=W_[0:1, 5:6], in_=W_[0:1, 4:5]))
    vop(lambda: c.v.tensor_tensor(out=W_[0:1, 6:7], in0=W_[0:1, 2:3], in1=W_[0:1, 5:6], op=OP.mult))
    if upto == 2:
        bc('comb4b', W_[0:1, 6:7])
        return tiles
    vop(lambda: c.v.tensor_tensor(out=W_[0:1, 7:8], in0=W_[0:1, 4:5], in1=W_[0:1, 12:13], op=OP.mult))
    vop(lambda: c.v.tensor_tensor(out=W_[0:1, 3:4], in0=W_[0:1, 10:11], in1=W_[0:1, 7:8], op=OP.mult))
    vop(lambda: c.v.tensor_scalar(out=W_[0:1, 9:10], in0=W_[0:1, 3:4], scalar1=rqm[0:1, 0:1],
                                  scalar2=1e-12, op0=OP.mult, op1=OP.max))
    vop(lambda: c.v.reciprocal(out=W_[0:1, 11:12], in_=W_[0:1, 9:10]))
    vop(lambda: c.v.tensor_tensor(out=W_[0:1, 2:3], in0=W_[0:1, 7:8], in1=W_[0:1, 11:12], op=OP.mult))
    bc('comb5b', W_[0:1, 2:3])
    bc('s5b', W_[0:1, 9:10])
    return tiles


def _qdq_int_chain(c, pre, src_ap, rs_ap, f_t, f_i32, f_c, f_m, out_ap, width,
                   clip_lo, clip_hi, u_has_half=False):
    vs, as_, ps = pre + '_v', pre + '_a', pre + '_p'
    if not u_has_half:
        r = c.v.tensor_scalar(out=f_t, in0=src_ap, scalar1=rs_ap, scalar2=0.5,
                              op0=OP.mult, op1=OP.add)
        nv = c.tick(r, vs)
        u_ap = f_t
    else:
        u_ap = src_ap
        nv = c.n(vs)
    c.s.wait_ge(c.sem(vs), nv)
    r = c.s.activation(out=f_i32, in_=u_ap, func=AF.Identity)
    na = c.tick(r, as_)
    c.g.wait_ge(c.sem(as_), na)
    r = c.g.tensor_copy(out=f_c, in_=f_i32)
    npl = c.tick(r, ps)
    c.v.wait_ge(c.sem(ps), npl)
    r = c.v.tensor_tensor(out=f_m, in0=f_c, in1=u_ap, op=OP.is_gt)
    nv = c.tick(r, vs)
    c.v.wait_ge(c.sem(vs), nv)
    r = c.v.scalar_tensor_tensor(out=f_m, in0=f_m, scalar=-1.0, in1=f_c,
                                 op0=OP.mult, op1=OP.add)
    nv = c.tick(r, vs)
    c.v.wait_ge(c.sem(vs), nv)
    r = c.v.tensor_scalar(out=out_ap, in0=f_m, scalar1=float(clip_lo), scalar2=float(clip_hi),
                          op0=OP.max, op1=OP.min)
    return c.tick(r, vs)


def phase_p3(c):
    seq = c.seq
    ncc = seq // 512
    NT = FF // 128
    xq_fm = c.dram('xq_fm', [D, seq], BF16)
    G1Q = c.dram('G1Q', [FF, seq], I16)
    Q2 = c.dram('Q2', [FF, seq], U16)
    UP1Q = c.dram('UP1Q', [FF, seq], I16)
    scalB = c.dram('scalB', [1, 8], F32)
    wu_d = c.dram('wT_up', [D, FF], BF16)
    wg_d = c.dram('wT_gate', [D, FF], BF16)
    wup = c.tile('B0', [128, 32768], BF16)
    wgt = tile_b1(c)
    xqs = c.tile('B2', [128, 4096], BF16)
    sc0 = c.base_tile('SC0')
    ut = (sc0[:, 0:1024], sc0[:, 1024:2048])
    mt = (sc0[:, 2048:3072], sc0[:, 3072:4096])
    g1 = (sc0[:, 4096:5120], sc0[:, 5120:6144])
    q2s = (sc0[:, 6144:7168], sc0[:, 7168:8192])
    sc1 = c.base_tile('SC1')
    ct_ = (sc1[:, 0:1024], sc1[:, 1024:2048])
    ia = (sc1[:, 2048:3072].bitcast(I32), sc1[:, 3072:4096].bitcast(I32))
    sg = (sc1[:, 4096:5120], sc1[:, 5120:6144])
    rc2 = c.tile('p4_rc', [128, 2], F32)
    am = c.tile('p3_am', [128, 1], F32)
    row = c.tile('p2_row', [1, 8], F32)
    pA = c.psum('pA', [128, 2048], F32)
    pB = c.psum('pB', [128, 1024], F32)
    pC = c.psum('pC', [128, 1024], F32)
    sc = _ffn_scales(c, 'p3s', 1)
    c.t.wait_ge(c.sem('p3sb_v'), c.n('p3sb_v'))
    r = c.v.memset(am[:], 0.0); c.tick(r, 'p3_v')

    r = c.sy.dma_start(out=wup[:, 0:8 * FF].rearrange('p (a o) -> p a o', a=8),
                       in_=wu_d.ap().rearrange('(a p) o -> p a o', p=128))
    c.tick(r, 'p3_w', 16)
    r = c.sy.dma_start(out=wgt[:, 0:8 * FF].rearrange('p (a o) -> p a o', a=8),
                       in_=wg_d.ap().rearrange('(a p) o -> p a o', p=128))
    nw = c.tick(r, 'p3_w', 16)

    G1Q_r = G1Q.ap().rearrange('(T p) s -> p T s', p=128)
    Q2_r = Q2.ap().rearrange('(T p) s -> p T s', p=128)
    UP1Q_r = UP1Q.ap().rearrange('(T p) s -> p T s', p=128)

    H = dict(ngmm={}, numm={}, ncastA={}, nuA={}, ncA={}, nsttA={}, nsig={},
             ncastB={}, nuB={}, ncB={}, nmin={}, ncastC={}, nuC={}, ncC={},
             nsttC={}, nam={}, nsttB={})

    def emit_pair(cc, k0):
        bb = (cc * (NT // 2) + k0, cc * (NT // 2) + k0 + 1)
        cs = slice(cc * 512, (cc + 1) * 512)
        for b in bb:
            s = b % 2
            kk = (b % (NT // 2)) * 2
            if b >= 2:
                c.t.wait_ge(c.sem('p3_a'), H['nuA'][b - 2])
            for tq in range(2):
                for a in range(8):
                    r = c.t.matmul(out=pA[:, s * 1024 + tq * 512: s * 1024 + tq * 512 + 512],
                                   lhsT=wgt[:, a * FF + (kk + tq) * 128: a * FF + (kk + tq) * 128 + 128],
                                   rhs=xqs[:, a * 512:(a + 1) * 512],
                                   start=(a == 0), stop=(a == 7))
                    H['ngmm'][b] = c.tick(r, 'p3_t')
        for b in bb:
            s = b % 2
            kk = (b % (NT // 2)) * 2
            po = pB if s == 0 else pC
            if b >= 2:
                c.t.wait_ge(c.sem('p3_a'), H['nuC'][b - 2])
            for tq in range(2):
                for a in range(8):
                    r = c.t.matmul(out=po[:, tq * 512: tq * 512 + 512],
                                   lhsT=wup[:, a * FF + (kk + tq) * 128: a * FF + (kk + tq) * 128 + 128],
                                   rhs=xqs[:, a * 512:(a + 1) * 512],
                                   start=(a == 0), stop=(a == 7))
                    H['numm'][b] = c.tick(r, 'p3_t')
        for b in bb:
            s = b % 2
            c.s.wait_ge(c.sem('p3_t'), H['ngmm'][b])
            if b >= 2:
                c.s.wait_ge(c.sem('p3_a'), H['ncC'][b - 2])
                c.s.wait_ge(c.sem('p3_v'), H['nsttC'][b - 2])
            r = c.s.activation(out=ia[s], in_=pA[:, s * 1024:s * 1024 + 1024],
                               func=AF.Identity, scale=sc['rs3b'][:, 0:1], bias=c.kbias(0.5))
            H['ncastA'][b] = c.tick(r, 'p3_a')
            r = c.s.activation(out=ut[s], in_=pA[:, s * 1024:s * 1024 + 1024],
                               func=AF.Identity, scale=sc['rs3b'][:, 0:1], bias=c.kbias(0.5))
            H['nuA'][b] = c.tick(r, 'p3_a')
        for b in bb:
            s = b % 2
            c.s.wait_ge(c.sem('p3_a'), H['ncastA'][b])
            if b >= 2:
                c.s.wait_ge(c.sem('p3_v'), H['nsttC'][b - 2])
            r = c.s.activation(out=ct_[s], in_=ia[s], func=AF.Identity)
            H['ncA'][b] = c.tick(r, 'p3_a')
        for b in bb:
            s = b % 2
            c.v.wait_ge(c.sem('p3_a'), H['ncA'][b])
            c.v.wait_ge(c.sem('p3_a'), H['nuA'][b])
            c.v.wait_ge(c.sem('p3_st'), c.n('p3_st'))
            if b >= 2:
                c.v.wait_ge(c.sem('p3_a'), H['nsig'][b - 2])
            r = c.v.tensor_tensor(out=mt[s], in0=ct_[s], in1=ut[s], op=OP.is_gt)
            nv = c.tick(r, 'p3_v')
            c.v.wait_ge(c.sem('p3_v'), nv)
            r = c.v.scalar_tensor_tensor(out=g1[s], in0=mt[s], scalar=-1.0, in1=ct_[s],
                                         op0=OP.mult, op1=OP.add)
            H['nsttA'][b] = c.tick(r, 'p3_v')
        for b in bb:
            s = b % 2
            kk = (b % (NT // 2)) * 2
            c.g.wait_ge(c.sem('p3_v'), H['nsttA'][b])
            r = c.g.dma_start(out=G1Q_r[:, kk:kk + 2, cs],
                              in_=g1[s].rearrange('p (T s) -> p T s', T=2))
            c.tick(r, 'p3_st', 16)
        for b in bb:
            s = b % 2
            c.s.wait_ge(c.sem('p3_v'), H['nsttA'][b])
            r = c.s.activation(out=sg[s], in_=g1[s], func=AF.Sigmoid, scale=sc['s3b'][:, 0:1])
            H['nsig'][b] = c.tick(r, 'p3_a')
        for b in bb:
            s = b % 2
            c.s.wait_ge(c.sem('p3_a'), H['nsig'][b])
            c.s.wait_ge(c.sem('p3_a'), H['ncA'][b])
            r = c.s.activation(out=ia[s], in_=sg[s], func=AF.Identity,
                               scale=65536.0, bias=c.kbias(0.5))
            H['ncastB'][b] = c.tick(r, 'p3_a')
            c.s.wait_ge(c.sem('p3_v'), H['nsttA'][b])
            r = c.s.activation(out=ut[s], in_=sg[s], func=AF.Identity,
                               scale=65536.0, bias=c.kbias(0.5))
            H['nuB'][b] = c.tick(r, 'p3_a')
        for b in bb:
            s = b % 2
            c.s.wait_ge(c.sem('p3_a'), H['ncastB'][b])
            c.s.wait_ge(c.sem('p3_v'), H['nsttA'][b])
            r = c.s.activation(out=ct_[s], in_=ia[s], func=AF.Identity)
            H['ncB'][b] = c.tick(r, 'p3_a')
        for b in bb:
            s = b % 2
            c.v.wait_ge(c.sem('p3_a'), H['ncB'][b])
            c.v.wait_ge(c.sem('p3_a'), H['nuB'][b])
            r = c.v.tensor_tensor(out=mt[s], in0=ct_[s], in1=ut[s], op=OP.is_gt)
            nv = c.tick(r, 'p3_v')
            c.v.wait_ge(c.sem('p3_v'), nv)
            r = c.v.scalar_tensor_tensor(out=q2s[s], in0=mt[s], scalar=-1.0, in1=ct_[s],
                                         op0=OP.mult, op1=OP.add)
            H['nsttB'][b] = nv = c.tick(r, 'p3_v')
            c.v.wait_ge(c.sem('p3_v'), nv)
            r = c.v.tensor_scalar(out=q2s[s], in0=q2s[s], scalar1=65535.0, scalar2=None,
                                  op0=OP.min)
            H['nmin'][b] = c.tick(r, 'p3_v')
        for b in bb:
            s = b % 2
            kk = (b % (NT // 2)) * 2
            c.g.wait_ge(c.sem('p3_v'), H['nmin'][b])
            r = c.g.dma_start(out=Q2_r[:, kk:kk + 2, cs],
                              in_=q2s[s].rearrange('p (T s) -> p T s', T=2))
            c.tick(r, 'p3_st', 16)
        for b in bb:
            s = b % 2
            r = c.v.tensor_tensor(out=mt[s], in0=g1[s], in1=q2s[s], op=OP.mult)
            nv = c.tick(r, 'p3_v')
            c.v.wait_ge(c.sem('p3_v'), nv)
            r = c.v.reduce_max(out=rc2[:, s:s + 1], in_=mt[s], axis=AX.X,
                               apply_absolute_value=True)
            nv = c.tick(r, 'p3_v')
            c.v.wait_ge(c.sem('p3_v'), nv)
            r = c.v.tensor_tensor(out=am[:], in0=am[:], in1=rc2[:, s:s + 1], op=OP.max)
            H['nam'][b] = c.tick(r, 'p3_v')
        for b in bb:
            s = b % 2
            po = pB if s == 0 else pC
            c.s.wait_ge(c.sem('p3_t'), H['numm'][b])
            c.s.wait_ge(c.sem('p3_a'), H['ncB'][b])
            r = c.s.activation(out=ia[s], in_=po[:, 0:1024],
                               func=AF.Identity, scale=sc['rs2b'][:, 0:1], bias=c.kbias(0.5))
            H['ncastC'][b] = c.tick(r, 'p3_a')
            c.s.wait_ge(c.sem('p3_v'), H['nmin'][b])
            r = c.s.activation(out=ut[s], in_=po[:, 0:1024],
                               func=AF.Identity, scale=sc['rs2b'][:, 0:1], bias=c.kbias(0.5))
            H['nuC'][b] = c.tick(r, 'p3_a')
        for b in bb:
            s = b % 2
            c.s.wait_ge(c.sem('p3_a'), H['ncastC'][b])
            c.s.wait_ge(c.sem('p3_v'), H['nmin'][b])
            r = c.s.activation(out=ct_[s], in_=ia[s], func=AF.Identity)
            H['ncC'][b] = c.tick(r, 'p3_a')
        for b in bb:
            s = b % 2
            c.v.wait_ge(c.sem('p3_a'), H['ncC'][b])
            c.v.wait_ge(c.sem('p3_a'), H['nuC'][b])
            r = c.v.tensor_tensor(out=mt[s], in0=ct_[s], in1=ut[s], op=OP.is_gt)
            nv = c.tick(r, 'p3_v')
            c.v.wait_ge(c.sem('p3_v'), nv)
            r = c.v.scalar_tensor_tensor(out=mt[s], in0=mt[s], scalar=-1.0, in1=ct_[s],
                                         op0=OP.mult, op1=OP.add)
            H['nsttC'][b] = c.tick(r, 'p3_v')
        for b in bb:
            s = b % 2
            kk = (b % (NT // 2)) * 2
            c.g.wait_ge(c.sem('p3_v'), H['nsttC'][b])
            r = c.g.dma_start(out=UP1Q_r[:, kk:kk + 2, cs],
                              in_=mt[s].rearrange('p (T s) -> p T s', T=2))
            c.tick(r, 'p3_st', 16)

    for cc in range(ncc):
        c.sy.wait_ge(c.sem('p3_t'), c.n('p3_t'))
        r = c.sy.dma_start(out=xqs[:].rearrange('p (a s) -> p a s', a=8),
                           in_=xq_fm.ap().rearrange('(a p) s -> p a s', p=128)[:, :, cc * 512:(cc + 1) * 512])
        nld = c.tick(r, 'p3_ld', 16)
        c.t.wait_ge(c.sem('p3_ld'), nld)
        if cc == 0:
            c.t.wait_ge(c.sem('p3_w'), nw)
        for k in range(NT // 4):
            emit_pair(cc, 2 * k)
    c.t.wait_ge(c.sem('p3_v'), c.n('p3_v'))
    _xpart_max(c, 'p3x', am[:], row[0:1, 0:1])
    c.g.wait_ge(c.sem('p3_st'), c.n('p3_st'))
    c.g.wait_ge(c.sem('p3x_v'), c.n('p3x_v'))
    ng = _coll_max(c, 'p3c', row[:], 1, row[:])
    c.g.wait_ge(c.sem('p3c_g'), ng)
    r = c.g.dma_start(out=scalB.ap(), in_=row[:])
    c.tick(r, 'p3c_g', 16)
    c.g.wait_ge(c.sem('p3c_g'), c.n('p3c_g'))


def phase_p4a(c):
    seq = c.seq
    W2 = min(1024, seq)
    ncol = seq // W2
    NT = FF // 128
    G1Q = c.dram('G1Q', [FF, seq], I16)
    Q2 = c.dram('Q2', [FF, seq], U16)
    UP1Q = c.dram('UP1Q', [FF, seq], I16)
    G2Q = c.dram('G2Q', [FF, seq], I16)
    scalC = c.dram('scalC', [1, 8], F32)
    sc0 = c.base_tile('SC0')
    fg = (sc0[:, 0:W2], sc0[:, 1024:1024 + W2])
    fq = (sc0[:, 2048:2048 + W2], sc0[:, 3072:3072 + W2])
    fu = (sc0[:, 4096:4096 + W2], sc0[:, 5120:5120 + W2])
    ut = (sc0[:, 6144:6144 + W2], sc0[:, 7168:7168 + W2])
    sc1 = c.base_tile('SC1')
    ct_ = (sc1[:, 0:W2], sc1[:, 1024:1024 + W2])
    ia = (sc1[:, 2048:2048 + W2].bitcast(I32), sc1[:, 3072:3072 + W2].bitcast(I32))
    mt = (sc1[:, 4096:4096 + W2], sc1[:, 5120:5120 + W2])
    rc = c.tile('p4_rc', [128, 2], F32)
    am = c.tile('p3_am', [128, 1], F32)
    row = c.tile('p2_row', [1, 8], F32)
    sc = _ffn_scales(c, 'p4s', 2)
    r = c.v.memset(am[:], 0.0); c.tick(r, 'p4_v')

    H = dict(nld={}, nm4={}, ncast={}, nu={}, nc={}, nstt={}, nm5={}, nrm={})

    def emit_pair(b0):
        for b in (b0, b0 + 1):
            s = b % 2
            t = b // ncol
            cs = slice((b % ncol) * W2, (b % ncol + 1) * W2)
            if b >= 2:
                c.g.wait_ge(c.sem('p4_v'), H['nrm'][b - 2])
                c.g.wait_ge(c.sem('p4_st'), c.n('p4_st'))
            r = c.g.dma_start(out=fg[s], in_=G1Q.ap()[t * 128:(t + 1) * 128, cs])
            c.tick(r, 'p4_l%d' % s, 16)
            r = c.g.dma_start(out=fq[s], in_=Q2.ap()[t * 128:(t + 1) * 128, cs])
            c.tick(r, 'p4_l%d' % s, 16)
            r = c.g.dma_start(out=fu[s], in_=UP1Q.ap()[t * 128:(t + 1) * 128, cs])
            H['nld'][b] = c.tick(r, 'p4_l%d' % s, 16)
        for b in (b0, b0 + 1):
            s = b % 2
            c.v.wait_ge(c.sem('p4_l%d' % s), H['nld'][b])
            r = c.v.tensor_tensor(out=fg[s], in0=fg[s], in1=fq[s], op=OP.mult)
            H['nm4'][b] = c.tick(r, 'p4_v')
        for b in (b0, b0 + 1):
            s = b % 2
            c.s.wait_ge(c.sem('p4_v'), H['nm4'][b])
            if b >= 2:
                pass
                c.s.wait_ge(c.sem('p4_v'), H['nstt'][b - 2])
            r = c.s.activation(out=ia[s], in_=fg[s], func=AF.Identity,
                               scale=sc['comb4b'][:, 0:1], bias=c.kbias(0.5))
            H['ncast'][b] = c.tick(r, 'p4_a')
            r = c.s.activation(out=ut[s], in_=fg[s], func=AF.Identity,
                               scale=sc['comb4b'][:, 0:1], bias=c.kbias(0.5))
            H['nu'][b] = c.tick(r, 'p4_a')
        for b in (b0, b0 + 1):
            s = b % 2
            c.s.wait_ge(c.sem('p4_a'), H['nu'][b])
            if b >= 2:
                c.s.wait_ge(c.sem('p4_v'), H['nstt'][b - 2])
            r = c.s.activation(out=ct_[s], in_=ia[s], func=AF.Identity)
            H['nc'][b] = c.tick(r, 'p4_a')
        for b in (b0, b0 + 1):
            s = b % 2
            t = b // ncol
            cs = slice((b % ncol) * W2, (b % ncol + 1) * W2)
            c.v.wait_ge(c.sem('p4_a'), H['nc'][b])
            c.v.wait_ge(c.sem('p4_a'), H['nu'][b])
            c.v.wait_ge(c.sem('p4_st'), c.n('p4_st'))
            r = c.v.tensor_tensor(out=mt[s], in0=ct_[s], in1=ut[s], op=OP.is_gt)
            nv = c.tick(r, 'p4_v')
            c.v.wait_ge(c.sem('p4_v'), nv)
            r = c.v.scalar_tensor_tensor(out=fq[s], in0=mt[s], scalar=-1.0, in1=ct_[s],
                                         op0=OP.mult, op1=OP.add)
            H['nstt'][b] = nv = c.tick(r, 'p4_v')
            c.g.wait_ge(c.sem('p4_v'), nv)
            r = c.g.dma_start(out=G2Q.ap()[t * 128:(t + 1) * 128, cs], in_=fq[s])
            c.tick(r, 'p4_st', 16)
            c.v.wait_ge(c.sem('p4_v'), nv)
            r = c.v.tensor_tensor(out=fu[s], in0=fq[s], in1=fu[s], op=OP.mult)
            H['nm5'][b] = nv = c.tick(r, 'p4_v')
            c.v.wait_ge(c.sem('p4_v'), nv)
            r = c.v.reduce_max(out=rc[:, s:s + 1], in_=fu[s], axis=AX.X,
                               apply_absolute_value=True)
            nv = c.tick(r, 'p4_v')
            c.v.wait_ge(c.sem('p4_v'), nv)
            r = c.v.tensor_tensor(out=am[:], in0=am[:], in1=rc[:, s:s + 1], op=OP.max)
            H['nrm'][b] = c.tick(r, 'p4_v')

    nblk = NT * ncol
    for k in range(nblk // 2):
        emit_pair(2 * k)
    c.t.wait_ge(c.sem('p4_v'), c.n('p4_v'))
    _xpart_max(c, 'p4x', am[:], row[0:1, 0:1])
    c.g.wait_ge(c.sem('p4_st'), c.n('p4_st'))
    c.g.wait_ge(c.sem('p4x_v'), c.n('p4x_v'))
    ng = _coll_max(c, 'p4c', row[:], 1, row[:])
    c.g.wait_ge(c.sem('p4c_g'), ng)
    r = c.g.dma_start(out=scalC.ap(), in_=row[:])
    c.tick(r, 'p4c_g', 16)
    c.g.wait_ge(c.sem('p4c_g'), c.n('p4c_g'))


def phase_p4b(c):
    seq = c.seq
    ncc = seq // 512
    NT = FF // 128
    G2Q = c.dram('G2Q', [FF, seq], I16)
    UP1Q = c.dram('UP1Q', [FF, seq], I16)
    x1 = c.dram('x1', [seq, D], F32)
    x2 = c.dram('x2', [seq, D], F32)
    wdn = c.dram('wT_down', [FF, D], BF16)
    oin = c.tile('B0', [128, 32768], BF16)
    wdt = tile_b1(c)
    sc0 = c.base_tile('SC0')
    fg = (sc0[:, 0:1024], sc0[:, 1024:2048])
    fu = (sc0[:, 2048:3072], sc0[:, 3072:4096])
    ut = (sc0[:, 4096:5120], sc0[:, 5120:6144])
    q5 = (sc0[:, 6144:7168], sc0[:, 7168:8192])
    sc1 = c.base_tile('SC1')
    ct_ = (sc1[:, 0:1024], sc1[:, 1024:2048])
    ia = (sc1[:, 2048:3072].bitcast(I32), sc1[:, 3072:4096].bitcast(I32))
    mt = (sc1[:, 4096:5120], sc1[:, 5120:6144])
    b2f = c.tile('B2', [128, 4096], BF16)
    xres = (b2f[:, 0:2048].bitcast(F32), b2f[:, 2048:4096].bitcast(F32))
    pA = c.psum('pA', [128, 2048], F32)
    sc = _ffn_scales(c, 'p4t', 3)
    r = c.sy.dma_start(out=wdt[:, 0:24 * 1024].rearrange('p (a o) -> p a o', a=24),
                       in_=wdn.ap().rearrange('(a p) o -> p a o', p=128))
    nwd = c.tick(r, 'p4b_w', 16)
    G2Q_r = G2Q.ap().rearrange('(T p) s -> p T s', p=128)
    UP1Q_r = UP1Q.ap().rearrange('(T p) s -> p T s', p=128)

    H = dict(nld={}, nm={}, ncast={}, nu={}, nc={}, nstt={}, nsc={})
    nmm_cc = {}
    nx_cc = {}

    def chain_pair(cc, k0):
        bb = (cc * (NT // 2) + k0, cc * (NT // 2) + k0 + 1)
        cs = slice(cc * 512, (cc + 1) * 512)
        for b in bb:
            s = b % 2
            kk = (b % (NT // 2)) * 2
            if b >= 2:
                c.g.wait_ge(c.sem('p4b_v'), H['nm'][b - 2])
                c.g.wait_ge(c.sem('p4b_v'), H['nsc'][b - 2])
            r = c.g.dma_start(out=fg[s].rearrange('p (T s) -> p T s', T=2),
                              in_=G2Q_r[:, kk:kk + 2, cs])
            c.tick(r, 'p4b_l%d' % s, 16)
            r = c.g.dma_start(out=fu[s].rearrange('p (T s) -> p T s', T=2),
                              in_=UP1Q_r[:, kk:kk + 2, cs])
            H['nld'][b] = c.tick(r, 'p4b_l%d' % s, 16)
        for b in bb:
            s = b % 2
            c.v.wait_ge(c.sem('p4b_l%d' % s), H['nld'][b])
            r = c.v.tensor_tensor(out=fg[s], in0=fg[s], in1=fu[s], op=OP.mult)
            H['nm'][b] = c.tick(r, 'p4b_v')
        for b in bb:
            s = b % 2
            c.s.wait_ge(c.sem('p4b_v'), H['nm'][b])
            if b >= 2:
                pass
                c.s.wait_ge(c.sem('p4b_v'), H['nstt'][b - 2])
            r = c.s.activation(out=ia[s], in_=fg[s], func=AF.Identity,
                               scale=sc['comb5b'][:, 0:1], bias=c.kbias(0.5))
            H['ncast'][b] = c.tick(r, 'p4b_a')
            r = c.s.activation(out=ut[s], in_=fg[s], func=AF.Identity,
                               scale=sc['comb5b'][:, 0:1], bias=c.kbias(0.5))
            H['nu'][b] = c.tick(r, 'p4b_a')
        for b in bb:
            s = b % 2
            c.s.wait_ge(c.sem('p4b_a'), H['nu'][b])
            if b >= 2:
                c.s.wait_ge(c.sem('p4b_v'), H['nstt'][b - 2])
            r = c.s.activation(out=ct_[s], in_=ia[s], func=AF.Identity)
            H['nc'][b] = c.tick(r, 'p4b_a')
        for b in bb:
            s = b % 2
            kk = (b % (NT // 2)) * 2
            c.v.wait_ge(c.sem('p4b_a'), H['nc'][b])
            c.v.wait_ge(c.sem('p4b_a'), H['nu'][b])
            if b >= 2:
                c.v.wait_ge(c.sem('p4b_v'), H['nsc'][b - 2])
            r = c.v.tensor_tensor(out=mt[s], in0=ct_[s], in1=ut[s], op=OP.is_gt)
            nv = c.tick(r, 'p4b_v')
            c.v.wait_ge(c.sem('p4b_v'), nv)
            r = c.v.scalar_tensor_tensor(out=q5[s], in0=mt[s], scalar=-1.0, in1=ct_[s],
                                         op0=OP.mult, op1=OP.add)
            H['nstt'][b] = nv = c.tick(r, 'p4b_v')
            c.v.wait_ge(c.sem('p4b_v'), nv)
            if cc >= 1 and kk == 0:
                c.v.wait_ge(c.sem('p4b_t'), nmm_cc[cc - 1])
            r = c.v.tensor_scalar(out=oin[:, kk * 512:(kk + 2) * 512], in0=q5[s],
                                  scalar1=sc['s5b'][:, 0:1], scalar2=None, op0=OP.mult)
            H['nsc'][b] = c.tick(r, 'p4b_v')

    def gemm(cc):
        c.t.wait_ge(c.sem('p4b_w'), nwd)
        c.t.wait_ge(c.sem('p4b_v'), H['nsc'][cc * (NT // 2) + NT // 2 - 1])
        for sub in range(4):
            bank = sub % 2
            i_tile = cc * 4 + sub
            if i_tile >= 2:
                c.t.wait_ge(c.sem('p4b_x'), nx_cc[i_tile - 2])
            for a in range(24):
                r = c.t.matmul(out=pA[:, bank * 1024:bank * 1024 + 512],
                               lhsT=oin[:, a * 512 + sub * 128: a * 512 + sub * 128 + 128],
                               rhs=wdt[:, a * 1024: a * 1024 + 512],
                               start=(a == 0), stop=(a == 23))
                c.tick(r, 'p4b_t')
                r = c.t.matmul(out=pA[:, bank * 1024 + 512:bank * 1024 + 1024],
                               lhsT=oin[:, a * 512 + sub * 128: a * 512 + sub * 128 + 128],
                               rhs=wdt[:, a * 1024 + 512: a * 1024 + 1024],
                               start=(a == 0), stop=(a == 23))
                nmm = c.tick(r, 'p4b_t')
            par = i_tile % 2
            if i_tile >= 1:
                c.sy.wait_ge(c.sem('p4b_x'), c.n('p4b_x'))
            if i_tile >= 2:
                c.sy.wait_ge(c.sem('p4b_g%d' % par), 16 * (i_tile // 2))
            r = c.sy.dma_start(out=xres[par], in_=x1.ap().rearrange('(t p) d -> t p d', p=128)[i_tile])
            nldx = c.tick(r, 'p4b_lx', 16)
            c.v.wait_ge(c.sem('p4b_lx'), nldx)
            c.v.wait_ge(c.sem('p4b_t'), nmm)
            r = c.v.tensor_tensor(out=xres[par], in0=xres[par],
                                  in1=pA[:, bank * 1024:bank * 1024 + 1024], op=OP.add)
            nx_cc[i_tile] = nx = c.tick(r, 'p4b_x')
            c.g.wait_ge(c.sem('p4b_x'), nx)
            r = c.g.dma_start(out=x2.ap().rearrange('(t p) d -> t p d', p=128)[i_tile],
                              in_=xres[par])
            c.tick(r, 'p4b_g%d' % par, 16)
        nmm_cc[cc] = nmm

    for cc in range(ncc):
        for k in range(NT // 4):
            chain_pair(cc, 2 * k)
        gemm(cc)
    c.g.wait_ge(c.sem('p4b_g0'), c.n('p4b_g0'))
    c.g.wait_ge(c.sem('p4b_g1'), c.n('p4b_g1'))



def phase_p5(c):
    x2 = c.dram('x2', [c.seq, D], F32)
    xn_tok = c.dram('xn_tok_l1', [c.seq, D], F32)
    xn_fm = c.dram('xn_fm_l1', [D, c.seq], BF16)
    layernorm_tiles(c, 'p5', x2, xn_tok, xn_fm, amax_tile=None)



def phase_p7(c):
    seq = c.seq
    ntile = c.ntile
    x2 = c.dram('x2', [seq, D], F32)
    x3 = c.dram('x3', [seq, D], F32)
    xn_fm = c.dram('xn_fm_l1', [D, seq], BF16)
    wq_d = c.dram('wT_l1q', [D, D], BF16)
    wk_d = c.dram('wT_l1k', [D, D], BF16)
    wv_d = c.dram('wT_l1v', [D, D], BF16)
    wfc_d = c.dram('wT_l1fc', [D, D], BF16)

    wall = c.tile('B0', [128, 32768], BF16)
    b1 = tile_b1(c)
    Q4 = b1[:, 0:4096]
    K4 = b1[:, 4096:8192]
    abf = (b1[:, 12288:13312], b1[:, 13312:14336])
    afm = b1[:, 14336:15360]
    xqs = c.tile('B2', [128, 4096], BF16)
    sc0 = c.base_tile('SC0')
    Pt = sc0[:, 0:2048]
    OUT = sc0[:, 2048:3072]
    OUTB = sc0[:, 3072:4096]
    S = sc0[:, 6144:6400]
    MX = sc0[:, 6400:6416]
    SM = sc0[:, 6416:6432]
    RC = sc0[:, 6432:6448]
    xres = (sc0[:, 4096:5120], sc0[:, 5120:6144])
    sc1 = c.base_tile('SC1')
    V4 = sc1[:, 0:4096]
    TMP = sc1[:, 4096:5120]
    TMPP = sc1[:, 5120:6144]
    idb = c.tile('p7_idb', [128, 128], BF16)
    pA = c.psum('pA', [128, 2048], F32)
    pB = c.psum('pB', [128, 1024], F32)
    pC = c.psum('pC', [128, 1024], F32)
    NPV = 11

    for wi, wd in enumerate((wq_d, wk_d, wv_d, wfc_d)):
        r = c.sy.dma_start(out=wall[:, wi * 8192:(wi + 1) * 8192].rearrange('p (a o) -> p a o', a=8),
                           in_=wd.ap().rearrange('(a p) o -> p a o', p=128))
        nwl = c.tick(r, 'p7_wl', 16)
    r = c.sy.dma_start(out=idb[:], in_=ident_bf16(c).ap())
    nwl = c.tick(r, 'p7_wl', 16)

    def proj(i):
        g, j = divmod(i, 4)
        if j == 0:
            c.sy.wait_ge(c.sem('p7_t'), c.n('p7_t'))
            r = c.sy.dma_start(out=xqs[:].rearrange('p (a s) -> p a s', a=8),
                               in_=xn_fm.ap().rearrange('(a p) s -> p a s', p=128)[:, :, g * 512:(g + 1) * 512])
            nld = c.tick(r, 'p7_ld', 16)
            c.t.wait_ge(c.sem('p7_ld'), nld)
            if i == 0:
                c.t.wait_ge(c.sem('p7_wl'), nwl)
        outs = ((pA[:, 0:1024], 0, Q4), (pA[:, 1024:2048], 8192, K4), (pB[:, 0:1024], 16384, None))
        c.t.wait_ge(c.sem('p7_ev'), c.n('p7_ev'))
        for po, woff, dst4 in outs:
            for half in range(2):
                for a in range(8):
                    r = c.t.matmul(out=po[:, half * 512:(half + 1) * 512],
                                   lhsT=xqs[:, a * 512 + j * 128: a * 512 + j * 128 + 128],
                                   rhs=wall[:, woff + a * 1024 + half * 512: woff + a * 1024 + half * 512 + 512],
                                   start=(a == 0), stop=(a == 7))
                    nt = c.tick(r, 'p7_t')
            c.s.wait_ge(c.sem('p7_t'), nt)
            if dst4 is None:
                r = c.s.activation(out=V4[:, j * 1024:(j + 1) * 1024], in_=po, func=AF.Identity)
            else:
                r = c.s.activation(out=dst4[:, j * 1024:(j + 1) * 1024], in_=po, func=AF.Identity)
            c.tick(r, 'p7_ev')

    def compute(i):
        j = i % 4
        q3 = Q4[:, j * 1024:(j + 1) * 1024].rearrange('p (h d) -> p h d', d=HD)
        S3 = S.rearrange('p (h t) -> p h t', t=NH)
        P3 = Pt.rearrange('p (h d) -> p h d', d=HD)
        O3 = OUT.rearrange('p (h d) -> p h d', d=HD)
        OB3 = OUTB.rearrange('p (h d) -> p h d', d=HD)
        nev = c.n('p7_ev')
        c.v.wait_ge(c.sem('p7_ev'), nev)
        c.g.wait_ge(c.sem('p7_ev'), nev)
        q4 = q3.unsqueeze(2).broadcast_to([128, NH, 2, HD])
        P4 = Pt.rearrange('p (h t d) -> p h t d', t=2, d=HD)
        for g2 in range(NH // 2):
            k4 = K4[:, j * 1024 + g2 * 2 * HD: j * 1024 + (g2 + 1) * 2 * HD] \
                .rearrange('p (t d) -> p t d', d=HD).unsqueeze(1).broadcast_to([128, NH, 2, HD])
            r = c.v.tensor_tensor(out=P4, in0=q4, in1=k4, op=OP.mult)
            nv = c.tick(r, 'p7_v')
            c.v.wait_ge(c.sem('p7_v'), nv)
            r = c.v.reduce_sum(out=S3[:, :, g2 * 2:(g2 + 1) * 2].unsqueeze(3), in_=P4, axis=AX.X)
            nv = c.tick(r, 'p7_v')
            c.v.wait_ge(c.sem('p7_v'), nv)
        r = c.v.reduce_max(out=MX.unsqueeze(2), in_=S3, axis=AX.X)
        nv = c.tick(r, 'p7_v')
        c.v.wait_ge(c.sem('p7_v'), nv)
        r = c.v.tensor_tensor(out=S3, in0=S3, in1=MX.unsqueeze(2).broadcast_to([128, NH, NH]),
                              op=OP.subtract)
        nv = c.tick(r, 'p7_v')
        c.s.wait_ge(c.sem('p7_v'), nv)
        r = c.s.activation(out=S, in_=S, func=AF.Exp, scale=0.125)
        ne = c.tick(r, 'p7_e')
        c.v.wait_ge(c.sem('p7_e'), ne)
        r = c.v.reduce_sum(out=SM.unsqueeze(2), in_=S3, axis=AX.X)
        nv = c.tick(r, 'p7_v')
        c.v.wait_ge(c.sem('p7_v'), nv)
        r = c.v.reciprocal(out=RC, in_=SM)
        nv = c.tick(r, 'p7_v')
        c.v.wait_ge(c.sem('p7_v'), nv)
        nexp = nv
        for t in range(NPV):
            e3 = S3[:, :, t:t + 1].broadcast_to([128, NH, HD])
            v3 = V4[:, j * 1024 + t * HD: j * 1024 + (t + 1) * HD].unsqueeze(1).broadcast_to([128, NH, HD])
            if t == 0:
                r = c.v.tensor_tensor(out=O3, in0=e3, in1=v3, op=OP.mult)
            else:
                r = c.v.tensor_tensor(out=TMP.rearrange('p (h d) -> p h d', d=HD), in0=e3, in1=v3, op=OP.mult)
            nv = c.tick(r, 'p7_v')
            c.v.wait_ge(c.sem('p7_v'), nv)
            if t > 0:
                r = c.v.tensor_tensor(out=OUT, in0=OUT, in1=TMP, op=OP.add)
                nv = c.tick(r, 'p7_v')
                c.v.wait_ge(c.sem('p7_v'), nv)
        c.g.wait_ge(c.sem('p7_v'), nexp)
        c.g.wait_ge(c.sem('p7_x'), c.n('p7_x'))
        for t in range(NPV, NH):
            e3 = S3[:, :, t:t + 1].broadcast_to([128, NH, HD])
            v3 = V4[:, j * 1024 + t * HD: j * 1024 + (t + 1) * HD].unsqueeze(1).broadcast_to([128, NH, HD])
            if t == NPV:
                r = c.g.tensor_tensor(out=OB3, in0=e3, in1=v3, op=OP.mult)
            else:
                r = c.g.tensor_tensor(out=TMPP.rearrange('p (h d) -> p h d', d=HD), in0=e3, in1=v3, op=OP.mult)
            np_ = c.tick(r, 'p7_p')
            c.g.wait_ge(c.sem('p7_p'), np_)
            if t > NPV:
                r = c.g.tensor_tensor(out=OUTB, in0=OUTB, in1=TMPP, op=OP.add)
                np_ = c.tick(r, 'p7_p')
                c.g.wait_ge(c.sem('p7_p'), np_)
        c.v.wait_ge(c.sem('p7_p'), c.n('p7_p'))
        r = c.v.tensor_tensor(out=O3, in0=O3, in1=OB3, op=OP.add)
        nx2 = c.tick(r, 'p7_x')
        c.v.wait_ge(c.sem('p7_x'), nx2)
        if i >= 2:
            c.v.wait_ge(c.sem('p7_tt'), 8 * (i - 1))
        a3 = abf[i % 2].rearrange('p (h d) -> p h d', d=HD)
        r = c.v.tensor_tensor(out=a3, in0=O3,
                              in1=RC.unsqueeze(2).broadcast_to([128, NH, HD]), op=OP.mult)
        ndv = c.tick(r, 'p7_v')
        c.t.wait_ge(c.sem('p7_v'), ndv)
        c.t.wait_ge(c.sem('p7_ev'), c.n('p7_ev'))
        for b in range(8):
            r = c.t.transpose(out=pB[:, b * 64:(b + 1) * 64].bitcast(BF16),
                              in_=abf[i % 2][:, b * 128:(b + 1) * 128], identity=idb[:])
            ntt = c.tick(r, 'p7_tt')
        c.v.wait_ge(c.sem('p7_tt'), ntt)
        c.v.wait_ge(c.sem('p7_t3'), c.n('p7_t3'))
        r = c.v.tensor_copy(out=afm, in_=pB[:, 0:512].bitcast(BF16))
        nea = c.tick(r, 'p7_ev')
        c.t.wait_ge(c.sem('p7_ev'), nea)
        if i >= 1:
            c.t.wait_ge(c.sem('p7_x'), c.n('p7_x'))
        for half in range(2):
            for a in range(8):
                r = c.t.matmul(out=pC[:, half * 512:(half + 1) * 512],
                               lhsT=afm[:, a * 128:(a + 1) * 128],
                               rhs=wall[:, 24576 + a * 1024 + half * 512: 24576 + a * 1024 + half * 512 + 512],
                               start=(a == 0), stop=(a == 7))
                nt3 = c.tick(r, 'p7_t3')
        if i >= 1:
            c.sy.wait_ge(c.sem('p7_x'), c.n('p7_x'))
        if i >= 2:
            c.sy.wait_ge(c.sem('p7_g%d' % (i % 2)), 16 * (i // 2))
        r = c.sy.dma_start(out=xres[i % 2], in_=x2.ap().rearrange('(t p) d -> t p d', p=128)[i])
        nlx = c.tick(r, 'p7_lx', 16)
        c.v.wait_ge(c.sem('p7_lx'), nlx)
        c.v.wait_ge(c.sem('p7_t3'), nt3)
        r = c.v.tensor_tensor(out=xres[i % 2], in0=xres[i % 2], in1=pC[:, 0:1024], op=OP.add)
        nx = c.tick(r, 'p7_x')
        c.g.wait_ge(c.sem('p7_x'), nx)
        r = c.g.dma_start(out=x3.ap().rearrange('(t p) d -> t p d', p=128)[i], in_=xres[i % 2])
        c.tick(r, 'p7_g%d' % (i % 2), 16)

    proj(0)
    for i in range(ntile):
        if i + 1 < ntile:
            proj(i + 1)
        compute(i)
    c.g.wait_ge(c.sem('p7_g0'), c.n('p7_g0'))
    c.g.wait_ge(c.sem('p7_g1'), c.n('p7_g1'))




def ones_row_bf(c):
    return c.const('ones_row_bf', np.ones((1, 128), dtype=ml_dtypes.bfloat16))


def phase_p8(c):
    seq = c.seq
    ntile = c.ntile
    x3 = c.dram('x3', [seq, D], F32)
    out_d = c.dram('out', [seq, D], F32)
    xn_fm = c.dram('xn_fm_l1', [D, seq], BF16)
    xn_tok = c.dram('xn_tok_l1', [seq, D], F32)
    wb_d = c.dram('L1_ffn_W_base', [D, D], F32)
    bb_d = c.dram('L1_ffn_b_base', [1, D], F32)
    wd1_d = c.dram('L1_ffn_Wd1', [NE, D, EH], F32)
    bd1_d = c.dram('L1_ffn_bd1', [NE, EH], F32)
    wd2_d = c.dram('L1_ffn_Wd2', [NE, EH, D], F32)
    bd2_d = c.dram('L1_ffn_bd2', [NE, D], F32)
    wad_d = c.dram('L1_ffn_Wa_down', [NE, D, RK], F32)
    bad_d = c.dram('L1_ffn_ba_down', [NE, RK], F32)
    wau_d = c.dram('L1_ffn_Wa_up', [NE, RK, D], F32)
    bau_d = c.dram('L1_ffn_ba_up', [NE, D], F32)

    b0 = c.tile('B0', [128, 32768], BF16)
    b1 = tile_b1(c)
    wbase = b0[:, 0:8192]
    wd1 = b0[:, 8192:24576]
    wad = b0[:, 24576:28672]
    bd1r = b0[0:1, 28672:30720]
    wd2 = b1[:, 0:16384]
    wau = b1[0:64, 16384:24576]
    bd2r = b1[0:1, 24576:32768]
    xfm = c.tile('B2', [128, 4096], BF16)[:, 0:1024]
    f1 = c.tile('F1', [128, 2048], F32)
    xtok2 = (f1[:, 0:1024], f1[:, 1024:2048])
    f0 = c.tile('F0', [128, 2048], F32)
    diff2 = (f0[:, 0:1024], f0[:, 1024:2048])
    sc1 = c.base_tile('SC1')
    sq_out = sc1[:, 0:1024]
    wad_acc = sc1[:, 1024:1536]
    wup_acc = sc1[0:64, 2048:3072]
    out2 = (sc1[:, 0:1024], sc1[:, 1024:2048])
    r1 = sc1[:, 4096:4608].bitcast(BF16)[0:1, 0:1024]
    wadb = sc1[:, 4608:4864].bitcast(BF16)[:, 0:512]
    wupb = sc1[:, 4864:5376].bitcast(BF16)[0:64, 0:1024]
    acc2 = sc1[:, 5376:5378]
    hB = c.tile('p8_h', [128, 512], BF16)
    hf = c.tile('p8_hf', [128, 512], BF16)
    aB = c.tile('p8_a', [128, 128], BF16)
    af = c.tile('p8_af', [64, 256], BF16)
    ls = c.tile('p8_ls', [128, 8], F32)
    lc = c.tile('p8_lc', [8, 1], F32)
    lr = c.tile('p8_lr', [1, 8], F32)
    mrow = c.tile('p8_m', [1, 1], F32)
    oh = c.tile('p8_oh', [1, 8], F32)
    ohb = c.tile('p8_ohb', [128, 8], F32)
    bad_acc = c.tile('p8_bad', [1, 64], F32)
    badb = c.tile('p8_badb', [1, 64], BF16)
    bau_acc = sc1[:, 3072:4096][0:1, 0:1024]
    baub = c.tile('B2', [128, 4096], BF16)[:, 1024:2048][0:1, 0:1024]
    bbb = c.tile('p8_bbb', [1, 1024], BF16)
    row0 = c.tile('B2', [128, 4096], BF16)[:, 2048:4096].bitcast(F32)[0:1, 0:1024]
    onb = c.tile('p8_onb', [1, 128], BF16)
    idt = c.tile('w0_id', [128, 128], BF16)
    idf = c.tile('p0_idf', [128, 128], F32)
    pA = c.psum('pA', [128, 2048], F32)
    pB = c.psum('pB', [128, 1024], F32)
    pC = c.psum('pC', [128, 1024], F32)

    r = c.g.dma_start(out=wbase.rearrange('p (a o) -> p a o', a=8),
                      in_=wb_d.ap().rearrange('(a p) o -> p a o', p=128))
    c.tick(r, 'p8_wg', 16)
    r = c.g.dma_start(out=wd1.rearrange('p (e a h) -> p e a h', e=NE, a=8),
                      in_=wd1_d.ap().rearrange('e (a p) h -> p e a h', p=128))
    c.tick(r, 'p8_wg', 16)
    r = c.g.dma_start(out=wad.rearrange('p (e a k) -> p e a k', e=NE, a=8),
                      in_=wad_d.ap().rearrange('e (a p) k -> p e a k', p=128))
    c.tick(r, 'p8_wg', 16)
    r = c.g.dma_start(out=wd2.rearrange('p (e hc o) -> p e hc o', e=NE, hc=2),
                      in_=wd2_d.ap().rearrange('e (hc p) o -> p e hc o', p=128))
    c.tick(r, 'p8_wg', 16)
    r = c.g.dma_start(out=wau.rearrange('p (e o) -> p e o', e=NE),
                      in_=wau_d.ap().rearrange('e k o -> k e o'))
    c.tick(r, 'p8_wg', 16)
    r = c.g.dma_start(out=bbb[:], in_=bb_d.ap())
    c.tick(r, 'p8_wg', 16)
    for e in range(NE):
        r = c.g.dma_start(out=bd1r[0:1, e * 256:(e + 1) * 256], in_=bd1_d.ap()[e:e + 1, :])
        c.tick(r, 'p8_wg', 16)
        r = c.g.dma_start(out=bd2r[0:1, e * 1024:(e + 1) * 1024], in_=bd2_d.ap()[e:e + 1, :])
        c.tick(r, 'p8_wg', 16)
    r = c.sy.dma_start(out=onb[:], in_=ones_row_bf(c).ap()); c.tick(r, 'p8_ld', 16)
    r = c.sy.dma_start(out=idt[:], in_=ident_bf16(c).ap()); c.tick(r, 'p8_ld', 16)
    r = c.sy.dma_start(out=idf[:], in_=ident_f32(c).ap()); c.tick(r, 'p8_ld', 16)
    onrt = c.tile('w0_onr', [1, 128], F32)
    r = c.sy.dma_start(out=onrt[:], in_=ones_row(c).ap()); c.tick(r, 'p8_ld', 16)
    r = c.v.memset(ls[:], 0.0); c.tick(r, 'p8_v')
    nwg = c.n('p8_wg')

    total = ntile * NE
    na_hist = {}
    nvd_hist = {}
    nld_ct = {}
    nlx_ct = {}
    nsq_hist = {}
    nwd1_ct = {}

    def stage_a(it):
        ct, e = divmod(it, NE)
        q = it % 2
        if e == 0:
            if ct >= 1:
                c.sy.wait_ge(c.sem('p8_t'), nwd1_ct[ct - 1])
            if ct >= 2:
                c.sy.wait_ge(c.sem('p8_v'), nvd_hist[(ct - 2) * NE + NE - 1])
            r = c.sy.dma_start(out=xfm.rearrange('p (a w) -> p a w', a=8),
                               in_=xn_fm.ap().rearrange('(a p) s -> p a s', p=128)[:, :, ct * 128:(ct + 1) * 128])
            nld_ct[ct] = c.tick(r, 'p8_ld', 16)
            r = c.sy.dma_start(out=xtok2[ct % 2], in_=xn_tok.ap().rearrange('(t p) d -> t p d', p=128)[ct])
            nlx_ct[ct] = c.tick(r, 'p8_lx%d' % (ct % 2), 16)
        off = 0 if q == 0 else 512
        c.t.wait_ge(c.sem('p8_ld'), nld_ct[ct])
        if it == 0:
            c.t.wait_ge(c.sem('p8_wg'), nwg)
        if it >= 2:
            c.t.wait_ge(c.sem('p8_a'), na_hist[it - 2])
        for a in range(8):
            r = c.t.matmul(out=pA[:, off:off + 256], lhsT=xfm[:, a * 128:(a + 1) * 128],
                           rhs=wd1[:, (e * 8 + a) * 256:(e * 8 + a + 1) * 256],
                           start=(a == 0), stop=False)
            c.tick(r, 'p8_t')
        r = c.t.matmul(out=pA[:, off:off + 256], lhsT=onb[:], rhs=bd1r[0:1, e * 256:(e + 1) * 256],
                       start=False, stop=True)
        nt = c.tick(r, 'p8_t')
        if e == NE - 1:
            nwd1_ct[ct] = nt
        c.s.wait_ge(c.sem('p8_t'), nt)
        r = c.s.activation(out=hB[:, q * 256:(q + 1) * 256], in_=pA[:, off:off + 256], func=AF.Relu)
        na_hist[it] = c.tick(r, 'p8_a')

    def stage_b(it):
        ct, e = divmod(it, NE)
        q = it % 2
        c.t.wait_ge(c.sem('p8_a'), na_hist[it])
        for b in range(2):
            gx = c.aux.get('p8_tt', 0); c.aux['p8_tt'] = gx + 1
            slot = gx % 2
            pv = pC[:, slot * 512:slot * 512 + 128].bitcast(BF16)[:, 0:128]
            if gx >= 2:
                c.t.wait_ge(c.sem('p8_ev'), c.n('p8_ev') - 1)
            r = c.t.transpose(out=pv, in_=hB[:, q * 256 + b * 128:q * 256 + (b + 1) * 128], identity=idt[:])
            ntt = c.tick(r, 'p8_t')
            c.v.wait_ge(c.sem('p8_t'), ntt)
            r = c.v.tensor_copy(out=hf[:, q * 256 + b * 128:q * 256 + (b + 1) * 128], in_=pv)
            c.tick(r, 'p8_ev')
        rec = pB[:, 0:1024] if q == 0 else pA[:, 1024:2048]
        c.t.wait_ge(c.sem('p8_ev'), c.n('p8_ev'))
        if it >= 2:
            c.t.wait_ge(c.sem('p8_v'), nvd_hist[it - 2])
        for half in range(2):
            for hc in range(2):
                r = c.t.matmul(out=rec[:, half * 512:half * 512 + 512],
                               lhsT=hf[:, q * 256 + hc * 128:q * 256 + (hc + 1) * 128],
                               rhs=wd2[:, (e * 2 + hc) * 1024 + half * 512:(e * 2 + hc) * 1024 + half * 512 + 512],
                               start=(hc == 0), stop=False)
                c.tick(r, 'p8_t')
            r = c.t.matmul(out=rec[:, half * 512:half * 512 + 512], lhsT=onb[:],
                           rhs=bd2r[0:1, e * 1024 + half * 512:e * 1024 + half * 512 + 512],
                           start=False, stop=True)
            nt = c.tick(r, 'p8_t')
        c.v.wait_ge(c.sem('p8_t'), nt)
        c.v.wait_ge(c.sem('p8_lx%d' % (ct % 2)), nlx_ct[ct])
        if it >= 2:
            c.v.wait_ge(c.sem('p8_a'), nsq_hist[it - 2])
        r = c.v.tensor_tensor(out=diff2[q], in0=rec, in1=xtok2[ct % 2], op=OP.subtract)
        nvd_hist[it] = nv = c.tick(r, 'p8_v')
        c.s.wait_ge(c.sem('p8_v'), nv)
        r = c.s.activation(out=sq_out, in_=diff2[q], func=AF.Square, accum_out=acc2[:, q:q + 1])
        nsq_hist[it] = na = c.tick(r, 'p8_a')
        c.v.wait_ge(c.sem('p8_a'), na)
        r = c.v.tensor_tensor(out=ls[:, e:e + 1], in0=ls[:, e:e + 1], in1=acc2[:, q:q + 1], op=OP.add)
        c.tick(r, 'p8_v')

    for it in range(total + 1):
        if it < total:
            stage_a(it)
        if it >= 1:
            stage_b(it - 1)

    c.t.wait_ge(c.sem('p8_v'), c.n('p8_v'))
    c.t.wait_ge(c.sem('p8_ld'), c.n('p8_ld'))
    r = c.t.matmul(out=pC[0:8, 0:128], lhsT=ls[:], rhs=idf[:], start=True, stop=True)
    nt = c.tick(r, 'p8_t2')
    c.v.wait_ge(c.sem('p8_t2'), nt)
    r = c.v.reduce_sum(out=lc[:], in_=pC[0:8, 0:128], axis=AX.X)
    nv = c.tick(r, 'p8_v')
    c.t.wait_ge(c.sem('p8_v'), nv)
    r = c.t.matmul(out=pC[0:1, 512:520], lhsT=lc[:], rhs=idf[0:8, 0:8], start=True, stop=True)
    nt = c.tick(r, 'p8_t2')
    c.v.wait_ge(c.sem('p8_t2'), nt)
    r = c.v.tensor_copy(out=lr[:], in_=pC[0:1, 512:520])
    nv = c.tick(r, 'p8_v')
    c.v.wait_ge(c.sem('p8_v'), nv)
    r = c.v.tensor_reduce(out=mrow[:], in_=lr[:], axis=AX.X, op=OP.min)
    nv = c.tick(r, 'p8_v')
    c.v.wait_ge(c.sem('p8_v'), nv)
    r = c.v.tensor_single_scalar(out=oh[:], in_=lr[:], scalar=mrow[0:1, 0:1], op=OP.is_equal)
    nv = c.tick(r, 'p8_v')
    c.t.wait_ge(c.sem('p8_v'), nv)
    r = c.t.matmul(out=pC[0:128, 520:528], lhsT=onrt[:], rhs=oh[:], start=True, stop=True)
    nt = c.tick(r, 'p8_t2')
    c.v.wait_ge(c.sem('p8_t2'), nt)
    r = c.v.tensor_copy(out=ohb[:], in_=pC[0:128, 520:528])
    nv = c.tick(r, 'p8_v')
    for e in range(NE):
        c.v.wait_ge(c.sem('p8_v'), c.n('p8_v'))
        if e == 0:
            r = c.v.tensor_scalar(out=wad_acc, in0=wad[:, 0:512], scalar1=ohb[:, 0:1],
                                  scalar2=None, op0=OP.mult)
        else:
            r = c.v.scalar_tensor_tensor(out=wad_acc, in0=wad[:, e * 512:(e + 1) * 512],
                                         scalar=ohb[:, e:e + 1], in1=wad_acc,
                                         op0=OP.mult, op1=OP.add)
        c.tick(r, 'p8_v')
        c.v.wait_ge(c.sem('p8_v'), c.n('p8_v'))
        if e == 0:
            r = c.v.tensor_scalar(out=wup_acc, in0=wau[0:64, 0:1024], scalar1=ohb[0:64, 0:1],
                                  scalar2=None, op0=OP.mult)
        else:
            r = c.v.scalar_tensor_tensor(out=wup_acc, in0=wau[0:64, e * 1024:(e + 1) * 1024],
                                         scalar=ohb[0:64, e:e + 1], in1=wup_acc,
                                         op0=OP.mult, op1=OP.add)
        c.tick(r, 'p8_v')
        c.sy.wait_ge(c.sem('p8_v'), c.n('p8_v'))
        c.sy.wait_ge(c.sem('p8_ld'), c.n('p8_ld'))
        r = c.sy.dma_start(out=row0[0:1, 0:64], in_=bad_d.ap()[e:e + 1, :])
        nl = c.tick(r, 'p8_ld', 16)
        c.v.wait_ge(c.sem('p8_ld'), nl)
        if e == 0:
            r = c.v.tensor_single_scalar(out=bad_acc[:], in_=row0[0:1, 0:64],
                                         scalar=oh[0:1, 0:1], op=OP.mult)
        else:
            r = c.v.scalar_tensor_tensor(out=bad_acc[:], in0=row0[0:1, 0:64],
                                         scalar=oh[0:1, e:e + 1], in1=bad_acc[:],
                                         op0=OP.mult, op1=OP.add)
        c.tick(r, 'p8_v')
        c.sy.wait_ge(c.sem('p8_v'), c.n('p8_v'))
        c.sy.wait_ge(c.sem('p8_ld'), c.n('p8_ld'))
        r = c.sy.dma_start(out=row0, in_=bau_d.ap()[e:e + 1, :])
        nl = c.tick(r, 'p8_ld', 16)
        c.v.wait_ge(c.sem('p8_ld'), nl)
        if e == 0:
            r = c.v.tensor_single_scalar(out=bau_acc, in_=row0,
                                         scalar=oh[0:1, 0:1], op=OP.mult)
        else:
            r = c.v.scalar_tensor_tensor(out=bau_acc, in0=row0,
                                         scalar=oh[0:1, e:e + 1], in1=bau_acc,
                                         op0=OP.mult, op1=OP.add)
        c.tick(r, 'p8_v')
    c.v.wait_ge(c.sem('p8_v'), c.n('p8_v'))
    r = c.v.tensor_copy(out=wadb, in_=wad_acc); c.tick(r, 'p8_v')
    r = c.v.tensor_copy(out=wupb, in_=wup_acc); c.tick(r, 'p8_v')
    r = c.v.tensor_copy(out=badb[:], in_=bad_acc[:]); c.tick(r, 'p8_v')
    r = c.v.tensor_copy(out=baub, in_=bau_acc); nvsel = c.tick(r, 'p8_v')

    fa_hist = {}; fld_hist = {}; flx_hist = {}; fout_hist = {}; fnt_hist = {}
    c.sy.wait_ge(c.sem('p8_ld'), c.n('p8_ld'))

    def fstage_a(ct):
        q = ct % 2
        if ct == 0:
            c.sy.wait_ge(c.sem('p8_v'), nvsel)
        if ct >= 1:
            c.sy.wait_ge(c.sem('p8_t'), fnt_hist[ct - 1])
        if ct >= 2:
            c.sy.wait_ge(c.sem('p8_v'), fout_hist[ct - 2])
        r = c.sy.dma_start(out=xfm.rearrange('p (a w) -> p a w', a=8),
                           in_=xn_fm.ap().rearrange('(a p) s -> p a s', p=128)[:, :, ct * 128:(ct + 1) * 128])
        nld = c.tick(r, 'p8_ld', 16)
        r = c.sy.dma_start(out=xtok2[q], in_=x3.ap().rearrange('(t p) d -> t p d', p=128)[ct])
        flx_hist[ct] = c.tick(r, 'p8_lx%d' % q, 16)
        fld_hist[ct] = nld
        base = pA[:, 0:1024] if q == 0 else pA[:, 1024:2048]
        ada = pC[:, 0:64] if q == 0 else pC[:, 512:576]
        c.t.wait_ge(c.sem('p8_ld'), nld)
        if ct == 0:
            c.t.wait_ge(c.sem('p8_v'), nvsel)
        if ct >= 2:
            c.t.wait_ge(c.sem('p8_v'), fout_hist[ct - 2])
        for half in range(2):
            for a in range(8):
                r = c.t.matmul(out=base[:, half * 512:half * 512 + 512],
                               lhsT=xfm[:, a * 128:(a + 1) * 128],
                               rhs=wbase[:, a * 1024 + half * 512: a * 1024 + half * 512 + 512],
                               start=(a == 0), stop=False)
                c.tick(r, 'p8_t')
            r = c.t.matmul(out=base[:, half * 512:half * 512 + 512],
                           lhsT=onb[:], rhs=bbb[0:1, half * 512:(half + 1) * 512],
                           start=False, stop=True)
            c.tick(r, 'p8_t')
        if ct >= 2:
            c.t.wait_ge(c.sem('p8_a'), fa_hist[ct - 2])
        for a in range(8):
            r = c.t.matmul(out=ada, lhsT=xfm[:, a * 128:(a + 1) * 128],
                           rhs=wadb[:, a * 64:(a + 1) * 64], start=(a == 0), stop=False)
            c.tick(r, 'p8_t')
        r = c.t.matmul(out=ada, lhsT=onb[:], rhs=badb[:], start=False, stop=True)
        nt = c.tick(r, 'p8_t')
        fnt_hist[ct] = nt
        c.s.wait_ge(c.sem('p8_t'), nt)
        r = c.s.activation(out=aB[:, q * 64:(q + 1) * 64], in_=ada, func=AF.Relu)
        fa_hist[ct] = c.tick(r, 'p8_a')

    def fstage_b(ct):
        q = ct % 2
        base = pA[:, 0:1024] if q == 0 else pA[:, 1024:2048]
        c.t.wait_ge(c.sem('p8_a'), fa_hist[ct])
        pv = pC[:, 64 + q * 512:64 + q * 512 + 64].bitcast(BF16)[0:64, 0:128]
        r = c.t.transpose(out=pv, in_=aB[:, q * 64:(q + 1) * 64], identity=idt[:])
        nt = c.tick(r, 'p8_t')
        c.v.wait_ge(c.sem('p8_t'), nt)
        r = c.v.tensor_copy(out=af[0:64, q * 128:(q + 1) * 128], in_=pv)
        nv2 = c.tick(r, 'p8_ev')
        c.t.wait_ge(c.sem('p8_ev'), nv2)
        if ct >= 1:
            c.t.wait_ge(c.sem('p8_v'), fout_hist[ct - 1])
        for half in range(2):
            r = c.t.matmul(out=pB[:, half * 512:(half + 1) * 512], lhsT=af[0:64, q * 128:(q + 1) * 128],
                           rhs=wupb[0:64, half * 512:(half + 1) * 512], start=True, stop=False)
            c.tick(r, 'p8_t')
            r = c.t.matmul(out=pB[:, half * 512:(half + 1) * 512], lhsT=onb[:],
                           rhs=baub[0:1, half * 512:(half + 1) * 512], start=False, stop=True)
            nt = c.tick(r, 'p8_t')
        c.v.wait_ge(c.sem('p8_t'), nt)
        c.v.wait_ge(c.sem('p8_lx%d' % q), flx_hist[ct])
        if ct >= 2:
            c.v.wait_ge(c.sem('p8_g%d' % q), 16 * (ct // 2))
        r = c.v.tensor_copy(out=diff2[q], in_=base)
        nv2 = c.tick(r, 'p8_v')
        c.v.wait_ge(c.sem('p8_v'), nv2)
        r = c.v.tensor_tensor(out=diff2[q], in0=diff2[q], in1=pB[:, 0:1024], op=OP.add)
        nv2 = c.tick(r, 'p8_v')
        c.v.wait_ge(c.sem('p8_v'), nv2)
        r = c.v.tensor_tensor(out=out2[q], in0=xtok2[q], in1=diff2[q], op=OP.add)
        fout_hist[ct] = nx = c.tick(r, 'p8_v')
        c.g.wait_ge(c.sem('p8_v'), nx)
        r = c.g.dma_start(out=out_d.ap().rearrange('(t p) d -> t p d', p=128)[ct],
                          in_=out2[q])
        c.tick(r, 'p8_g%d' % q, 16)

    for ct in range(ntile + 1):
        if ct < ntile:
            fstage_a(ct)
        if ct >= 1:
            fstage_b(ct - 1)
    c.g.wait_ge(c.sem('p8_g0'), c.n('p8_g0'))
    c.g.wait_ge(c.sem('p8_g1'), c.n('p8_g1'))


PHASES = {'w0': phase_w0, 'w1': phase_w1, 'p0': phase_p0, 'p1': phase_p1,
          'p2': phase_p2, 'p3': phase_p3, 'p4a': phase_p4a, 'p4b': phase_p4b,
          'p5': phase_p5, 'p7': phase_p7, 'p8': phase_p8}



def build_program(phase_names, seq, ext_in, ext_out, n_cores=NCORES):
    nc = bass.Bass('TRN2', target_bir_lowering=False)
    c = Ctx(nc, seq, ext_in, ext_out, n_cores)
    c.prologue()
    for i, pn in enumerate(phase_names):
        PHASES[pn](c)
        if i != len(phase_names) - 1:
            c.barrier()
    c.replay()
    return nc, c


ALL_PHASES = ['w0', 'w1', 'p0', 'p1', 'p2', 'p3', 'p4a', 'p4b', 'p5', 'p7', 'p8']
REAL_INPUTS = ['L0_attn_Wq', 'L0_attn_Wk', 'L0_attn_Wv', 'L0_attn_Wo', 'L0_attn_norm_const',
               'L0_ffn_w_gate', 'L0_ffn_w_up', 'L0_ffn_w_down',
               'L1_attn_Wq', 'L1_attn_Wk', 'L1_attn_Wv', 'L1_attn_Wfc',
               'L1_ffn_W_base', 'L1_ffn_b_base', 'L1_ffn_Wd1', 'L1_ffn_bd1',
               'L1_ffn_Wd2', 'L1_ffn_bd2', 'L1_ffn_Wa_down', 'L1_ffn_ba_down',
               'L1_ffn_Wa_up', 'L1_ffn_ba_up']


def kernel(**inputs):
    seq = 4096
    x = np.ascontiguousarray(inputs['L0_attn_x'], dtype=np.float32)
    nb = x.shape[0]
    weights = {}
    for nm in REAL_INPUTS:
        v = np.ascontiguousarray(inputs[nm], dtype=np.float32)
        if nm == 'L0_attn_norm_const':
            v = v.reshape(1, NH)
        elif nm == 'L1_ffn_b_base':
            v = v.reshape(1, D)
        weights[nm] = v
    nc, c = build_program(ALL_PHASES, seq, set(REAL_INPUTS + ['x0']), {'out'}, NCORES)
    in_maps = []
    for i in range(NCORES):
        d = dict(weights)
        d['x0'] = x[i]
        in_maps.append(d)
    res = run_bass_kernel_spmd(nc, in_maps, core_ids=list(range(NCORES)))
    out = np.stack([np.asarray(res.results[i]['out'], dtype=np.float32)
                    for i in range(NCORES)])
    return out


# version 11
# speedup vs baseline: 1.6228x; 1.0253x over previous
import sys
import numpy as np
import ml_dtypes

import concourse.bass as bass
import concourse.mybir as mybir
from concourse.alu_op_type import AluOpType
from concourse.bass_utils import run_bass_kernel_spmd

F32 = mybir.dt.float32
BF16 = mybir.dt.bfloat16
I32 = mybir.dt.int32
I16 = mybir.dt.int16
U16 = mybir.dt.uint16
AF = mybir.ActivationFunctionType
AX = mybir.AxisListType
OP = AluOpType

D = 1024
FF = 3072
NH = 16
HD = 64
NE = 8
EH = 256
RK = 64
NCORES = 8

ENGINES = ('sync', 'gpsimd', 'vector', 'scalar', 'tensor')


class RecInst:
    __slots__ = ('incs',)
    def __init__(self):
        self.incs = []
    def then_inc(self, sem, n=1):
        self.incs.append((sem, n))
        return self


class Rec:
    _CTRL = frozenset(('wait_ge', 'sem_inc', 'nop'))
    def __init__(self, q):
        self.__dict__['_q'] = q
        self.__dict__['last'] = None
    def __getattr__(self, name):
        q = self._q
        ctrl = name in Rec._CTRL
        rec = self
        def call(*a, **k):
            ri = RecInst()
            q.append((name, a, k, ri))
            if not ctrl:
                rec.__dict__['last'] = (name, ri)
            return ri
        return call


class Ctx:
    def __init__(self, nc, seq, ext_in, ext_out, n_cores=NCORES):
        self.nc = nc
        self.seq = seq
        self.n_cores = n_cores
        self.ntile = seq // 128
        self.nqg = seq // 512
        self.q = {e: [] for e in ENGINES}
        self.rec = {e: Rec(self.q[e]) for e in ENGINES}
        self.cnt = {}
        self.aux = {}
        self._sems = {}
        self._seminfo = {}
        self._clock = 0
        self._semidx = 0
        self.T = {}
        self.P = {}
        self.Dr = {}
        self.ext_in = ext_in
        self.ext_out = ext_out
        self.consts = {}
        self._nbar = 0

    @property
    def sy(self): return self.rec['sync']
    @property
    def g(self): return self.rec['gpsimd']
    @property
    def v(self): return self.rec['vector']
    @property
    def s(self): return self.rec['scalar']
    @property
    def t(self): return self.rec['tensor']


    def sem(self, name):
        info = self._seminfo.setdefault(name, dict(first=None, last=None, dma=False))
        self._use(name)
        return ('SEM', name)

    def _use(self, name):
        self._clock += 1
        info = self._seminfo[name]
        if info['first'] is None:
            info['first'] = self._clock
        info['last'] = self._clock

    def tick(self, ri, name, n=1):
        self.sem(name)
        if n >= 16:
            self._seminfo[name]['dma'] = True
        ri.then_inc(('SEM', name), n)
        self.cnt[name] = self.cnt.get(name, 0) + n
        return self.cnt[name]

    def n(self, name):
        self.sem(name)
        return self.cnt.get(name, 0)

    def _resolve_sems(self):
        names = sorted(self._seminfo, key=lambda nm: self._seminfo[nm]['first'])
        phys = []
        n2h, n2off = {}, {}
        for nm in names:
            info = self._seminfo[nm]
            total = self.cnt.get(nm, 0)
            if info['dma']:
                h = self.nc.semaphore(f'sd_{nm}').__enter__()
                n2h[nm], n2off[nm] = h, 0
                continue
            slot = None
            for p in phys:
                if p['last'] < info['first']:
                    slot = p
                    break
            if slot is None:
                slot = dict(handle=self.nc.semaphore(f'sp{len(phys)}').__enter__(),
                            last=-1, total=0)
                phys.append(slot)
            n2h[nm] = slot['handle']
            n2off[nm] = slot['total']
            slot['total'] += total
            slot['last'] = info['last']
        return n2h, n2off

    def barrier(self, engines=ENGINES):
        self._nbar += 1
        snap = getattr(self, '_bar_snap', {})
        diff = [(nm, v) for nm, v in self.cnt.items()
                if v > 0 and snap.get(nm, -1) != v]
        self._bar_snap = dict(self.cnt)
        for e in engines:
            for nm, v in diff:
                self.sem(nm)
                self.rec[e].wait_ge(('SEM', nm), v)

    VIEW_MAP = {
        'F0':      ('SC0', 0, 2048, 'f32'),
        'F1':      ('SC0', 2048, 4096, 'f32'),
        'F2':      ('SC1', 0, 2048, 'f32'),
        'F3':      ('SC1', 2048, 4096, 'f32'),
        'I0':      ('SC1', 4096, 6144, 'i32'),
        'p3_i16':  ('SC0', 4096, 5120, 'i16'),
        'p3_u16':  ('SC0', 5120, 6144, 'u16'),
        'p4_i16b': ('SC0', 6144, 7168, 'i16'),
        'p4b_x':   ('SC0', 7168, 8192, 'f32'),
        'B3':      ('SC0', 4096, 8192, 'bf16'),
    }
    _VDT = {'f32': F32, 'i32': I32, 'i16': I16, 'u16': U16, 'bf16': BF16}

    def base_tile(self, name):
        if name not in self.T:
            self.T[name] = self.nc.alloc_sbuf_tensor(f't_{name}', [128, 8192 if name == 'SC0' else 6144], F32)
        return self.T[name]

    def tile(self, name, shape, dtype):
        vm = self.VIEW_MAP.get(name)
        if vm is not None:
            base, c0, c1, dts = vm
            bt = self.base_tile(base)
            ap = bt[:, c0:c1]
            tgt = self._VDT[dts]
            if tgt != F32:
                ap = ap.bitcast(tgt)
            return ap
        if name not in self.T:
            self.T[name] = self.nc.alloc_sbuf_tensor(f't_{name}', list(shape), dtype)
        return self.T[name]

    def psum(self, name, shape, dtype=F32):
        if name not in self.P:
            self.P[name] = self.nc.alloc_psum_tensor(f'p_{name}', list(shape), dtype)
        return self.P[name]

    def dram(self, name, shape, dtype):
        if name not in self.Dr:
            if name in self.ext_in:
                kind = 'ExternalInput'
            elif name in self.ext_out:
                kind = 'ExternalOutput'
            else:
                kind = 'Internal'
            self.Dr[name] = self.nc.dram_tensor(name, list(shape), dtype, kind=kind)
        return self.Dr[name]

    def const(self, name, arr):
        if name not in self.consts:
            self.consts[name] = self.nc.inline_tensor(np.ascontiguousarray(arr), f'c_{name}')
        return self.consts[name]

    KCONST_VALS = (0.5, 1e-5, 1.0, -0.5)

    def kbias(self, val):
        t = self.tile('kconst', [128, 8], F32)
        i = self.KCONST_VALS.index(val)
        return t[:, i:i + 1]

    def prologue(self):
        t = self.tile('kconst', [128, 8], F32)
        arr = np.zeros((128, 8), np.float32)
        for i, v in enumerate(self.KCONST_VALS):
            arr[:, i] = v
        h = self.const('kconst', arr)
        r = self.sy.dma_start(out=t[:], in_=h.ap())
        self.tick(r, 'kc', 16)
        for e in ENGINES:
            self.rec[e].wait_ge(self.sem('kc'), 16)

    def replay(self):
        n2h, n2off = self._resolve_sems()

        def xl(v):
            if isinstance(v, tuple) and len(v) == 2 and v[0] == 'SEM':
                return n2h[v[1]]
            return v

        with self.nc.Block() as block:
            for ename in ENGINES:
                qq = self.q[ename]
                def mk(qq):
                    def body(eng):
                        for name, a, k, ri in qq:
                            if name == 'wait_ge':
                                semref, val = a[0], a[1]
                                nm = semref[1]
                                eng.wait_ge(n2h[nm], val + n2off[nm])
                                continue
                            a = tuple(xl(x) for x in a)
                            k = {kk: xl(vv) for kk, vv in k.items()}
                            r = getattr(eng, name)(*a, **k)
                            for semref, n in ri.incs:
                                r.then_inc(xl(semref), n)
                    return body
                getattr(block, ename)(mk(qq))



def tile_b1(c):
    return c.tile('B1', [128, max(c.ntile * 1040, 32768)], BF16)


def ident_bf16(c):
    return c.const('id_bf16', np.eye(128, dtype=ml_dtypes.bfloat16))

def ident_f32(c):
    return c.const('id_f32', np.eye(128, dtype=np.float32))

def ones_row(c):
    return c.const('ones_row', np.ones((1, 128), dtype=np.float32))


def qdq_round(c, pre, u_tile, i32_tile, cfp_tile, m_tile, out_tile, src_ap, rs_ap,
              vsem, psem, asem, clip_lo=None, clip_hi=None, out_scale=None,
              out_slice=None, extra_u_bias=0.5):
    v, s, g = c.v, c.s, c.g
    r = v.tensor_scalar(out=u_tile, in0=src_ap, scalar1=rs_ap, scalar2=extra_u_bias,
                        op0=OP.mult, op1=OP.add)
    nu = c.tick(r, vsem)
    s.wait_ge(c.sem(vsem), nu)
    r = s.activation(out=i32_tile, in_=u_tile, func=AF.Identity)
    na = c.tick(r, asem)
    g.wait_ge(c.sem(asem), na)
    r = g.tensor_copy(out=cfp_tile, in_=i32_tile)
    np_ = c.tick(r, psem)
    v.wait_ge(c.sem(psem), np_)
    r = v.tensor_tensor(out=m_tile, in0=cfp_tile, in1=u_tile, op=OP.is_gt)
    nu = c.tick(r, vsem)
    v.wait_ge(c.sem(vsem), nu)
    tgt = out_tile if out_slice is None else out_slice
    if clip_lo is None and out_scale is None:
        r = v.scalar_tensor_tensor(out=tgt, in0=m_tile, scalar=-1.0, in1=cfp_tile,
                                   op0=OP.mult, op1=OP.add)
        nu = c.tick(r, vsem)
        return nu
    r = v.scalar_tensor_tensor(out=m_tile, in0=m_tile, scalar=-1.0, in1=cfp_tile,
                               op0=OP.mult, op1=OP.add)
    nu = c.tick(r, vsem)
    v.wait_ge(c.sem(vsem), nu)
    if clip_lo is not None and out_scale is None:
        r = v.tensor_scalar(out=tgt, in0=m_tile, scalar1=clip_lo, scalar2=clip_hi,
                            op0=OP.max, op1=OP.min)
        nu = c.tick(r, vsem)
        return nu
    if clip_lo is not None:
        r = v.tensor_scalar(out=m_tile, in0=m_tile, scalar1=clip_lo, scalar2=clip_hi,
                            op0=OP.max, op1=OP.min)
        nu = c.tick(r, vsem)
        v.wait_ge(c.sem(vsem), nu)
    r = v.tensor_scalar(out=tgt, in0=m_tile, scalar1=out_scale, scalar2=None,
                        op0=OP.mult)
    nu = c.tick(r, vsem)
    return nu



def phase_w0(c):
    nc = c.nc
    names = [('L0_attn_Wq', 'wT_l0q'), ('L0_attn_Wk', 'wT_l0k'),
             ('L0_attn_Wv', 'wT_l0v'), ('L0_attn_Wo', 'wT_l0o'),
             ('L1_attn_Wq', 'wT_l1q'), ('L1_attn_Wk', 'wT_l1k'),
             ('L1_attn_Wv', 'wT_l1v'), ('L1_attn_Wfc', 'wT_l1fc')]
    b0 = c.tile('B0', [128, 32768], BF16)
    wbf = b0[:, 0:8192]
    ev = b0[:, 8192:16384]
    svb = c.tile('w0_svb', [128, 16], F32)
    sv_row = c.tile('w0_svr', [1, 16], F32)
    scr1 = c.tile('w0_s1', [1, 16], F32)
    idb = ident_bf16(c)
    onr = ones_row(c)
    idt = c.tile('w0_id', [128, 128], BF16)
    pT = c.psum('pA', [128, 2048], F32)
    pB = c.psum('pB', [128, 1024], F32)
    nct = c.dram('L0_attn_norm_const', [1, 16], F32)

    seqc = c.const('seqval', np.full((1, 1), float(c.seq), np.float32))
    seqt = c.tile('w0_seq', [1, 1], F32)
    lnt = c.tile('w0_ln', [1, 1], F32)
    onrt = c.tile('w0_onr', [1, 128], F32)
    r = c.sy.dma_start(out=idt[:], in_=idb.ap()); c.tick(r, 'w0_ld', 16)
    r = c.sy.dma_start(out=sv_row[:], in_=nct.ap()); c.tick(r, 'w0_ld', 16)
    r = c.sy.dma_start(out=seqt[:], in_=seqc.ap()); c.tick(r, 'w0_ld', 16)
    r = c.sy.dma_start(out=onrt[:], in_=onr.ap()); c.tick(r, 'w0_ld', 16)
    c.s.wait_ge(c.sem('w0_ld'), 64)
    r = c.s.activation(out=scr1[:], in_=sv_row[:], func=AF.Sigmoid)
    c.tick(r, 'w0_a')
    r = c.s.activation(out=lnt[:], in_=seqt[:], func=AF.Ln)
    c.tick(r, 'w0_a')
    c.v.wait_ge(c.sem('w0_a'), 2)
    r = c.v.tensor_scalar(out=scr1[:], in0=scr1[:], scalar1=lnt[0:1, 0:1], scalar2=-1.0,
                          op0=OP.mult, op1=OP.mult)
    c.tick(r, 'w0_v')
    c.s.wait_ge(c.sem('w0_v'), 1)
    r = c.s.activation(out=sv_row[:], in_=scr1[:], func=AF.Exp)
    c.tick(r, 'w0_a')
    c.t.wait_ge(c.sem('w0_a'), 3)
    c.t.wait_ge(c.sem('w0_ld'), 64)
    r = c.t.matmul(out=pB[0:128, 0:16], lhsT=onrt[:], rhs=sv_row[:], start=True, stop=True)
    c.tick(r, 'w0_t')
    c.v.wait_ge(c.sem('w0_t'), 1)
    r = c.v.tensor_copy(out=svb[:], in_=pB[0:128, 0:16])
    c.tick(r, 'w0_v')

    for wi, (src, dst) in enumerate(names):
        wd = c.dram(src, [D, D], F32)
        wt = c.dram(dst, [D, D], BF16)
        r = c.g.dma_start(out=wbf.rearrange('p (a d) -> p a d', a=8), in_=wd.ap().rearrange('(a p) d -> p a d', p=128))
        c.tick(r, 'w0_gld', 16)
        c.v.wait_ge(c.sem('w0_st'), 16 * wi)
        c.t.wait_ge(c.sem('w0_gld'), c.n('w0_gld'))
        if src == 'L0_attn_Wv':
            c.v.wait_ge(c.sem('w0_gld'), c.n('w0_gld'))
            for a in range(8):
                for hh in range(2):
                    h = 2 * a + hh
                    r = c.v.tensor_scalar(
                        out=wbf[hh * 64:(hh + 1) * 64, a * 1024:(a + 1) * 1024],
                        in0=wbf[hh * 64:(hh + 1) * 64, a * 1024:(a + 1) * 1024],
                        scalar1=svb[hh * 64:(hh + 1) * 64, h:h + 1], scalar2=None,
                        op0=OP.mult)
                    c.tick(r, 'w0_v')
            c.t.wait_ge(c.sem('w0_v'), c.n('w0_v'))
        for a in range(8):
            for b in range(8):
                g_idx = wi * 64 + a * 8 + b
                bank = g_idx % 2
                pv_view = pT[:, bank * 1024:bank * 1024 + 128].bitcast(BF16)[:, 0:128]
                if g_idx >= 2:
                    c.t.wait_ge(c.sem('w0_ev'), c.n('w0_ev') - 1)
                r = c.t.transpose(out=pv_view, in_=wbf[:, a * 1024 + b * 128: a * 1024 + b * 128 + 128],
                                  identity=idt[:])
                nt = c.tick(r, 'w0_t')
                c.v.wait_ge(c.sem('w0_t'), nt)
                r = c.v.tensor_copy(out=ev[:, b * 1024 + a * 128: b * 1024 + a * 128 + 128],
                                    in_=pv_view)
                c.tick(r, 'w0_ev')
        c.sy.wait_ge(c.sem('w0_ev'), c.n('w0_ev'))
        r = c.sy.dma_start(out=wt.ap().rearrange('(b p) o -> p b o', p=128), in_=ev.rearrange('p (b o) -> p b o', b=8))
        c.tick(r, 'w0_st', 16)
        c.g.wait_ge(c.sem('w0_t'), c.n('w0_t'))
    c.sy.wait_ge(c.sem('w0_st'), c.n('w0_st'))



def phase_w1(c):
    f0 = c.tile('F0', [128, 2048], F32)
    f1 = c.tile('F1', [128, 2048], F32)
    f2 = c.tile('F2', [128, 2048], F32)
    f3 = c.tile('F3', [128, 2048], F32)
    i0 = c.tile('I0', [128, 2048], I32)
    b0 = c.tile('B0', [128, 32768], BF16)
    dq = b0[:, 0:8192]
    ev = b0[:, 8192:16384]
    idt = c.tile('w0_id', [128, 128], BF16)
    onrt = c.tile('w0_onr', [1, 128], F32)
    bm = c.tile('w1_bm', [128, 96], F32)
    sblk = c.tile('w1_s', [128, 96], F32)
    rsblk = c.tile('w1_rs', [128, 96], F32)
    sc7 = c.tile('w1_7', [1, 1], F32)
    r7 = c.tile('w1_r7', [1, 1], F32)
    r7b = c.tile('w1_r7b', [128, 1], F32)
    pT = c.psum('pA', [128, 2048], F32)
    pB = c.psum('pB', [128, 1024], F32)

    c7 = c.const('c7', np.full((1, 1), 7.0, np.float32))
    r = c.sy.dma_start(out=sc7[:], in_=c7.ap()); c.tick(r, 'w1_ld', 16)
    r = c.sy.dma_start(out=idt[:], in_=ident_bf16(c).ap()); c.tick(r, 'w1_ld', 16)
    r = c.sy.dma_start(out=onrt[:], in_=ones_row(c).ap()); c.tick(r, 'w1_ld', 16)
    c.v.wait_ge(c.sem('w1_ld'), 48)
    r = c.v.reciprocal(out=r7[:], in_=sc7[:]); c.tick(r, 'w1_v')
    c.t.wait_ge(c.sem('w1_v'), c.n('w1_v'))
    r = c.t.matmul(out=pB[0:128, 0:1], lhsT=onrt[:], rhs=r7[:], start=True, stop=True)
    c.tick(r, 'w1_t')
    c.v.wait_ge(c.sem('w1_t'), c.n('w1_t'))
    r = c.v.tensor_copy(out=r7b[:], in_=pB[0:128, 0:1]); c.tick(r, 'w1_v')

    specs = [('L0_ffn_w_up', 'wT_up', FF, D), ('L0_ffn_w_gate', 'wT_gate', FF, D),
             ('L0_ffn_w_down', 'wT_down', D, FF)]
    sc0b = c.base_tile('SC0')
    wslot = (f0[:, 0:2048], sc0b[:, 4096:6144])
    blocks = []
    for src_, dst, O, I in specs:
        wd = c.dram(src_, [O, I], F32)
        wt = c.dram(dst, [I, O], BF16)
        ics = [(0, 2048), (2048, 1024)] if I == 3072 else [(0, 1024)]
        for ti in range(O // 128):
            for ci, (i_off, i_w) in enumerate(ics):
                blocks.append((wd, wt, ti, i_off, i_w, ci == len(ics) - 1, I))
    nmw = {}

    def load(k):
        wd, wt, ti, i_off, i_w, last, I = blocks[k]
        s = k % 2
        if k >= 2:
            c.sy.wait_ge(c.sem('w1_v'), nmw[k - 2])
        r = c.sy.dma_start(out=wslot[s][:, 0:i_w],
                           in_=wd.ap()[ti * 128:(ti + 1) * 128, i_off:i_off + i_w])
        return c.tick(r, 'w1_l%d' % s, 16)

    nlds = {}
    nlds[0] = load(0)
    for k in range(len(blocks)):
        if k + 1 < len(blocks):
            nlds[k + 1] = load(k + 1)
        wd, wt, ti, i_off, i_w, last, I = blocks[k]
        s = k % 2
        nb = i_w // 32
        wa = wslot[s][:, 0:i_w]
        c.v.wait_ge(c.sem('w1_l%d' % s), nlds[k])
        r = c.v.reduce_max(out=bm[:, 0:nb], in_=wa.rearrange('p (b k) -> p b k', k=32),
                           axis=AX.X, apply_absolute_value=True)
        nv = c.tick(r, 'w1_v')
        c.v.wait_ge(c.sem('w1_v'), nv)
        r = c.v.tensor_scalar(out=sblk[:, 0:nb], in0=bm[:, 0:nb], scalar1=r7b[:, 0:1],
                              scalar2=1e-12, op0=OP.mult, op1=OP.max)
        nv = c.tick(r, 'w1_v')
        c.v.wait_ge(c.sem('w1_v'), nv)
        r = c.v.reciprocal(out=rsblk[:, 0:nb], in_=sblk[:, 0:nb])
        nv = c.tick(r, 'w1_v')
        c.v.wait_ge(c.sem('w1_v'), nv)
        c.v.wait_ge(c.sem('w1_a'), c.n('w1_a'))
        r = c.v.tensor_tensor(out=f1[:, 0:i_w].rearrange('p (b k) -> p b k', k=32),
                              in0=wa.rearrange('p (b k) -> p b k', k=32),
                              in1=rsblk[:, 0:nb].unsqueeze(2).broadcast_to([128, nb, 32]),
                              op=OP.mult)
        nmw[k] = nv = c.tick(r, 'w1_v')
        c.s.wait_ge(c.sem('w1_v'), nv)
        r = c.s.activation(out=i0[:, 0:i_w], in_=f1[:, 0:i_w], func=AF.Identity, bias=c.kbias(0.5))
        na = c.tick(r, 'w1_a')
        c.s.wait_ge(c.sem('w1_a'), na)
        r = c.s.activation(out=f2[:, 0:i_w], in_=i0[:, 0:i_w], func=AF.Identity)
        na = c.tick(r, 'w1_a')
        c.v.wait_ge(c.sem('w1_a'), na)
        r = c.v.scalar_tensor_tensor(out=f3[:, 0:i_w], in0=f2[:, 0:i_w], scalar=-0.5,
                                     in1=f1[:, 0:i_w], op0=OP.add, op1=OP.is_gt)
        nv = c.tick(r, 'w1_v')
        c.v.wait_ge(c.sem('w1_v'), nv)
        r = c.v.scalar_tensor_tensor(out=f1[:, 0:i_w], in0=f3[:, 0:i_w], scalar=-1.0,
                                     in1=f2[:, 0:i_w], op0=OP.mult, op1=OP.add)
        nv = c.tick(r, 'w1_v')
        c.v.wait_ge(c.sem('w1_v'), nv)
        r = c.v.tensor_tensor(out=dq[:, i_off:i_off + i_w].rearrange('p (b k) -> p b k', k=32),
                              in0=f1[:, 0:i_w].rearrange('p (b k) -> p b k', k=32),
                              in1=sblk[:, 0:nb].unsqueeze(2).broadcast_to([128, nb, 32]),
                              op=OP.mult)
        c.tick(r, 'w1_v')
        if last:
            c.t.wait_ge(c.sem('w1_v'), c.n('w1_v'))
            for b in range(I // 128):
                g_idx = c.aux.get('w1_tt', 0)
                c.aux['w1_tt'] = g_idx + 1
                bank = g_idx % 2
                pv = pT[:, bank * 1024:bank * 1024 + 128].bitcast(BF16)[:, 0:128]
                if g_idx >= 2:
                    c.t.wait_ge(c.sem('w1_ev'), c.n('w1_ev') - 1)
                r = c.t.transpose(out=pv, in_=dq[:, b * 128:(b + 1) * 128], identity=idt[:])
                nt = c.tick(r, 'w1_t')
                c.v.wait_ge(c.sem('w1_t'), nt)
                r = c.v.tensor_copy(out=ev[:, b * 128:(b + 1) * 128], in_=pv)
                c.tick(r, 'w1_ev')
            c.g.wait_ge(c.sem('w1_ev'), c.n('w1_ev'))
            r = c.g.dma_start(out=wt.ap().rearrange('(b p) o -> p b o', p=128)[:, :, ti * 128:(ti + 1) * 128],
                              in_=ev[:, 0:I].rearrange('p (b o) -> p b o', o=128))
            nst = c.tick(r, 'w1_st', 16)
            c.v.wait_ge(c.sem('w1_st'), nst)
    c.g.wait_ge(c.sem('w1_st'), c.n('w1_st'))



def layernorm_tiles(c, pre, x_dram, xn_tok_dram, xn_fm_dram, amax_tile=None):
    f0 = c.tile('F0', [128, 2048], F32)
    f1 = c.tile('F1', [128, 2048], F32)
    f2 = c.tile('F2', [128, 2048], F32)
    st = c.tile('p0_st', [128, 16], F32)
    xnb = c.tile('B3', None, None)[:, 0:4096]
    stg = c.tile('B2', [128, 4096], BF16)
    idt = c.tile('w0_id', [128, 128], BF16)
    pT = c.psum('pA', [128, 2048], F32)
    ld, st_s, vs, as_, ts_, evs, gs = (pre + x for x in ('_ld', '_st', '_v', '_a', '_t', '_ev', '_g'))

    r = c.sy.dma_start(out=idt[:], in_=ident_bf16(c).ap()); c.tick(r, ld, 16)
    if amax_tile is not None:
        r = c.v.memset(amax_tile, 0.0); c.tick(r, vs)
    H = dict(nld={}, nsub={}, nsq={}, nsqrt={}, nmul={}, nbf={}, ntr={}, nst={})

    def stage_a(i):
        s = i % 2
        xa = f0[:, s * 1024:(s + 1) * 1024]
        if i >= 2:
            c.sy.wait_ge(c.sem(vs), H['nsub'][i - 2])
        r = c.sy.dma_start(out=xa, in_=x_dram.ap().rearrange('(t p) d -> t p d', p=128)[i])
        H['nld'][i] = nld = c.tick(r, ld + str(s), 16)
        c.v.wait_ge(c.sem(ld + str(s)), nld)
        r = c.v.reduce_sum(out=st[:, s * 8:s * 8 + 1], in_=xa, axis=AX.X); nv = c.tick(r, vs)
        c.v.wait_ge(c.sem(vs), nv)
        r = c.v.tensor_scalar(out=st[:, s * 8 + 1:s * 8 + 2], in0=st[:, s * 8:s * 8 + 1],
                              scalar1=1.0 / 1024, scalar2=None, op0=OP.mult); nv = c.tick(r, vs)
        c.v.wait_ge(c.sem(vs), nv)
        if i >= 2:
            c.v.wait_ge(c.sem(as_), H['nsq'][i - 2])
        r = c.v.tensor_scalar(out=f1[:, s * 1024:(s + 1) * 1024], in0=xa,
                              scalar1=st[:, s * 8 + 1:s * 8 + 2], scalar2=None,
                              op0=OP.subtract)
        H['nsub'][i] = c.tick(r, vs)
        c.s.wait_ge(c.sem(vs), H['nsub'][i])
        if xn_tok_dram is not None and i >= 2:
            c.s.wait_ge(c.sem(st_s + str(s)), H['nst'][i - 2])
        r = c.s.activation(out=f2[:, s * 1024:(s + 1) * 1024], in_=f1[:, s * 1024:(s + 1) * 1024],
                           func=AF.Square, accum_out=st[:, s * 8 + 2:s * 8 + 3])
        H['nsq'][i] = na = c.tick(r, as_)
        c.s.wait_ge(c.sem(as_), na)
        r = c.s.activation(out=st[:, s * 8 + 3:s * 8 + 4], in_=st[:, s * 8 + 2:s * 8 + 3],
                           func=AF.Sqrt, scale=1.0 / 1024, bias=c.kbias(1e-5))
        H['nsqrt'][i] = c.tick(r, as_)

    def stage_b(i):
        s = i % 2
        c.v.wait_ge(c.sem(as_), H['nsqrt'][i])
        r = c.v.reciprocal(out=st[:, s * 8 + 4:s * 8 + 5], in_=st[:, s * 8 + 3:s * 8 + 4])
        nv = c.tick(r, vs)
        c.v.wait_ge(c.sem(vs), nv)
        r = c.v.tensor_scalar(out=f2[:, s * 1024:(s + 1) * 1024],
                              in0=f1[:, s * 1024:(s + 1) * 1024],
                              scalar1=st[:, s * 8 + 4:s * 8 + 5], scalar2=None, op0=OP.mult)
        H['nmul'][i] = nv = c.tick(r, vs)
        if xn_tok_dram is not None:
            c.sy.wait_ge(c.sem(vs), nv)
            r = c.sy.dma_start(out=xn_tok_dram.ap().rearrange('(t p) d -> t p d', p=128)[i],
                               in_=f2[:, s * 1024:(s + 1) * 1024])
            H['nst'][i] = c.tick(r, st_s + str(s), 16)
        if amax_tile is not None:
            c.v.wait_ge(c.sem(vs), nv)
            r = c.v.reduce_max(out=st[:, s * 8 + 5:s * 8 + 6], in_=f2[:, s * 1024:(s + 1) * 1024],
                               axis=AX.X, apply_absolute_value=True); nv = c.tick(r, vs)
            c.v.wait_ge(c.sem(vs), nv)
            r = c.v.tensor_tensor(out=amax_tile, in0=amax_tile, in1=st[:, s * 8 + 5:s * 8 + 6],
                                  op=OP.max)
            nv = c.tick(r, vs)
        c.v.wait_ge(c.sem(vs), nv)
        if i >= 2:
            c.v.wait_ge(c.sem(ts_), H['ntr'][i - 2])
        r = c.v.tensor_copy(out=xnb[:, s * 1024:(s + 1) * 1024],
                            in_=f2[:, s * 1024:(s + 1) * 1024])
        H['nbf'][i] = nv = c.tick(r, vs)
        c.t.wait_ge(c.sem(vs), nv)
        if i == 0:
            c.t.wait_ge(c.sem(ld), 16)
        col = (i % 4) * 128
        for b in range(8):
            g_idx = c.aux.get(pre + '_tt', 0)
            c.aux[pre + '_tt'] = g_idx + 1
            bank = g_idx % 2
            pv = pT[:, bank * 1024:bank * 1024 + 128].bitcast(BF16)[:, 0:128]
            if g_idx >= 2:
                c.t.wait_ge(c.sem(evs), c.n(evs) - 1)
            r = c.t.transpose(out=pv, in_=xnb[:, s * 1024 + b * 128: s * 1024 + (b + 1) * 128],
                              identity=idt[:])
            H['ntr'][i] = nt = c.tick(r, ts_)
            c.v.wait_ge(c.sem(ts_), nt)
            if i % 4 == 0 and b == 0:
                c.v.wait_ge(c.sem(gs), c.n(gs))
            r = c.v.tensor_copy(out=stg[:, b * 512 + col: b * 512 + col + 128], in_=pv)
            c.tick(r, evs)
        if i % 4 == 3:
            c.g.wait_ge(c.sem(evs), c.n(evs))
            j = i // 4
            r = c.g.dma_start(
                out=xn_fm_dram.ap().rearrange('(a p) s -> p a s', p=128)[:, :, j * 512:(j + 1) * 512],
                in_=stg[:, 0:4096].rearrange('p (a s) -> p a s', s=512))
            c.tick(r, gs, 16)

    for i in range(c.ntile + 1):
        if i < c.ntile:
            stage_a(i)
        if i >= 1:
            stage_b(i - 1)
    c.g.wait_ge(c.sem(gs), c.n(gs))
    if xn_tok_dram is not None:
        c.sy.wait_ge(c.sem(st_s + '0'), c.n(st_s + '0'))
        c.sy.wait_ge(c.sem(st_s + '1'), c.n(st_s + '1'))


def global_absmax_to_scale(c, pre, amax_tile, qmax, s_out, rs_out):
    onrt = c.tile('w0_onr', [1, 128], F32)
    idf = c.tile('p0_idf', [128, 128], F32)
    rowm = c.tile('p0_rowm', [1, 128], F32)
    sc = c.tile('p0_sc', [1, 8], F32)
    qm = c.tile('p0_qm', [1, 1], F32)
    rqm = c.tile('p0_rqm', [1, 1], F32)
    pB = c.psum('pB', [128, 1024], F32)
    cc_in = c.dram(pre + '_ccin', [1, 8], F32)
    cc_out = c.dram(pre + '_ccout', [1, 8], F32)

    r = c.sy.dma_start(out=idf[:], in_=ident_f32(c).ap()); c.tick(r, pre + '_ld2', 16)
    qmc = c.const('qmax_%d' % int(qmax), np.full((1, 1), float(qmax), np.float32))
    r = c.sy.dma_start(out=qm[:], in_=qmc.ap()); c.tick(r, pre + '_ld2', 16)
    r = c.sy.dma_start(out=onrt[:], in_=ones_row(c).ap()); c.tick(r, pre + '_ld2', 16)
    c.t.wait_ge(c.sem(pre + '_ld2'), 48)
    r = c.t.matmul(out=pB[0:1, 0:128], lhsT=amax_tile, rhs=idf[:], start=True, stop=True)
    nt = c.tick(r, pre + '_t2')
    c.v.wait_ge(c.sem(pre + '_t2'), nt)
    r = c.v.reduce_max(out=rowm[0:1, 0:1], in_=pB[0:1, 0:128], axis=AX.X)
    nv = c.tick(r, pre + '_v2')
    c.v.wait_ge(c.sem(pre + '_v2'), nv)
    r = c.v.tensor_copy(out=sc[:], in_=rowm[0:1, 0:1].broadcast_to([1, 8]))
    nv = c.tick(r, pre + '_v2')
    c.g.wait_ge(c.sem(pre + '_v2'), nv)
    r = c.g.dma_start(out=cc_in.ap(), in_=sc[:]); c.tick(r, pre + '_g2', 16)
    c.g.wait_ge(c.sem(pre + '_g2'), c.n(pre + '_g2'))
    r = c.g.collective_compute('AllReduce', OP.max,
                               replica_groups=[list(range(c.n_cores))],
                               ins=[cc_in.ap()], outs=[cc_out.ap()])
    c.tick(r, pre + '_cc')
    c._seminfo[pre + '_cc']['dma'] = True
    c.g.wait_ge(c.sem(pre + '_cc'), 1)
    r = c.g.dma_start(out=sc[:], in_=cc_out.ap()); c.tick(r, pre + '_g2', 16)
    c.v.wait_ge(c.sem(pre + '_g2'), c.n(pre + '_g2'))
    r = c.v.reciprocal(out=rqm[:], in_=qm[:]); nv = c.tick(r, pre + '_v2')
    c.v.wait_ge(c.sem(pre + '_v2'), nv)
    r = c.v.tensor_scalar(out=sc[0:1, 1:2], in0=sc[0:1, 0:1], scalar1=rqm[0:1, 0:1],
                          scalar2=1e-12, op0=OP.mult, op1=OP.max)
    nv = c.tick(r, pre + '_v2')
    c.v.wait_ge(c.sem(pre + '_v2'), nv)
    r = c.v.reciprocal(out=sc[0:1, 2:3], in_=sc[0:1, 1:2]); nv = c.tick(r, pre + '_v2')
    c.t.wait_ge(c.sem(pre + '_v2'), nv)
    r = c.t.matmul(out=pB[0:128, 8:10], lhsT=onrt[:], rhs=sc[0:1, 1:3], start=True, stop=True)
    nt = c.tick(r, pre + '_t2')
    c.v.wait_ge(c.sem(pre + '_t2'), nt)
    r = c.v.tensor_copy(out=s_out, in_=pB[0:128, 8:9]); c.tick(r, pre + '_v2')
    r = c.v.tensor_copy(out=rs_out, in_=pB[0:128, 9:10]); nv = c.tick(r, pre + '_v2')
    return nv


def phase_p0(c):
    x_dram = c.dram('x0', [c.seq, D], F32)
    xn_tok = c.dram('xn_tok_l0', [c.seq, D], F32)
    xn_fm = c.dram('xn_fm_l0', [D, c.seq], BF16)
    xq_fm = c.dram('xq_fm', [D, c.seq], BF16)
    amax = c.tile('p0_amax', [128, 1], F32)
    layernorm_tiles(c, 'p0', x_dram, xn_tok, xn_fm, amax_tile=amax[:])
    c.barrier()
    s1b = c.tile('p0_s1b', [128, 1], F32)
    rs1b = c.tile('p0_rs1b', [128, 1], F32)
    global_absmax_to_scale(c, 'p0c', amax[:], 32767.0, s1b[:], rs1b[:])
    c.barrier()
    f0 = c.tile('F0', [128, 2048], F32)
    f1 = c.tile('F1', [128, 2048], F32)
    f2 = c.tile('F2', [128, 2048], F32)
    f3 = c.tile('F3', [128, 2048], F32)
    i0 = c.tile('I0', [128, 2048], I32)
    xnb = c.tile('B3', None, None)[:, 0:4096]
    stg = c.tile('B2', [128, 4096], BF16)
    idt = c.tile('w0_id', [128, 128], BF16)
    pT = c.psum('pA', [128, 2048], F32)
    Hq = dict(nld={}, nts={}, na={}, np={}, nstt={}, nsc={}, ntr={})

    def qstage_a(i):
        s = i % 2
        if i >= 2:
            c.sy.wait_ge(c.sem('q0_v'), Hq['nts'][i - 2])
        r = c.sy.dma_start(out=f0[:, s * 1024:(s + 1) * 1024],
                           in_=xn_tok.ap().rearrange('(t p) d -> t p d', p=128)[i])
        Hq['nld'][i] = nld = c.tick(r, 'q0_l%d' % s, 16)
        c.v.wait_ge(c.sem('q0_l%d' % s), nld)
        if i >= 2:
            c.v.wait_ge(c.sem('q0_a'), Hq['na'][i - 2])
        r = c.v.tensor_scalar(out=f1[:, s * 1024:(s + 1) * 1024],
                              in0=f0[:, s * 1024:(s + 1) * 1024], scalar1=rs1b[:, 0:1],
                              scalar2=None, op0=OP.mult)
        Hq['nts'][i] = nv = c.tick(r, 'q0_v')
        c.s.wait_ge(c.sem('q0_v'), nv)
        if i >= 2:
            c.s.wait_ge(c.sem('q0_p'), Hq['np'][i - 2])
        r = c.s.activation(out=i0[:, s * 1024:(s + 1) * 1024],
                           in_=f1[:, s * 1024:(s + 1) * 1024], func=AF.Identity, bias=c.kbias(0.5))
        Hq['na'][i] = na = c.tick(r, 'q0_a')
        c.g.wait_ge(c.sem('q0_a'), na)
        if i >= 2:
            c.g.wait_ge(c.sem('q0_v'), Hq['nstt'][i - 2])
        r = c.g.tensor_copy(out=f2[:, s * 1024:(s + 1) * 1024], in_=i0[:, s * 1024:(s + 1) * 1024])
        Hq['np'][i] = c.tick(r, 'q0_p')

    def qstage_b(i):
        s = i % 2
        c.v.wait_ge(c.sem('q0_p'), Hq['np'][i])
        r = c.v.scalar_tensor_tensor(out=f3[:, s * 1024:(s + 1) * 1024],
                                     in0=f2[:, s * 1024:(s + 1) * 1024], scalar=-0.5,
                                     in1=f1[:, s * 1024:(s + 1) * 1024], op0=OP.add, op1=OP.is_gt)
        nv = c.tick(r, 'q0_v')
        c.v.wait_ge(c.sem('q0_v'), nv)
        r = c.v.scalar_tensor_tensor(out=f1[:, s * 1024:(s + 1) * 1024],
                                     in0=f3[:, s * 1024:(s + 1) * 1024], scalar=-1.0,
                                     in1=f2[:, s * 1024:(s + 1) * 1024], op0=OP.mult, op1=OP.add)
        Hq['nstt'][i] = nv = c.tick(r, 'q0_v')
        c.v.wait_ge(c.sem('q0_v'), nv)
        if i >= 2:
            c.v.wait_ge(c.sem('q0_t'), Hq['ntr'][i - 2])
        r = c.v.tensor_scalar(out=xnb[:, s * 1024:(s + 1) * 1024],
                              in0=f1[:, s * 1024:(s + 1) * 1024], scalar1=s1b[:, 0:1],
                              scalar2=None, op0=OP.mult)
        Hq['nsc'][i] = nv = c.tick(r, 'q0_v')
        c.t.wait_ge(c.sem('q0_v'), nv)
        col = (i % 4) * 128
        for b in range(8):
            g_idx = c.aux.get('q0_tt', 0)
            c.aux['q0_tt'] = g_idx + 1
            bank = g_idx % 2
            pv = pT[:, bank * 1024:bank * 1024 + 128].bitcast(BF16)[:, 0:128]
            if g_idx >= 2:
                c.t.wait_ge(c.sem('q0_ev'), c.n('q0_ev') - 1)
            r = c.t.transpose(out=pv, in_=xnb[:, s * 1024 + b * 128: s * 1024 + (b + 1) * 128],
                              identity=idt[:])
            Hq['ntr'][i] = nt = c.tick(r, 'q0_t')
            c.v.wait_ge(c.sem('q0_t'), nt)
            if i % 4 == 0 and b == 0:
                c.v.wait_ge(c.sem('q0_g'), c.n('q0_g'))
            r = c.v.tensor_copy(out=stg[:, b * 512 + col: b * 512 + col + 128], in_=pv)
            c.tick(r, 'q0_ev')
        if i % 4 == 3:
            c.g.wait_ge(c.sem('q0_ev'), c.n('q0_ev'))
            j = i // 4
            r = c.g.dma_start(
                out=xq_fm.ap().rearrange('(a p) s -> p a s', p=128)[:, :, j * 512:(j + 1) * 512],
                in_=stg[:, 0:4096].rearrange('p (a s) -> p a s', s=512))
            c.tick(r, 'q0_g', 16)

    for i in range(c.ntile + 1):
        if i < c.ntile:
            qstage_a(i)
        if i >= 1:
            qstage_b(i - 1)
    c.g.wait_ge(c.sem('q0_g'), c.n('q0_g'))



def _proj_tok(c, pre, xfm, w_sb, a_cols, out_ps, nmm_sem):
    n = 0
    for half in range(2):
        for a in range(8):
            r = c.t.matmul(out=out_ps[:, half * 512:(half + 1) * 512],
                           lhsT=xfm[:, a * 128:(a + 1) * 128],
                           rhs=w_sb[:, a_cols + a * 1024 + half * 512: a_cols + a * 1024 + half * 512 + 512],
                           start=(a == 0), stop=(a == 7))
            n = c.tick(r, nmm_sem)
    return n


def _l2norm_tok(c, pre, src_f32, dst_bf, st, vsem, asem):
    r = c.v.reduce_sum(out=st[:, 0:16], in_=src_f32.rearrange('p (h e) -> p h e', e=HD),
                       axis=AX.X, apply_absolute_value=False)
    return r


def phase_p1(c):
    seq = c.seq
    x0 = c.dram('x0', [seq, D], F32)
    x1 = c.dram('x1', [seq, D], F32)
    xn_fm = c.dram('xn_fm_l0', [D, seq], BF16)
    wk = c.dram('wT_l0k', [D, D], BF16)
    wv = c.dram('wT_l0v', [D, D], BF16)
    wq = c.dram('wT_l0q', [D, D], BF16)
    wo = c.dram('wT_l0o', [D, D], BF16)

    w_sb = c.tile('B0', [128, 32768], BF16)
    sc1_ = c.base_tile('SC1')
    f3bf = sc1_[:, 2048:4096].bitcast(BF16)
    i0bf = sc1_[:, 4096:6144].bitcast(BF16)
    xfm = f3bf[:, 0:1024]
    KB = f3bf[:, 1024:2048]
    VB = f3bf[:, 2048:3072]
    afm = f3bf[:, 3072:4096]
    kvb = i0bf[:, 0:1024]
    f0 = c.tile('F0', [128, 2048], F32)
    f1 = c.tile('F1', [128, 2048], F32)
    f2 = c.tile('F2', [128, 2048], F32)
    st = c.tile('p0_st', [128, 8], F32)
    stq = c.tile('p1_stq', [128, 48], F32)
    idt = c.tile('w0_id', [128, 128], BF16)
    pA = c.psum('pA', [128, 2048], F32)
    pB = c.psum('pB', [128, 1024], F32)
    pC = c.psum('pC', [128, 1024], F32)

    r = c.sy.dma_start(out=idt[:], in_=ident_bf16(c).ap()); c.tick(r, 'p1_w', 16)
    r = c.sy.dma_start(out=w_sb[:, 0:8192].rearrange('p (a o) -> p a o', a=8),
                       in_=wk.ap().rearrange('(a p) o -> p a o', p=128))
    c.tick(r, 'p1_w', 16)
    r = c.sy.dma_start(out=w_sb[:, 8192:16384].rearrange('p (a o) -> p a o', a=8),
                       in_=wv.ap().rearrange('(a p) o -> p a o', p=128))
    c.tick(r, 'p1_w', 16)
    c.t.wait_ge(c.sem('p1_w'), 48)

    def sq_norm_scale(src_f32, dst_bf, col0):
        ss, no, rn = (stq[:, col0:col0 + 16], stq[:, col0 + 16:col0 + 32],
                      stq[:, col0 + 32:col0 + 48])
        c.s.wait_ge(c.sem('p1_v'), c.n('p1_v'))
        c.s.wait_ge(c.sem('p1_g'), c.n('p1_g'))
        r = c.s.activation(out=f1[:, 0:1024], in_=src_f32, func=AF.Square)
        na = c.tick(r, 'p1_a')
        c.v.wait_ge(c.sem('p1_a'), na)
        r = c.v.reduce_sum(out=ss, in_=f1[:, 0:1024].rearrange('p (h e) -> p h e', e=HD),
                           axis=AX.X)
        nv = c.tick(r, 'p1_v')
        c.s.wait_ge(c.sem('p1_v'), nv)
        r = c.s.activation(out=no, in_=ss, func=AF.Sqrt)
        na = c.tick(r, 'p1_a')
        c.v.wait_ge(c.sem('p1_a'), na)
        r = c.v.tensor_scalar(out=no, in0=no, scalar1=1e-12, scalar2=None, op0=OP.max)
        nv = c.tick(r, 'p1_v')
        c.v.wait_ge(c.sem('p1_v'), nv)
        r = c.v.reciprocal(out=rn, in_=no)
        nv = c.tick(r, 'p1_v')
        c.v.wait_ge(c.sem('p1_v'), nv)
        nv = 0
        for h in range(NH):
            r = c.v.tensor_scalar(out=dst_bf[:, h * HD:(h + 1) * HD],
                                  in0=src_f32[:, h * HD:(h + 1) * HD],
                                  scalar1=rn[:, h:h + 1], scalar2=None, op0=OP.mult)
            nv = c.tick(r, 'p1_v')
        return nv

    for i in range(c.ntile):
        c.sy.wait_ge(c.sem('p1_t'), c.n('p1_t'))
        r = c.sy.dma_start(out=xfm.rearrange('p (a w) -> p a w', a=8),
                           in_=xn_fm.ap().rearrange('(a p) s -> p a s', p=128)[:, :, i * 128:(i + 1) * 128])
        nld = c.tick(r, 'p1_ld', 16)
        c.t.wait_ge(c.sem('p1_ld'), nld)
        if i > 0:
            c.t.wait_ge(c.sem('p1_v'), c.n('p1_v'))
            c.t.wait_ge(c.sem('p1_a'), c.n('p1_a'))
        _proj_tok(c, 'p1', xfm, w_sb, 0, pA[:, 0:1024], 'p1_t')
        nmmv = _proj_tok(c, 'p1', xfm, w_sb, 8192, pA[:, 1024:2048], 'p1_t')
        c.v.wait_ge(c.sem('p1_t'), nmmv)
        r = c.v.tensor_copy(out=f0[:, 0:1024], in_=pA[:, 0:1024])
        c.tick(r, 'p1_v')
        c.s.wait_ge(c.sem('p1_t'), nmmv)
        r = c.s.activation(out=VB, in_=pA[:, 1024:2048], func=AF.Identity)
        nva = c.tick(r, 'p1_a')
        c.v.wait_ge(c.sem('p1_v'), c.n('p1_v'))
        nkn = sq_norm_scale(f0[:, 0:1024], KB, 0)
        c.t.wait_ge(c.sem('p1_v'), nkn)
        c.t.wait_ge(c.sem('p1_a'), nva)
        for h in range(NH):
            par = h % 2
            r = c.t.matmul(out=pB[par * 64:par * 64 + 64,
                                  par * 512 + (h // 2) * 64: par * 512 + (h // 2) * 64 + 64],
                           lhsT=KB[:, h * 64:(h + 1) * 64], rhs=VB[:, h * 64:(h + 1) * 64],
                           start=(i == 0 and h in (0, 1)),
                           stop=(i == c.ntile - 1 and h in (14, 15)),
                           tile_position=(0, par * 64))
            c.tick(r, 'p1_t')

    c.v.wait_ge(c.sem('p1_t'), c.n('p1_t'))
    r = c.v.tensor_copy(out=kvb[0:64, 0:512], in_=pB[0:64, 0:512])
    c.tick(r, 'p1_v')
    r = c.v.tensor_copy(out=kvb[64:128, 512:1024], in_=pB[64:128, 512:1024])
    nkv = c.tick(r, 'p1_v')
    c.sy.wait_ge(c.sem('p1_t'), c.n('p1_t'))
    r = c.sy.dma_start(out=w_sb[:, 0:8192].rearrange('p (a o) -> p a o', a=8),
                       in_=wq.ap().rearrange('(a p) o -> p a o', p=128))
    c.tick(r, 'p1_w', 16)
    r = c.sy.dma_start(out=w_sb[:, 8192:16384].rearrange('p (a o) -> p a o', a=8),
                       in_=wo.ap().rearrange('(a p) o -> p a o', p=128))
    nld2 = c.tick(r, 'p1_w', 16)
    c.t.wait_ge(c.sem('p1_w'), nld2)
    c.t.wait_ge(c.sem('p1_v'), nkv)

    for i in range(c.ntile):
        c.sy.wait_ge(c.sem('p1_t'), c.n('p1_t'))
        r = c.sy.dma_start(out=xfm.rearrange('p (a w) -> p a w', a=8),
                           in_=xn_fm.ap().rearrange('(a p) s -> p a s', p=128)[:, :, i * 128:(i + 1) * 128])
        nld = c.tick(r, 'p1_ld', 16)
        c.sy.wait_ge(c.sem('p1_v'), c.n('p1_v'))
        r = c.sy.dma_start(out=f2[:, 0:1024],
                           in_=x0.ap().rearrange('(t p) d -> t p d', p=128)[i])
        nldx = c.tick(r, 'p1_ld', 16)
        c.t.wait_ge(c.sem('p1_ld'), nldx)
        if i > 0:
            c.t.wait_ge(c.sem('p1_v'), c.n('p1_v'))
            c.t.wait_ge(c.sem('p1_a'), c.n('p1_a'))
        nmm = _proj_tok(c, 'p1', xfm, w_sb, 0, pA[:, 0:1024], 'p1_t')
        c.v.wait_ge(c.sem('p1_t'), nmm)
        r = c.v.tensor_copy(out=f0[:, 0:1024], in_=pA[:, 0:1024])
        c.tick(r, 'p1_v')
        c.v.wait_ge(c.sem('p1_v'), c.n('p1_v'))
        nqn = sq_norm_scale(f0[:, 0:1024], KB, 0)
        c.t.wait_ge(c.sem('p1_v'), nqn)
        for b in range(8):
            g_idx = c.aux.get('p1_tt', 0); c.aux['p1_tt'] = g_idx + 1
            bank = g_idx % 2
            pv = pC[:, bank * 512:bank * 512 + 128].bitcast(BF16)[:, 0:128]
            if g_idx >= 2:
                c.t.wait_ge(c.sem('p1_ev'), c.n('p1_ev') - 1)
            r = c.t.transpose(out=pv, in_=KB[:, b * 128:(b + 1) * 128], identity=idt[:])
            nt = c.tick(r, 'p1_t')
            c.v.wait_ge(c.sem('p1_t'), nt)
            r = c.v.tensor_copy(out=afm[:, b * 128:(b + 1) * 128], in_=pv)
            c.tick(r, 'p1_ev')
        c.t.wait_ge(c.sem('p1_ev'), c.n('p1_ev'))
        for h in range(NH):
            par = h % 2
            lhs = kvb[par * 64:par * 64 + 64,
                      par * 512 + (h // 2) * 64: par * 512 + (h // 2) * 64 + 64]
            rhs = afm[(h % 2) * 64:(h % 2) * 64 + 64, (h // 2) * 128:(h // 2) * 128 + 128]
            r = c.t.matmul(out=pA[(h % 2) * 64:(h % 2) * 64 + 64,
                                  1024 + (h // 2) * 128:1024 + (h // 2) * 128 + 128],
                           lhsT=lhs, rhs=rhs, start=True, stop=True,
                           tile_position=((h % 2) * 64, (h % 2) * 64))
            natt = c.tick(r, 'p1_t')
        c.v.wait_ge(c.sem('p1_t'), natt)
        r = c.v.tensor_copy(out=afm, in_=pA[:, 1024:2048])
        nev = c.tick(r, 'p1_v')
        c.t.wait_ge(c.sem('p1_v'), nev)
        for half in range(2):
            for a in range(8):
                r = c.t.matmul(out=pB[:, half * 512:(half + 1) * 512],
                               lhsT=afm[:, a * 128:(a + 1) * 128],
                               rhs=w_sb[:, 8192 + a * 1024 + half * 512: 8192 + a * 1024 + half * 512 + 512],
                               start=(a == 0), stop=(a == 7))
                no = c.tick(r, 'p1_t')
        c.v.wait_ge(c.sem('p1_t'), no)
        c.v.wait_ge(c.sem('p1_a'), c.n('p1_a'))
        r = c.v.tensor_tensor(out=f1[:, 0:1024], in0=f2[:, 0:1024], in1=pB[:, 0:1024],
                              op=OP.add)
        nx1 = c.tick(r, 'p1_v')
        c.g.wait_ge(c.sem('p1_v'), nx1)
        r = c.g.dma_start(out=x1.ap().rearrange('(t p) d -> t p d', p=128)[i],
                          in_=f1[:, 0:1024])
        c.tick(r, 'p1_g', 16)
        c.v.wait_ge(c.sem('p1_g'), c.n('p1_g'))
    c.g.wait_ge(c.sem('p1_g'), c.n('p1_g'))



def _row_to_scal(c, pre, vals_and_dsts):
    pass


def _coll_max(c, pre, src_row_ap, n_items, dst_row_ap):
    cc_in = c.dram(pre + '_ccin', [1, 8], F32)
    cc_out = c.dram(pre + '_ccout', [1, 8], F32)
    r = c.g.dma_start(out=cc_in.ap(), in_=src_row_ap)
    c.tick(r, pre + '_g', 16)
    c.g.wait_ge(c.sem(pre + '_g'), c.n(pre + '_g'))
    r = c.g.collective_compute('AllReduce', OP.max,
                               replica_groups=[list(range(c.n_cores))],
                               ins=[cc_in.ap()], outs=[cc_out.ap()])
    c.tick(r, pre + '_cc')
    c._seminfo[pre + '_cc']['dma'] = True
    c.g.wait_ge(c.sem(pre + '_cc'), 1)
    r = c.g.dma_start(out=dst_row_ap, in_=cc_out.ap())
    return c.tick(r, pre + '_g', 16)


def _xpart_max(c, pre, amax_col_ap, dst_row_ap):
    idf = c.tile('p0_idf', [128, 128], F32)
    pB = c.psum('pB', [128, 1024], F32)
    if not c.aux.get(pre + '_idf'):
        c.aux[pre + '_idf'] = True
        r = c.sy.dma_start(out=idf[:], in_=ident_f32(c).ap()); c.tick(r, pre + '_ld', 16)
    c.t.wait_ge(c.sem(pre + '_ld'), c.n(pre + '_ld'))
    c.t.wait_ge(c.sem(pre + '_v'), c.n(pre + '_v'))
    r = c.t.matmul(out=pB[0:1, 0:128], lhsT=amax_col_ap, rhs=idf[:], start=True, stop=True)
    nt = c.tick(r, pre + '_t')
    c.v.wait_ge(c.sem(pre + '_t'), nt)
    r = c.v.reduce_max(out=dst_row_ap, in_=pB[0:1, 0:128], axis=AX.X)
    return c.tick(r, pre + '_v')


def _bcast_cols(c, pre, src_row_ap, n, dst_tile_ap, psum_off=512):
    onrt = c.tile('w0_onr', [1, 128], F32)
    pB = c.psum('pB', [128, 1024], F32)
    c.t.wait_ge(c.sem(pre + '_v'), c.n(pre + '_v'))
    r = c.t.matmul(out=pB[0:128, psum_off:psum_off + n], lhsT=onrt[:], rhs=src_row_ap,
                   start=True, stop=True)
    nt = c.tick(r, pre + '_t')
    c.v.wait_ge(c.sem(pre + '_t'), nt)
    r = c.v.tensor_copy(out=dst_tile_ap, in_=pB[0:128, psum_off:psum_off + n])
    return c.tick(r, pre + '_v')


def phase_p2(c):
    seq = c.seq
    ncc = seq // 512
    xq_fm = c.dram('xq_fm', [D, seq], BF16)
    scalA = c.dram('scalA', [1, 8], F32)
    w_sb = c.tile('B0', [128, 32768], BF16)
    xqs = c.tile('B2', [128, 4096], BF16)
    am = c.tile('p2_am', [128, 2], F32)
    rc = c.tile('p2_rc', [128, 1], F32)
    row = c.tile('p2_row', [1, 8], F32)
    pA = c.psum('pA', [128, 2048], F32)

    r = c.v.memset(am[:], 0.0); c.tick(r, 'p2_v')
    r = c.v.memset(row[:], 0.0); c.tick(r, 'p2_v')
    nred = {}
    for wi, wname in enumerate(['wT_up', 'wT_gate']):
        wd = c.dram(wname, [D, FF], BF16)
        c.sy.wait_ge(c.sem('p2_t'), c.n('p2_t'))
        r = c.sy.dma_start(out=w_sb[:, 0:8 * FF].rearrange('p (a o) -> p a o', a=8),
                           in_=wd.ap().rearrange('(a p) o -> p a o', p=128))
        nw = c.tick(r, 'p2_w', 16)
        for cc in range(ncc):
            c.sy.wait_ge(c.sem('p2_t'), c.n('p2_t'))
            r = c.sy.dma_start(out=xqs[:].rearrange('p (a s) -> p a s', a=8),
                               in_=xq_fm.ap().rearrange('(a p) s -> p a s', p=128)[:, :, cc * 512:(cc + 1) * 512])
            nld = c.tick(r, 'p2_ld', 16)
            c.t.wait_ge(c.sem('p2_ld'), nld)
            c.t.wait_ge(c.sem('p2_w'), nw)
            for t in range(FF // 128):
                blk = (wi * ncc + cc) * (FF // 128) + t
                bank = blk % 2
                if blk >= 2:
                    c.t.wait_ge(c.sem('p2_v'), nred[blk - 2])
                for a in range(8):
                    r = c.t.matmul(out=pA[:, bank * 512:(bank + 1) * 512],
                                   lhsT=w_sb[:, a * FF + t * 128: a * FF + t * 128 + 128],
                                   rhs=xqs[:, a * 512:(a + 1) * 512],
                                   start=(a == 0), stop=(a == 7))
                    nmm = c.tick(r, 'p2_t')
                c.v.wait_ge(c.sem('p2_t'), nmm)
                c.v.wait_ge(c.sem('p2_v'), c.n('p2_v'))
                r = c.v.reduce_max(out=rc[:], in_=pA[:, bank * 512:(bank + 1) * 512],
                                   axis=AX.X, apply_absolute_value=True)
                nv = c.tick(r, 'p2_v')
                c.v.wait_ge(c.sem('p2_v'), nv)
                r = c.v.tensor_tensor(out=am[:, wi:wi + 1], in0=am[:, wi:wi + 1],
                                      in1=rc[:], op=OP.max)
                nred[blk] = c.tick(r, 'p2_v')
    c.t.wait_ge(c.sem('p2_v'), c.n('p2_v'))
    _xpart_max(c, 'p2x', am[:, 0:1], row[0:1, 0:1])
    _xpart_max(c, 'p2x', am[:, 1:2], row[0:1, 1:2])
    nv = c.n('p2x_v')
    c.g.wait_ge(c.sem('p2x_v'), nv)
    ng = _coll_max(c, 'p2c', row[:], 2, row[:])
    c.g.wait_ge(c.sem('p2c_g'), ng)
    r = c.g.dma_start(out=scalA.ap(), in_=row[:])
    c.tick(r, 'p2c_g', 16)
    c.g.wait_ge(c.sem('p2c_g'), c.n('p2c_g'))


def _ffn_scales(c, pre, upto):
    sc = c.tile(pre + '_sc', [1, 16], F32)
    qm = c.tile('p0_qm', [1, 1], F32)
    rqm = c.tile('p0_rqm', [1, 1], F32)
    out = {}
    scalA = c.dram('scalA', [1, 8], F32)
    r = c.sy.dma_start(out=sc[0:1, 0:8], in_=scalA.ap()); c.tick(r, pre + '_sld', 16)
    qmc = c.const('qmax_32767', np.full((1, 1), 32767.0, np.float32))
    r = c.sy.dma_start(out=qm[:], in_=qmc.ap()); c.tick(r, pre + '_sld', 16)
    onrt = c.tile('w0_onr', [1, 128], F32)
    r = c.sy.dma_start(out=onrt[:], in_=ones_row(c).ap()); c.tick(r, pre + '_sld', 16)
    if upto >= 2:
        scalB = c.dram('scalB', [1, 8], F32)
        r = c.sy.dma_start(out=sc[0:1, 8:10], in_=scalB.ap()[0:1, 0:2]); c.tick(r, pre + '_sld', 16)
    if upto >= 3:
        scalC = c.dram('scalC', [1, 8], F32)
        r = c.sy.dma_start(out=sc[0:1, 10:12], in_=scalC.ap()[0:1, 0:2]); c.tick(r, pre + '_sld', 16)
    c.v.wait_ge(c.sem(pre + '_sld'), c.n(pre + '_sld'))
    r = c.v.reciprocal(out=rqm[:], in_=qm[:]); nv = c.tick(r, pre + '_v')
    c.v.wait_ge(c.sem(pre + '_v'), nv)
    W_ = sc
    def vop(fn):
        nonlocal nv
        r = fn()
        nv = c.tick(r, pre + '_v')
        c.v.wait_ge(c.sem(pre + '_v'), nv)
    vop(lambda: c.v.tensor_scalar(out=W_[0:1, 12:13], in0=W_[0:1, 0:1], scalar1=rqm[0:1, 0:1],
                                  scalar2=1e-12, op0=OP.mult, op1=OP.max))
    vop(lambda: c.v.reciprocal(out=W_[0:1, 13:14], in_=W_[0:1, 12:13]))
    vop(lambda: c.v.tensor_scalar(out=W_[0:1, 14:15], in0=W_[0:1, 1:2], scalar1=rqm[0:1, 0:1],
                                  scalar2=1e-12, op0=OP.mult, op1=OP.max))
    vop(lambda: c.v.reciprocal(out=W_[0:1, 15:16], in_=W_[0:1, 14:15]))
    tiles = {}
    def bc(name, src_ap):
        t = c.tile(pre + '_' + name, [128, 1], F32)
        c.t.wait_ge(c.sem(pre + '_v'), c.n(pre + '_v'))
        _bcast_cols(c, pre + 'b', src_ap, 1, t[:])
        tiles[name] = t
    if upto == 1:
        bc('rs2b', W_[0:1, 13:14])
        bc('rs3b', W_[0:1, 15:16])
        bc('s3b', W_[0:1, 14:15])
        return tiles
    vop(lambda: c.v.tensor_scalar(out=W_[0:1, 2:3], in0=W_[0:1, 14:15], scalar1=1.0 / 65536.0,
                                  scalar2=None, op0=OP.mult))
    vop(lambda: c.v.tensor_tensor(out=W_[0:1, 3:4], in0=W_[0:1, 8:9], in1=W_[0:1, 2:3], op=OP.mult))
    vop(lambda: c.v.tensor_scalar(out=W_[0:1, 4:5], in0=W_[0:1, 3:4], scalar1=rqm[0:1, 0:1],
                                  scalar2=1e-12, op0=OP.mult, op1=OP.max))
    vop(lambda: c.v.reciprocal(out=W_[0:1, 5:6], in_=W_[0:1, 4:5]))
    vop(lambda: c.v.tensor_tensor(out=W_[0:1, 6:7], in0=W_[0:1, 2:3], in1=W_[0:1, 5:6], op=OP.mult))
    if upto == 2:
        bc('comb4b', W_[0:1, 6:7])
        return tiles
    vop(lambda: c.v.tensor_tensor(out=W_[0:1, 7:8], in0=W_[0:1, 4:5], in1=W_[0:1, 12:13], op=OP.mult))
    vop(lambda: c.v.tensor_tensor(out=W_[0:1, 3:4], in0=W_[0:1, 10:11], in1=W_[0:1, 7:8], op=OP.mult))
    vop(lambda: c.v.tensor_scalar(out=W_[0:1, 9:10], in0=W_[0:1, 3:4], scalar1=rqm[0:1, 0:1],
                                  scalar2=1e-12, op0=OP.mult, op1=OP.max))
    vop(lambda: c.v.reciprocal(out=W_[0:1, 11:12], in_=W_[0:1, 9:10]))
    vop(lambda: c.v.tensor_tensor(out=W_[0:1, 2:3], in0=W_[0:1, 7:8], in1=W_[0:1, 11:12], op=OP.mult))
    bc('comb5b', W_[0:1, 2:3])
    bc('s5b', W_[0:1, 9:10])
    return tiles


def _qdq_int_chain(c, pre, src_ap, rs_ap, f_t, f_i32, f_c, f_m, out_ap, width,
                   clip_lo, clip_hi, u_has_half=False):
    vs, as_, ps = pre + '_v', pre + '_a', pre + '_p'
    if not u_has_half:
        r = c.v.tensor_scalar(out=f_t, in0=src_ap, scalar1=rs_ap, scalar2=0.5,
                              op0=OP.mult, op1=OP.add)
        nv = c.tick(r, vs)
        u_ap = f_t
    else:
        u_ap = src_ap
        nv = c.n(vs)
    c.s.wait_ge(c.sem(vs), nv)
    r = c.s.activation(out=f_i32, in_=u_ap, func=AF.Identity)
    na = c.tick(r, as_)
    c.g.wait_ge(c.sem(as_), na)
    r = c.g.tensor_copy(out=f_c, in_=f_i32)
    npl = c.tick(r, ps)
    c.v.wait_ge(c.sem(ps), npl)
    r = c.v.tensor_tensor(out=f_m, in0=f_c, in1=u_ap, op=OP.is_gt)
    nv = c.tick(r, vs)
    c.v.wait_ge(c.sem(vs), nv)
    r = c.v.scalar_tensor_tensor(out=f_m, in0=f_m, scalar=-1.0, in1=f_c,
                                 op0=OP.mult, op1=OP.add)
    nv = c.tick(r, vs)
    c.v.wait_ge(c.sem(vs), nv)
    r = c.v.tensor_scalar(out=out_ap, in0=f_m, scalar1=float(clip_lo), scalar2=float(clip_hi),
                          op0=OP.max, op1=OP.min)
    return c.tick(r, vs)


def phase_p3(c):
    seq = c.seq
    ncc = seq // 512
    NT = FF // 128
    xq_fm = c.dram('xq_fm', [D, seq], BF16)
    G1Q = c.dram('G1Q', [FF, seq], I16)
    Q2 = c.dram('Q2', [FF, seq], U16)
    UP1Q = c.dram('UP1Q', [FF, seq], I16)
    scalB = c.dram('scalB', [1, 8], F32)
    wu_d = c.dram('wT_up', [D, FF], BF16)
    wg_d = c.dram('wT_gate', [D, FF], BF16)
    wup = c.tile('B0', [128, 32768], BF16)
    wgt = tile_b1(c)
    xqs = c.tile('B2', [128, 4096], BF16)
    sc0 = c.base_tile('SC0')
    ut = (sc0[:, 0:1024], sc0[:, 1024:2048])
    mt = (sc0[:, 2048:3072], sc0[:, 3072:4096])
    g1 = (sc0[:, 4096:5120], sc0[:, 5120:6144])
    q2s = (sc0[:, 6144:7168], sc0[:, 7168:8192])
    sc1 = c.base_tile('SC1')
    ct_ = (sc1[:, 0:1024], sc1[:, 1024:2048])
    ia = (sc1[:, 2048:3072].bitcast(I32), sc1[:, 3072:4096].bitcast(I32))
    sg = (sc1[:, 4096:5120], sc1[:, 5120:6144])
    rc2 = c.tile('p4_rc', [128, 2], F32)
    am = c.tile('p3_am', [128, 1], F32)
    row = c.tile('p2_row', [1, 8], F32)
    pA = c.psum('pA', [128, 2048], F32)
    pB = c.psum('pB', [128, 1024], F32)
    pC = c.psum('pC', [128, 1024], F32)
    sc = _ffn_scales(c, 'p3s', 1)
    c.t.wait_ge(c.sem('p3sb_v'), c.n('p3sb_v'))
    r = c.v.memset(am[:], 0.0); c.tick(r, 'p3_v')

    r = c.sy.dma_start(out=wup[:, 0:8 * FF].rearrange('p (a o) -> p a o', a=8),
                       in_=wu_d.ap().rearrange('(a p) o -> p a o', p=128))
    c.tick(r, 'p3_w', 16)
    r = c.sy.dma_start(out=wgt[:, 0:8 * FF].rearrange('p (a o) -> p a o', a=8),
                       in_=wg_d.ap().rearrange('(a p) o -> p a o', p=128))
    nw = c.tick(r, 'p3_w', 16)

    G1Q_r = G1Q.ap().rearrange('(T p) s -> p T s', p=128)
    Q2_r = Q2.ap().rearrange('(T p) s -> p T s', p=128)
    UP1Q_r = UP1Q.ap().rearrange('(T p) s -> p T s', p=128)

    H = dict(ngmm={}, numm={}, ncastA={}, nuA={}, ncA={}, nsttA={}, nsig={},
             ncastB={}, nuB={}, ncB={}, nmin={}, ncastC={}, nuC={}, ncC={},
             nsttC={}, nam={}, nsttB={})

    def emit_pair(cc, k0):
        bb = (cc * (NT // 2) + k0, cc * (NT // 2) + k0 + 1)
        cs = slice(cc * 512, (cc + 1) * 512)
        for b in bb:
            s = b % 2
            kk = (b % (NT // 2)) * 2
            if b >= 2:
                c.t.wait_ge(c.sem('p3_a'), H['nuA'][b - 2])
            for tq in range(2):
                for a in range(8):
                    r = c.t.matmul(out=pA[:, s * 1024 + tq * 512: s * 1024 + tq * 512 + 512],
                                   lhsT=wgt[:, a * FF + (kk + tq) * 128: a * FF + (kk + tq) * 128 + 128],
                                   rhs=xqs[:, a * 512:(a + 1) * 512],
                                   start=(a == 0), stop=(a == 7))
                    H['ngmm'][b] = c.tick(r, 'p3_t')
        for b in bb:
            s = b % 2
            kk = (b % (NT // 2)) * 2
            po = pB if s == 0 else pC
            if b >= 2:
                c.t.wait_ge(c.sem('p3_a'), H['nuC'][b - 2])
            for tq in range(2):
                for a in range(8):
                    r = c.t.matmul(out=po[:, tq * 512: tq * 512 + 512],
                                   lhsT=wup[:, a * FF + (kk + tq) * 128: a * FF + (kk + tq) * 128 + 128],
                                   rhs=xqs[:, a * 512:(a + 1) * 512],
                                   start=(a == 0), stop=(a == 7))
                    H['numm'][b] = c.tick(r, 'p3_t')
        for b in bb:
            s = b % 2
            c.s.wait_ge(c.sem('p3_t'), H['ngmm'][b])
            if b >= 2:
                c.s.wait_ge(c.sem('p3_a'), H['ncC'][b - 2])
                c.s.wait_ge(c.sem('p3_v'), H['nsttC'][b - 2])
            r = c.s.activation(out=ia[s], in_=pA[:, s * 1024:s * 1024 + 1024],
                               func=AF.Identity, scale=sc['rs3b'][:, 0:1], bias=c.kbias(0.5))
            H['ncastA'][b] = c.tick(r, 'p3_a')
            r = c.s.activation(out=ut[s], in_=pA[:, s * 1024:s * 1024 + 1024],
                               func=AF.Identity, scale=sc['rs3b'][:, 0:1], bias=c.kbias(0.5))
            H['nuA'][b] = c.tick(r, 'p3_a')
        for b in bb:
            s = b % 2
            c.s.wait_ge(c.sem('p3_a'), H['ncastA'][b])
            if b >= 2:
                c.s.wait_ge(c.sem('p3_v'), H['nsttC'][b - 2])
            r = c.s.activation(out=ct_[s], in_=ia[s], func=AF.Identity)
            H['ncA'][b] = c.tick(r, 'p3_a')
        for b in bb:
            s = b % 2
            c.v.wait_ge(c.sem('p3_a'), H['ncA'][b])
            c.v.wait_ge(c.sem('p3_a'), H['nuA'][b])
            c.v.wait_ge(c.sem('p3_st'), c.n('p3_st'))
            if b >= 2:
                c.v.wait_ge(c.sem('p3_a'), H['nsig'][b - 2])
            r = c.v.tensor_tensor(out=mt[s], in0=ct_[s], in1=ut[s], op=OP.is_gt)
            nv = c.tick(r, 'p3_v')
            c.v.wait_ge(c.sem('p3_v'), nv)
            r = c.v.scalar_tensor_tensor(out=g1[s], in0=mt[s], scalar=-1.0, in1=ct_[s],
                                         op0=OP.mult, op1=OP.add)
            H['nsttA'][b] = c.tick(r, 'p3_v')
        for b in bb:
            s = b % 2
            kk = (b % (NT // 2)) * 2
            c.g.wait_ge(c.sem('p3_v'), H['nsttA'][b])
            r = c.g.dma_start(out=G1Q_r[:, kk:kk + 2, cs],
                              in_=g1[s].rearrange('p (T s) -> p T s', T=2))
            c.tick(r, 'p3_st', 16)
        for b in bb:
            s = b % 2
            c.s.wait_ge(c.sem('p3_v'), H['nsttA'][b])
            r = c.s.activation(out=sg[s], in_=g1[s], func=AF.Sigmoid, scale=sc['s3b'][:, 0:1])
            H['nsig'][b] = c.tick(r, 'p3_a')
        for b in bb:
            s = b % 2
            c.s.wait_ge(c.sem('p3_a'), H['nsig'][b])
            c.s.wait_ge(c.sem('p3_a'), H['ncA'][b])
            r = c.s.activation(out=ia[s], in_=sg[s], func=AF.Identity,
                               scale=65536.0, bias=c.kbias(0.5))
            H['ncastB'][b] = c.tick(r, 'p3_a')
            c.s.wait_ge(c.sem('p3_v'), H['nsttA'][b])
            r = c.s.activation(out=ut[s], in_=sg[s], func=AF.Identity,
                               scale=65536.0, bias=c.kbias(0.5))
            H['nuB'][b] = c.tick(r, 'p3_a')
        for b in bb:
            s = b % 2
            c.s.wait_ge(c.sem('p3_a'), H['ncastB'][b])
            c.s.wait_ge(c.sem('p3_v'), H['nsttA'][b])
            r = c.s.activation(out=ct_[s], in_=ia[s], func=AF.Identity)
            H['ncB'][b] = c.tick(r, 'p3_a')
        for b in bb:
            s = b % 2
            c.v.wait_ge(c.sem('p3_a'), H['ncB'][b])
            c.v.wait_ge(c.sem('p3_a'), H['nuB'][b])
            r = c.v.tensor_tensor(out=mt[s], in0=ct_[s], in1=ut[s], op=OP.is_gt)
            nv = c.tick(r, 'p3_v')
            c.v.wait_ge(c.sem('p3_v'), nv)
            r = c.v.scalar_tensor_tensor(out=q2s[s], in0=mt[s], scalar=-1.0, in1=ct_[s],
                                         op0=OP.mult, op1=OP.add)
            H['nsttB'][b] = nv = c.tick(r, 'p3_v')
            c.v.wait_ge(c.sem('p3_v'), nv)
            r = c.v.tensor_scalar(out=q2s[s], in0=q2s[s], scalar1=65535.0, scalar2=None,
                                  op0=OP.min)
            H['nmin'][b] = c.tick(r, 'p3_v')
        for b in bb:
            s = b % 2
            kk = (b % (NT // 2)) * 2
            c.g.wait_ge(c.sem('p3_v'), H['nmin'][b])
            r = c.g.dma_start(out=Q2_r[:, kk:kk + 2, cs],
                              in_=q2s[s].rearrange('p (T s) -> p T s', T=2))
            c.tick(r, 'p3_st', 16)
        for b in bb:
            s = b % 2
            r = c.v.tensor_tensor(out=mt[s], in0=g1[s], in1=q2s[s], op=OP.mult)
            nv = c.tick(r, 'p3_v')
            c.v.wait_ge(c.sem('p3_v'), nv)
            r = c.v.reduce_max(out=rc2[:, s:s + 1], in_=mt[s], axis=AX.X,
                               apply_absolute_value=True)
            nv = c.tick(r, 'p3_v')
            c.v.wait_ge(c.sem('p3_v'), nv)
            r = c.v.tensor_tensor(out=am[:], in0=am[:], in1=rc2[:, s:s + 1], op=OP.max)
            H['nam'][b] = c.tick(r, 'p3_v')
        for b in bb:
            s = b % 2
            po = pB if s == 0 else pC
            c.s.wait_ge(c.sem('p3_t'), H['numm'][b])
            c.s.wait_ge(c.sem('p3_a'), H['ncB'][b])
            r = c.s.activation(out=ia[s], in_=po[:, 0:1024],
                               func=AF.Identity, scale=sc['rs2b'][:, 0:1], bias=c.kbias(0.5))
            H['ncastC'][b] = c.tick(r, 'p3_a')
            c.s.wait_ge(c.sem('p3_v'), H['nmin'][b])
            r = c.s.activation(out=ut[s], in_=po[:, 0:1024],
                               func=AF.Identity, scale=sc['rs2b'][:, 0:1], bias=c.kbias(0.5))
            H['nuC'][b] = c.tick(r, 'p3_a')
        for b in bb:
            s = b % 2
            c.s.wait_ge(c.sem('p3_a'), H['ncastC'][b])
            c.s.wait_ge(c.sem('p3_v'), H['nmin'][b])
            r = c.s.activation(out=ct_[s], in_=ia[s], func=AF.Identity)
            H['ncC'][b] = c.tick(r, 'p3_a')
        for b in bb:
            s = b % 2
            c.v.wait_ge(c.sem('p3_a'), H['ncC'][b])
            c.v.wait_ge(c.sem('p3_a'), H['nuC'][b])
            r = c.v.tensor_tensor(out=mt[s], in0=ct_[s], in1=ut[s], op=OP.is_gt)
            nv = c.tick(r, 'p3_v')
            c.v.wait_ge(c.sem('p3_v'), nv)
            r = c.v.scalar_tensor_tensor(out=mt[s], in0=mt[s], scalar=-1.0, in1=ct_[s],
                                         op0=OP.mult, op1=OP.add)
            H['nsttC'][b] = c.tick(r, 'p3_v')
        for b in bb:
            s = b % 2
            kk = (b % (NT // 2)) * 2
            c.g.wait_ge(c.sem('p3_v'), H['nsttC'][b])
            r = c.g.dma_start(out=UP1Q_r[:, kk:kk + 2, cs],
                              in_=mt[s].rearrange('p (T s) -> p T s', T=2))
            c.tick(r, 'p3_st', 16)

    for cc in range(ncc):
        c.sy.wait_ge(c.sem('p3_t'), c.n('p3_t'))
        r = c.sy.dma_start(out=xqs[:].rearrange('p (a s) -> p a s', a=8),
                           in_=xq_fm.ap().rearrange('(a p) s -> p a s', p=128)[:, :, cc * 512:(cc + 1) * 512])
        nld = c.tick(r, 'p3_ld', 16)
        c.t.wait_ge(c.sem('p3_ld'), nld)
        if cc == 0:
            c.t.wait_ge(c.sem('p3_w'), nw)
        for k in range(NT // 4):
            emit_pair(cc, 2 * k)
    c.t.wait_ge(c.sem('p3_v'), c.n('p3_v'))
    _xpart_max(c, 'p3x', am[:], row[0:1, 0:1])
    c.g.wait_ge(c.sem('p3_st'), c.n('p3_st'))
    c.g.wait_ge(c.sem('p3x_v'), c.n('p3x_v'))
    ng = _coll_max(c, 'p3c', row[:], 1, row[:])
    c.g.wait_ge(c.sem('p3c_g'), ng)
    r = c.g.dma_start(out=scalB.ap(), in_=row[:])
    c.tick(r, 'p3c_g', 16)
    c.g.wait_ge(c.sem('p3c_g'), c.n('p3c_g'))


def phase_p4a(c):
    seq = c.seq
    W2 = min(1024, seq)
    ncol = seq // W2
    NT = FF // 128
    G1Q = c.dram('G1Q', [FF, seq], I16)
    Q2 = c.dram('Q2', [FF, seq], U16)
    UP1Q = c.dram('UP1Q', [FF, seq], I16)
    G2Q = c.dram('G2Q', [FF, seq], I16)
    scalC = c.dram('scalC', [1, 8], F32)
    sc0 = c.base_tile('SC0')
    fg = (sc0[:, 0:W2], sc0[:, 1024:1024 + W2])
    fq = (sc0[:, 2048:2048 + W2], sc0[:, 3072:3072 + W2])
    fu = (sc0[:, 4096:4096 + W2], sc0[:, 5120:5120 + W2])
    ut = (sc0[:, 6144:6144 + W2], sc0[:, 7168:7168 + W2])
    sc1 = c.base_tile('SC1')
    ct_ = (sc1[:, 0:W2], sc1[:, 1024:1024 + W2])
    ia = (sc1[:, 2048:2048 + W2].bitcast(I32), sc1[:, 3072:3072 + W2].bitcast(I32))
    mt = (sc1[:, 4096:4096 + W2], sc1[:, 5120:5120 + W2])
    rc = c.tile('p4_rc', [128, 2], F32)
    am = c.tile('p3_am', [128, 1], F32)
    row = c.tile('p2_row', [1, 8], F32)
    sc = _ffn_scales(c, 'p4s', 2)
    r = c.v.memset(am[:], 0.0); c.tick(r, 'p4_v')

    H = dict(nld={}, nm4={}, ncast={}, nu={}, nc={}, nstt={}, nm5={}, nrm={})

    def emit_pair(b0):
        for b in (b0, b0 + 1):
            s = b % 2
            t = b // ncol
            cs = slice((b % ncol) * W2, (b % ncol + 1) * W2)
            if b >= 2:
                c.g.wait_ge(c.sem('p4_v'), H['nrm'][b - 2])
                c.g.wait_ge(c.sem('p4_st'), c.n('p4_st'))
            r = c.g.dma_start(out=fg[s], in_=G1Q.ap()[t * 128:(t + 1) * 128, cs])
            c.tick(r, 'p4_l%d' % s, 16)
            r = c.g.dma_start(out=fq[s], in_=Q2.ap()[t * 128:(t + 1) * 128, cs])
            c.tick(r, 'p4_l%d' % s, 16)
            r = c.g.dma_start(out=fu[s], in_=UP1Q.ap()[t * 128:(t + 1) * 128, cs])
            H['nld'][b] = c.tick(r, 'p4_l%d' % s, 16)
        for b in (b0, b0 + 1):
            s = b % 2
            c.v.wait_ge(c.sem('p4_l%d' % s), H['nld'][b])
            r = c.v.tensor_tensor(out=fg[s], in0=fg[s], in1=fq[s], op=OP.mult)
            H['nm4'][b] = c.tick(r, 'p4_v')
        for b in (b0, b0 + 1):
            s = b % 2
            c.s.wait_ge(c.sem('p4_v'), H['nm4'][b])
            if b >= 2:
                pass
                c.s.wait_ge(c.sem('p4_v'), H['nstt'][b - 2])
            r = c.s.activation(out=ia[s], in_=fg[s], func=AF.Identity,
                               scale=sc['comb4b'][:, 0:1], bias=c.kbias(0.5))
            H['ncast'][b] = c.tick(r, 'p4_a')
            r = c.s.activation(out=ut[s], in_=fg[s], func=AF.Identity,
                               scale=sc['comb4b'][:, 0:1], bias=c.kbias(0.5))
            H['nu'][b] = c.tick(r, 'p4_a')
        for b in (b0, b0 + 1):
            s = b % 2
            c.s.wait_ge(c.sem('p4_a'), H['nu'][b])
            if b >= 2:
                c.s.wait_ge(c.sem('p4_v'), H['nstt'][b - 2])
            r = c.s.activation(out=ct_[s], in_=ia[s], func=AF.Identity)
            H['nc'][b] = c.tick(r, 'p4_a')
        for b in (b0, b0 + 1):
            s = b % 2
            t = b // ncol
            cs = slice((b % ncol) * W2, (b % ncol + 1) * W2)
            c.v.wait_ge(c.sem('p4_a'), H['nc'][b])
            c.v.wait_ge(c.sem('p4_a'), H['nu'][b])
            c.v.wait_ge(c.sem('p4_st'), c.n('p4_st'))
            r = c.v.tensor_tensor(out=mt[s], in0=ct_[s], in1=ut[s], op=OP.is_gt)
            nv = c.tick(r, 'p4_v')
            c.v.wait_ge(c.sem('p4_v'), nv)
            r = c.v.scalar_tensor_tensor(out=fq[s], in0=mt[s], scalar=-1.0, in1=ct_[s],
                                         op0=OP.mult, op1=OP.add)
            H['nstt'][b] = nv = c.tick(r, 'p4_v')
            c.g.wait_ge(c.sem('p4_v'), nv)
            r = c.g.dma_start(out=G2Q.ap()[t * 128:(t + 1) * 128, cs], in_=fq[s])
            c.tick(r, 'p4_st', 16)
            c.v.wait_ge(c.sem('p4_v'), nv)
            r = c.v.tensor_tensor(out=fu[s], in0=fq[s], in1=fu[s], op=OP.mult)
            H['nm5'][b] = nv = c.tick(r, 'p4_v')
            c.v.wait_ge(c.sem('p4_v'), nv)
            r = c.v.reduce_max(out=rc[:, s:s + 1], in_=fu[s], axis=AX.X,
                               apply_absolute_value=True)
            nv = c.tick(r, 'p4_v')
            c.v.wait_ge(c.sem('p4_v'), nv)
            r = c.v.tensor_tensor(out=am[:], in0=am[:], in1=rc[:, s:s + 1], op=OP.max)
            H['nrm'][b] = c.tick(r, 'p4_v')

    nblk = NT * ncol
    for k in range(nblk // 2):
        emit_pair(2 * k)
    c.t.wait_ge(c.sem('p4_v'), c.n('p4_v'))
    _xpart_max(c, 'p4x', am[:], row[0:1, 0:1])
    c.g.wait_ge(c.sem('p4_st'), c.n('p4_st'))
    c.g.wait_ge(c.sem('p4x_v'), c.n('p4x_v'))
    ng = _coll_max(c, 'p4c', row[:], 1, row[:])
    c.g.wait_ge(c.sem('p4c_g'), ng)
    r = c.g.dma_start(out=scalC.ap(), in_=row[:])
    c.tick(r, 'p4c_g', 16)
    c.g.wait_ge(c.sem('p4c_g'), c.n('p4c_g'))


def phase_p4b(c):
    seq = c.seq
    ncc = seq // 512
    NT = FF // 128
    G2Q = c.dram('G2Q', [FF, seq], I16)
    UP1Q = c.dram('UP1Q', [FF, seq], I16)
    x1 = c.dram('x1', [seq, D], F32)
    x2 = c.dram('x2', [seq, D], F32)
    wdn = c.dram('wT_down', [FF, D], BF16)
    oin = c.tile('B0', [128, 32768], BF16)
    wdt = tile_b1(c)
    sc0 = c.base_tile('SC0')
    fg = (sc0[:, 0:1024], sc0[:, 1024:2048])
    fu = (sc0[:, 2048:3072], sc0[:, 3072:4096])
    ut = (sc0[:, 4096:5120], sc0[:, 5120:6144])
    q5 = (sc0[:, 6144:7168], sc0[:, 7168:8192])
    sc1 = c.base_tile('SC1')
    ct_ = (sc1[:, 0:1024], sc1[:, 1024:2048])
    ia = (sc1[:, 2048:3072].bitcast(I32), sc1[:, 3072:4096].bitcast(I32))
    mt = (sc1[:, 4096:5120], sc1[:, 5120:6144])
    b2f = c.tile('B2', [128, 4096], BF16)
    xres = (b2f[:, 0:2048].bitcast(F32), b2f[:, 2048:4096].bitcast(F32))
    pA = c.psum('pA', [128, 2048], F32)
    sc = _ffn_scales(c, 'p4t', 3)
    r = c.sy.dma_start(out=wdt[:, 0:24 * 1024].rearrange('p (a o) -> p a o', a=24),
                       in_=wdn.ap().rearrange('(a p) o -> p a o', p=128))
    nwd = c.tick(r, 'p4b_w', 16)
    G2Q_r = G2Q.ap().rearrange('(T p) s -> p T s', p=128)
    UP1Q_r = UP1Q.ap().rearrange('(T p) s -> p T s', p=128)

    H = dict(nld={}, nm={}, ncast={}, nu={}, nc={}, nstt={}, nsc={})
    nmm_cc = {}
    nx_cc = {}

    def chain_pair(cc, k0):
        bb = (cc * (NT // 2) + k0, cc * (NT // 2) + k0 + 1)
        cs = slice(cc * 512, (cc + 1) * 512)
        for b in bb:
            s = b % 2
            kk = (b % (NT // 2)) * 2
            if b >= 2:
                c.g.wait_ge(c.sem('p4b_v'), H['nm'][b - 2])
                c.g.wait_ge(c.sem('p4b_v'), H['nsc'][b - 2])
            r = c.g.dma_start(out=fg[s].rearrange('p (T s) -> p T s', T=2),
                              in_=G2Q_r[:, kk:kk + 2, cs])
            c.tick(r, 'p4b_l%d' % s, 16)
            r = c.g.dma_start(out=fu[s].rearrange('p (T s) -> p T s', T=2),
                              in_=UP1Q_r[:, kk:kk + 2, cs])
            H['nld'][b] = c.tick(r, 'p4b_l%d' % s, 16)
        for b in bb:
            s = b % 2
            c.v.wait_ge(c.sem('p4b_l%d' % s), H['nld'][b])
            r = c.v.tensor_tensor(out=fg[s], in0=fg[s], in1=fu[s], op=OP.mult)
            H['nm'][b] = c.tick(r, 'p4b_v')
        for b in bb:
            s = b % 2
            c.s.wait_ge(c.sem('p4b_v'), H['nm'][b])
            if b >= 2:
                pass
                c.s.wait_ge(c.sem('p4b_v'), H['nstt'][b - 2])
            r = c.s.activation(out=ia[s], in_=fg[s], func=AF.Identity,
                               scale=sc['comb5b'][:, 0:1], bias=c.kbias(0.5))
            H['ncast'][b] = c.tick(r, 'p4b_a')
            r = c.s.activation(out=ut[s], in_=fg[s], func=AF.Identity,
                               scale=sc['comb5b'][:, 0:1], bias=c.kbias(0.5))
            H['nu'][b] = c.tick(r, 'p4b_a')
        for b in bb:
            s = b % 2
            c.s.wait_ge(c.sem('p4b_a'), H['nu'][b])
            if b >= 2:
                c.s.wait_ge(c.sem('p4b_v'), H['nstt'][b - 2])
            r = c.s.activation(out=ct_[s], in_=ia[s], func=AF.Identity)
            H['nc'][b] = c.tick(r, 'p4b_a')
        for b in bb:
            s = b % 2
            kk = (b % (NT // 2)) * 2
            c.v.wait_ge(c.sem('p4b_a'), H['nc'][b])
            c.v.wait_ge(c.sem('p4b_a'), H['nu'][b])
            if b >= 2:
                c.v.wait_ge(c.sem('p4b_v'), H['nsc'][b - 2])
            r = c.v.tensor_tensor(out=mt[s], in0=ct_[s], in1=ut[s], op=OP.is_gt)
            nv = c.tick(r, 'p4b_v')
            c.v.wait_ge(c.sem('p4b_v'), nv)
            r = c.v.scalar_tensor_tensor(out=q5[s], in0=mt[s], scalar=-1.0, in1=ct_[s],
                                         op0=OP.mult, op1=OP.add)
            H['nstt'][b] = nv = c.tick(r, 'p4b_v')
            c.v.wait_ge(c.sem('p4b_v'), nv)
            if cc >= 1 and kk == 0:
                c.v.wait_ge(c.sem('p4b_t'), nmm_cc[cc - 1])
            r = c.v.tensor_scalar(out=oin[:, kk * 512:(kk + 2) * 512], in0=q5[s],
                                  scalar1=sc['s5b'][:, 0:1], scalar2=None, op0=OP.mult)
            H['nsc'][b] = c.tick(r, 'p4b_v')

    def gemm(cc):
        c.t.wait_ge(c.sem('p4b_w'), nwd)
        c.t.wait_ge(c.sem('p4b_v'), H['nsc'][cc * (NT // 2) + NT // 2 - 1])
        for sub in range(4):
            bank = sub % 2
            i_tile = cc * 4 + sub
            if i_tile >= 2:
                c.t.wait_ge(c.sem('p4b_x'), nx_cc[i_tile - 2])
            for a in range(24):
                r = c.t.matmul(out=pA[:, bank * 1024:bank * 1024 + 512],
                               lhsT=oin[:, a * 512 + sub * 128: a * 512 + sub * 128 + 128],
                               rhs=wdt[:, a * 1024: a * 1024 + 512],
                               start=(a == 0), stop=(a == 23))
                c.tick(r, 'p4b_t')
                r = c.t.matmul(out=pA[:, bank * 1024 + 512:bank * 1024 + 1024],
                               lhsT=oin[:, a * 512 + sub * 128: a * 512 + sub * 128 + 128],
                               rhs=wdt[:, a * 1024 + 512: a * 1024 + 1024],
                               start=(a == 0), stop=(a == 23))
                nmm = c.tick(r, 'p4b_t')
            par = i_tile % 2
            if i_tile >= 1:
                c.sy.wait_ge(c.sem('p4b_x'), c.n('p4b_x'))
            if i_tile >= 2:
                c.sy.wait_ge(c.sem('p4b_g%d' % par), 16 * (i_tile // 2))
            r = c.sy.dma_start(out=xres[par], in_=x1.ap().rearrange('(t p) d -> t p d', p=128)[i_tile])
            nldx = c.tick(r, 'p4b_lx', 16)
            c.v.wait_ge(c.sem('p4b_lx'), nldx)
            c.v.wait_ge(c.sem('p4b_t'), nmm)
            r = c.v.tensor_tensor(out=xres[par], in0=xres[par],
                                  in1=pA[:, bank * 1024:bank * 1024 + 1024], op=OP.add)
            nx_cc[i_tile] = nx = c.tick(r, 'p4b_x')
            c.g.wait_ge(c.sem('p4b_x'), nx)
            r = c.g.dma_start(out=x2.ap().rearrange('(t p) d -> t p d', p=128)[i_tile],
                              in_=xres[par])
            c.tick(r, 'p4b_g%d' % par, 16)
        nmm_cc[cc] = nmm

    for cc in range(ncc):
        for k in range(NT // 4):
            chain_pair(cc, 2 * k)
        gemm(cc)
    c.g.wait_ge(c.sem('p4b_g0'), c.n('p4b_g0'))
    c.g.wait_ge(c.sem('p4b_g1'), c.n('p4b_g1'))



def phase_p5(c):
    x2 = c.dram('x2', [c.seq, D], F32)
    xn_tok = c.dram('xn_tok_l1', [c.seq, D], F32)
    xn_fm = c.dram('xn_fm_l1', [D, c.seq], BF16)
    layernorm_tiles(c, 'p5', x2, xn_tok, xn_fm, amax_tile=None)



def phase_p7(c):
    seq = c.seq
    ntile = c.ntile
    x2 = c.dram('x2', [seq, D], F32)
    x3 = c.dram('x3', [seq, D], F32)
    xn_fm = c.dram('xn_fm_l1', [D, seq], BF16)
    wq_d = c.dram('wT_l1q', [D, D], BF16)
    wk_d = c.dram('wT_l1k', [D, D], BF16)
    wv_d = c.dram('wT_l1v', [D, D], BF16)
    wfc_d = c.dram('wT_l1fc', [D, D], BF16)

    wall = c.tile('B0', [128, 32768], BF16)
    b1 = tile_b1(c)
    Q4 = b1[:, 0:4096]
    K4 = b1[:, 4096:8192]
    abf = (b1[:, 12288:13312], b1[:, 13312:14336])
    afm = b1[:, 14336:15360]
    xqs = c.tile('B2', [128, 4096], BF16)
    sc0 = c.base_tile('SC0')
    Pt = sc0[:, 0:2048]
    OUT = sc0[:, 2048:3072]
    OUTB = sc0[:, 3072:4096]
    S = sc0[:, 6144:6400]
    MX = sc0[:, 6400:6416]
    SM = sc0[:, 6416:6432]
    RC = sc0[:, 6432:6448]
    xres = (sc0[:, 4096:5120], sc0[:, 5120:6144])
    sc1 = c.base_tile('SC1')
    V4 = sc1[:, 0:4096]
    TMP = sc1[:, 4096:5120]
    TMPP = sc1[:, 5120:6144]
    idb = c.tile('p7_idb', [128, 128], BF16)
    pA = c.psum('pA', [128, 2048], F32)
    pB = c.psum('pB', [128, 1024], F32)
    pC = c.psum('pC', [128, 1024], F32)
    NPV = 11

    for wi, wd in enumerate((wq_d, wk_d, wv_d, wfc_d)):
        r = c.sy.dma_start(out=wall[:, wi * 8192:(wi + 1) * 8192].rearrange('p (a o) -> p a o', a=8),
                           in_=wd.ap().rearrange('(a p) o -> p a o', p=128))
        nwl = c.tick(r, 'p7_wl', 16)
    r = c.sy.dma_start(out=idb[:], in_=ident_bf16(c).ap())
    nwl = c.tick(r, 'p7_wl', 16)

    def proj(i):
        g, j = divmod(i, 4)
        if j == 0:
            c.sy.wait_ge(c.sem('p7_t'), c.n('p7_t'))
            r = c.sy.dma_start(out=xqs[:].rearrange('p (a s) -> p a s', a=8),
                               in_=xn_fm.ap().rearrange('(a p) s -> p a s', p=128)[:, :, g * 512:(g + 1) * 512])
            nld = c.tick(r, 'p7_ld', 16)
            c.t.wait_ge(c.sem('p7_ld'), nld)
            if i == 0:
                c.t.wait_ge(c.sem('p7_wl'), nwl)
        outs = ((pA[:, 0:1024], 0, Q4), (pA[:, 1024:2048], 8192, K4), (pB[:, 0:1024], 16384, None))
        c.t.wait_ge(c.sem('p7_ev'), c.n('p7_ev'))
        for po, woff, dst4 in outs:
            for half in range(2):
                for a in range(8):
                    r = c.t.matmul(out=po[:, half * 512:(half + 1) * 512],
                                   lhsT=xqs[:, a * 512 + j * 128: a * 512 + j * 128 + 128],
                                   rhs=wall[:, woff + a * 1024 + half * 512: woff + a * 1024 + half * 512 + 512],
                                   start=(a == 0), stop=(a == 7))
                    nt = c.tick(r, 'p7_t')
            c.s.wait_ge(c.sem('p7_t'), nt)
            if dst4 is None:
                r = c.s.activation(out=V4[:, j * 1024:(j + 1) * 1024], in_=po, func=AF.Identity)
            else:
                r = c.s.activation(out=dst4[:, j * 1024:(j + 1) * 1024], in_=po, func=AF.Identity)
            c.tick(r, 'p7_ev')

    def compute(i):
        j = i % 4
        q3 = Q4[:, j * 1024:(j + 1) * 1024].rearrange('p (h d) -> p h d', d=HD)
        S3 = S.rearrange('p (h t) -> p h t', t=NH)
        P3 = Pt.rearrange('p (h d) -> p h d', d=HD)
        O3 = OUT.rearrange('p (h d) -> p h d', d=HD)
        OB3 = OUTB.rearrange('p (h d) -> p h d', d=HD)
        nev = c.n('p7_ev')
        c.v.wait_ge(c.sem('p7_ev'), nev)
        c.g.wait_ge(c.sem('p7_ev'), nev)
        q4 = q3.unsqueeze(2).broadcast_to([128, NH, 2, HD])
        P4 = Pt.rearrange('p (h t d) -> p h t d', t=2, d=HD)
        for g2 in range(NH // 2):
            k4 = K4[:, j * 1024 + g2 * 2 * HD: j * 1024 + (g2 + 1) * 2 * HD] \
                .rearrange('p (t d) -> p t d', d=HD).unsqueeze(1).broadcast_to([128, NH, 2, HD])
            r = c.v.tensor_tensor(out=P4, in0=q4, in1=k4, op=OP.mult)
            nv = c.tick(r, 'p7_v')
            c.v.wait_ge(c.sem('p7_v'), nv)
            r = c.v.reduce_sum(out=S3[:, :, g2 * 2:(g2 + 1) * 2].unsqueeze(3), in_=P4, axis=AX.X)
            nv = c.tick(r, 'p7_v')
            c.v.wait_ge(c.sem('p7_v'), nv)
        r = c.v.reduce_max(out=MX.unsqueeze(2), in_=S3, axis=AX.X)
        nv = c.tick(r, 'p7_v')
        c.v.wait_ge(c.sem('p7_v'), nv)
        r = c.v.tensor_tensor(out=S3, in0=S3, in1=MX.unsqueeze(2).broadcast_to([128, NH, NH]),
                              op=OP.subtract)
        nv = c.tick(r, 'p7_v')
        c.s.wait_ge(c.sem('p7_v'), nv)
        r = c.s.activation(out=S, in_=S, func=AF.Exp, scale=0.125)
        ne = c.tick(r, 'p7_e')
        c.v.wait_ge(c.sem('p7_e'), ne)
        r = c.v.reduce_sum(out=SM.unsqueeze(2), in_=S3, axis=AX.X)
        nv = c.tick(r, 'p7_v')
        c.v.wait_ge(c.sem('p7_v'), nv)
        r = c.v.reciprocal(out=RC, in_=SM)
        nv = c.tick(r, 'p7_v')
        c.v.wait_ge(c.sem('p7_v'), nv)
        nexp = nv
        for t in range(NPV):
            e3 = S3[:, :, t:t + 1].broadcast_to([128, NH, HD])
            v3 = V4[:, j * 1024 + t * HD: j * 1024 + (t + 1) * HD].unsqueeze(1).broadcast_to([128, NH, HD])
            if t == 0:
                r = c.v.tensor_tensor(out=O3, in0=e3, in1=v3, op=OP.mult)
            else:
                r = c.v.tensor_tensor(out=TMP.rearrange('p (h d) -> p h d', d=HD), in0=e3, in1=v3, op=OP.mult)
            nv = c.tick(r, 'p7_v')
            c.v.wait_ge(c.sem('p7_v'), nv)
            if t > 0:
                r = c.v.tensor_tensor(out=OUT, in0=OUT, in1=TMP, op=OP.add)
                nv = c.tick(r, 'p7_v')
                c.v.wait_ge(c.sem('p7_v'), nv)
        c.g.wait_ge(c.sem('p7_v'), nexp)
        c.g.wait_ge(c.sem('p7_x'), c.n('p7_x'))
        for t in range(NPV, NH):
            e3 = S3[:, :, t:t + 1].broadcast_to([128, NH, HD])
            v3 = V4[:, j * 1024 + t * HD: j * 1024 + (t + 1) * HD].unsqueeze(1).broadcast_to([128, NH, HD])
            if t == NPV:
                r = c.g.tensor_tensor(out=OB3, in0=e3, in1=v3, op=OP.mult)
            else:
                r = c.g.tensor_tensor(out=TMPP.rearrange('p (h d) -> p h d', d=HD), in0=e3, in1=v3, op=OP.mult)
            np_ = c.tick(r, 'p7_p')
            c.g.wait_ge(c.sem('p7_p'), np_)
            if t > NPV:
                r = c.g.tensor_tensor(out=OUTB, in0=OUTB, in1=TMPP, op=OP.add)
                np_ = c.tick(r, 'p7_p')
                c.g.wait_ge(c.sem('p7_p'), np_)
        c.v.wait_ge(c.sem('p7_p'), c.n('p7_p'))
        r = c.v.tensor_tensor(out=O3, in0=O3, in1=OB3, op=OP.add)
        nx2 = c.tick(r, 'p7_x')
        c.v.wait_ge(c.sem('p7_x'), nx2)
        if i >= 2:
            c.v.wait_ge(c.sem('p7_tt'), 8 * (i - 1))
        a3 = abf[i % 2].rearrange('p (h d) -> p h d', d=HD)
        r = c.v.tensor_tensor(out=a3, in0=O3,
                              in1=RC.unsqueeze(2).broadcast_to([128, NH, HD]), op=OP.mult)
        ndv = c.tick(r, 'p7_v')
        c.t.wait_ge(c.sem('p7_v'), ndv)
        c.t.wait_ge(c.sem('p7_ev'), c.n('p7_ev'))
        for b in range(8):
            r = c.t.transpose(out=pB[:, b * 64:(b + 1) * 64].bitcast(BF16),
                              in_=abf[i % 2][:, b * 128:(b + 1) * 128], identity=idb[:])
            ntt = c.tick(r, 'p7_tt')
        c.v.wait_ge(c.sem('p7_tt'), ntt)
        c.v.wait_ge(c.sem('p7_t3'), c.n('p7_t3'))
        r = c.v.tensor_copy(out=afm, in_=pB[:, 0:512].bitcast(BF16))
        nea = c.tick(r, 'p7_ev')
        c.t.wait_ge(c.sem('p7_ev'), nea)
        if i >= 1:
            c.t.wait_ge(c.sem('p7_x'), c.n('p7_x'))
        for half in range(2):
            for a in range(8):
                r = c.t.matmul(out=pC[:, half * 512:(half + 1) * 512],
                               lhsT=afm[:, a * 128:(a + 1) * 128],
                               rhs=wall[:, 24576 + a * 1024 + half * 512: 24576 + a * 1024 + half * 512 + 512],
                               start=(a == 0), stop=(a == 7))
                nt3 = c.tick(r, 'p7_t3')
        if i >= 1:
            c.sy.wait_ge(c.sem('p7_x'), c.n('p7_x'))
        if i >= 2:
            c.sy.wait_ge(c.sem('p7_g%d' % (i % 2)), 16 * (i // 2))
        r = c.sy.dma_start(out=xres[i % 2], in_=x2.ap().rearrange('(t p) d -> t p d', p=128)[i])
        nlx = c.tick(r, 'p7_lx', 16)
        c.v.wait_ge(c.sem('p7_lx'), nlx)
        c.v.wait_ge(c.sem('p7_t3'), nt3)
        r = c.v.tensor_tensor(out=xres[i % 2], in0=xres[i % 2], in1=pC[:, 0:1024], op=OP.add)
        nx = c.tick(r, 'p7_x')
        c.g.wait_ge(c.sem('p7_x'), nx)
        r = c.g.dma_start(out=x3.ap().rearrange('(t p) d -> t p d', p=128)[i], in_=xres[i % 2])
        c.tick(r, 'p7_g%d' % (i % 2), 16)

    proj(0)
    for i in range(ntile):
        if i + 1 < ntile:
            proj(i + 1)
        compute(i)
    c.g.wait_ge(c.sem('p7_g0'), c.n('p7_g0'))
    c.g.wait_ge(c.sem('p7_g1'), c.n('p7_g1'))




def ones_row_bf(c):
    return c.const('ones_row_bf', np.ones((1, 128), dtype=ml_dtypes.bfloat16))


def phase_p8(c):
    seq = c.seq
    ntile = c.ntile
    x3 = c.dram('x3', [seq, D], F32)
    out_d = c.dram('out', [seq, D], F32)
    xn_fm = c.dram('xn_fm_l1', [D, seq], BF16)
    xn_tok = c.dram('xn_tok_l1', [seq, D], F32)
    wb_d = c.dram('L1_ffn_W_base', [D, D], F32)
    bb_d = c.dram('L1_ffn_b_base', [1, D], F32)
    wd1_d = c.dram('L1_ffn_Wd1', [NE, D, EH], F32)
    bd1_d = c.dram('L1_ffn_bd1', [NE, EH], F32)
    wd2_d = c.dram('L1_ffn_Wd2', [NE, EH, D], F32)
    bd2_d = c.dram('L1_ffn_bd2', [NE, D], F32)
    wad_d = c.dram('L1_ffn_Wa_down', [NE, D, RK], F32)
    bad_d = c.dram('L1_ffn_ba_down', [NE, RK], F32)
    wau_d = c.dram('L1_ffn_Wa_up', [NE, RK, D], F32)
    bau_d = c.dram('L1_ffn_ba_up', [NE, D], F32)

    b0 = c.tile('B0', [128, 32768], BF16)
    b1 = tile_b1(c)
    wbase = b0[:, 0:8192]
    wd1 = b0[:, 8192:24576]
    wad = b0[:, 24576:28672]
    bd1r = b0[0:1, 28672:30720]
    wd2 = b1[:, 0:16384]
    wau = b1[0:64, 16384:24576]
    bd2r = b1[0:1, 24576:32768]
    xfm = c.tile('B2', [128, 4096], BF16)[:, 0:1024]
    f1 = c.tile('F1', [128, 2048], F32)
    xtok2 = (f1[:, 0:1024], f1[:, 1024:2048])
    f0 = c.tile('F0', [128, 2048], F32)
    diff2 = (f0[:, 0:1024], f0[:, 1024:2048])
    sc1 = c.base_tile('SC1')
    sq_out = sc1[:, 0:1024]
    wad_acc = sc1[:, 1024:1536]
    wup_acc = sc1[0:64, 2048:3072]
    out2 = (sc1[:, 0:1024], sc1[:, 1024:2048])
    r1 = sc1[:, 4096:4608].bitcast(BF16)[0:1, 0:1024]
    wadb = sc1[:, 4608:4864].bitcast(BF16)[:, 0:512]
    wupb = sc1[:, 4864:5376].bitcast(BF16)[0:64, 0:1024]
    acc2 = sc1[:, 5376:5378]
    hB = c.tile('p8_h', [128, 512], BF16)
    hf = c.tile('p8_hf', [128, 512], BF16)
    aB = c.tile('p8_a', [128, 128], BF16)
    af = c.tile('p8_af', [64, 256], BF16)
    ls = c.tile('p8_ls', [128, 8], F32)
    lc = c.tile('p8_lc', [8, 1], F32)
    lr = c.tile('p8_lr', [1, 8], F32)
    mrow = c.tile('p8_m', [1, 1], F32)
    oh = c.tile('p8_oh', [1, 8], F32)
    ohb = c.tile('p8_ohb', [128, 8], F32)
    bad_acc = c.tile('p8_bad', [1, 64], F32)
    badb = c.tile('p8_badb', [1, 64], BF16)
    bau_acc = sc1[:, 3072:4096][0:1, 0:1024]
    baub = c.tile('B2', [128, 4096], BF16)[:, 1024:2048][0:1, 0:1024]
    bbb = c.tile('p8_bbb', [1, 1024], BF16)
    row0 = c.tile('B2', [128, 4096], BF16)[:, 2048:4096].bitcast(F32)[0:1, 0:1024]
    onb = c.tile('p8_onb', [1, 128], BF16)
    idt = c.tile('w0_id', [128, 128], BF16)
    idf = c.tile('p0_idf', [128, 128], F32)
    pA = c.psum('pA', [128, 2048], F32)
    pB = c.psum('pB', [128, 1024], F32)
    pC = c.psum('pC', [128, 1024], F32)

    r = c.g.dma_start(out=wbase.rearrange('p (a o) -> p a o', a=8),
                      in_=wb_d.ap().rearrange('(a p) o -> p a o', p=128))
    c.tick(r, 'p8_wg', 16)
    r = c.g.dma_start(out=wd1.rearrange('p (e a h) -> p e a h', e=NE, a=8),
                      in_=wd1_d.ap().rearrange('e (a p) h -> p e a h', p=128))
    c.tick(r, 'p8_wg', 16)
    r = c.g.dma_start(out=wad.rearrange('p (e a k) -> p e a k', e=NE, a=8),
                      in_=wad_d.ap().rearrange('e (a p) k -> p e a k', p=128))
    c.tick(r, 'p8_wg', 16)
    r = c.g.dma_start(out=wd2.rearrange('p (e hc o) -> p e hc o', e=NE, hc=2),
                      in_=wd2_d.ap().rearrange('e (hc p) o -> p e hc o', p=128))
    c.tick(r, 'p8_wg', 16)
    r = c.g.dma_start(out=wau.rearrange('p (e o) -> p e o', e=NE),
                      in_=wau_d.ap().rearrange('e k o -> k e o'))
    c.tick(r, 'p8_wg', 16)
    r = c.g.dma_start(out=bbb[:], in_=bb_d.ap())
    c.tick(r, 'p8_wg', 16)
    for e in range(NE):
        r = c.g.dma_start(out=bd1r[0:1, e * 256:(e + 1) * 256], in_=bd1_d.ap()[e:e + 1, :])
        c.tick(r, 'p8_wg', 16)
        r = c.g.dma_start(out=bd2r[0:1, e * 1024:(e + 1) * 1024], in_=bd2_d.ap()[e:e + 1, :])
        c.tick(r, 'p8_wg', 16)
    r = c.sy.dma_start(out=onb[:], in_=ones_row_bf(c).ap()); c.tick(r, 'p8_ld', 16)
    r = c.sy.dma_start(out=idt[:], in_=ident_bf16(c).ap()); c.tick(r, 'p8_ld', 16)
    r = c.sy.dma_start(out=idf[:], in_=ident_f32(c).ap()); c.tick(r, 'p8_ld', 16)
    onrt = c.tile('w0_onr', [1, 128], F32)
    r = c.sy.dma_start(out=onrt[:], in_=ones_row(c).ap()); c.tick(r, 'p8_ld', 16)
    r = c.v.memset(ls[:], 0.0); c.tick(r, 'p8_v')
    nwg = c.n('p8_wg')

    total = ntile * NE
    na_hist = {}
    nvd_hist = {}
    nld_ct = {}
    nlx_ct = {}
    nsq_hist = {}
    nwd1_ct = {}

    def stage_a(it):
        ct, e = divmod(it, NE)
        q = it % 2
        if e == 0:
            if ct >= 1:
                c.sy.wait_ge(c.sem('p8_t'), nwd1_ct[ct - 1])
            if ct >= 2:
                c.sy.wait_ge(c.sem('p8_v'), nvd_hist[(ct - 2) * NE + NE - 1])
            r = c.sy.dma_start(out=xfm.rearrange('p (a w) -> p a w', a=8),
                               in_=xn_fm.ap().rearrange('(a p) s -> p a s', p=128)[:, :, ct * 128:(ct + 1) * 128])
            nld_ct[ct] = c.tick(r, 'p8_ld', 16)
            r = c.sy.dma_start(out=xtok2[ct % 2], in_=xn_tok.ap().rearrange('(t p) d -> t p d', p=128)[ct])
            nlx_ct[ct] = c.tick(r, 'p8_lx%d' % (ct % 2), 16)
        off = 0 if q == 0 else 512
        c.t.wait_ge(c.sem('p8_ld'), nld_ct[ct])
        if it == 0:
            c.t.wait_ge(c.sem('p8_wg'), nwg)
        if it >= 2:
            c.t.wait_ge(c.sem('p8_a'), na_hist[it - 2])
        for a in range(8):
            r = c.t.matmul(out=pA[:, off:off + 256], lhsT=xfm[:, a * 128:(a + 1) * 128],
                           rhs=wd1[:, (e * 8 + a) * 256:(e * 8 + a + 1) * 256],
                           start=(a == 0), stop=False)
            c.tick(r, 'p8_t')
        r = c.t.matmul(out=pA[:, off:off + 256], lhsT=onb[:], rhs=bd1r[0:1, e * 256:(e + 1) * 256],
                       start=False, stop=True)
        nt = c.tick(r, 'p8_t')
        if e == NE - 1:
            nwd1_ct[ct] = nt
        c.s.wait_ge(c.sem('p8_t'), nt)
        r = c.s.activation(out=hB[:, q * 256:(q + 1) * 256], in_=pA[:, off:off + 256], func=AF.Relu)
        na_hist[it] = c.tick(r, 'p8_a')

    def stage_b(it):
        ct, e = divmod(it, NE)
        q = it % 2
        c.t.wait_ge(c.sem('p8_a'), na_hist[it])
        for b in range(2):
            gx = c.aux.get('p8_tt', 0); c.aux['p8_tt'] = gx + 1
            slot = gx % 2
            pv = pC[:, slot * 512:slot * 512 + 128].bitcast(BF16)[:, 0:128]
            if gx >= 2:
                c.t.wait_ge(c.sem('p8_ev'), c.n('p8_ev') - 1)
            r = c.t.transpose(out=pv, in_=hB[:, q * 256 + b * 128:q * 256 + (b + 1) * 128], identity=idt[:])
            ntt = c.tick(r, 'p8_t')
            c.v.wait_ge(c.sem('p8_t'), ntt)
            r = c.v.tensor_copy(out=hf[:, q * 256 + b * 128:q * 256 + (b + 1) * 128], in_=pv)
            c.tick(r, 'p8_ev')
        rec = pB[:, 0:1024] if q == 0 else pA[:, 1024:2048]
        c.t.wait_ge(c.sem('p8_ev'), c.n('p8_ev'))
        if it >= 2:
            c.t.wait_ge(c.sem('p8_v'), nvd_hist[it - 2])
        for half in range(2):
            for hc in range(2):
                r = c.t.matmul(out=rec[:, half * 512:half * 512 + 512],
                               lhsT=hf[:, q * 256 + hc * 128:q * 256 + (hc + 1) * 128],
                               rhs=wd2[:, (e * 2 + hc) * 1024 + half * 512:(e * 2 + hc) * 1024 + half * 512 + 512],
                               start=(hc == 0), stop=False)
                c.tick(r, 'p8_t')
            r = c.t.matmul(out=rec[:, half * 512:half * 512 + 512], lhsT=onb[:],
                           rhs=bd2r[0:1, e * 1024 + half * 512:e * 1024 + half * 512 + 512],
                           start=False, stop=True)
            nt = c.tick(r, 'p8_t')
        c.v.wait_ge(c.sem('p8_t'), nt)
        c.v.wait_ge(c.sem('p8_lx%d' % (ct % 2)), nlx_ct[ct])
        if it >= 2:
            c.v.wait_ge(c.sem('p8_a'), nsq_hist[it - 2])
        r = c.v.tensor_tensor(out=diff2[q], in0=rec, in1=xtok2[ct % 2], op=OP.subtract)
        nvd_hist[it] = nv = c.tick(r, 'p8_v')
        c.s.wait_ge(c.sem('p8_v'), nv)
        r = c.s.activation(out=sq_out, in_=diff2[q], func=AF.Square, accum_out=acc2[:, q:q + 1])
        nsq_hist[it] = na = c.tick(r, 'p8_a')
        c.v.wait_ge(c.sem('p8_a'), na)
        r = c.v.tensor_tensor(out=ls[:, e:e + 1], in0=ls[:, e:e + 1], in1=acc2[:, q:q + 1], op=OP.add)
        c.tick(r, 'p8_v')

    for it in range(total + 1):
        if it < total:
            stage_a(it)
        if it >= 1:
            stage_b(it - 1)

    c.t.wait_ge(c.sem('p8_v'), c.n('p8_v'))
    c.t.wait_ge(c.sem('p8_ld'), c.n('p8_ld'))
    r = c.t.matmul(out=pC[0:8, 0:128], lhsT=ls[:], rhs=idf[:], start=True, stop=True)
    nt = c.tick(r, 'p8_t2')
    c.v.wait_ge(c.sem('p8_t2'), nt)
    r = c.v.reduce_sum(out=lc[:], in_=pC[0:8, 0:128], axis=AX.X)
    nv = c.tick(r, 'p8_v')
    c.t.wait_ge(c.sem('p8_v'), nv)
    r = c.t.matmul(out=pC[0:1, 512:520], lhsT=lc[:], rhs=idf[0:8, 0:8], start=True, stop=True)
    nt = c.tick(r, 'p8_t2')
    c.v.wait_ge(c.sem('p8_t2'), nt)
    r = c.v.tensor_copy(out=lr[:], in_=pC[0:1, 512:520])
    nv = c.tick(r, 'p8_v')
    c.v.wait_ge(c.sem('p8_v'), nv)
    r = c.v.tensor_reduce(out=mrow[:], in_=lr[:], axis=AX.X, op=OP.min)
    nv = c.tick(r, 'p8_v')
    c.v.wait_ge(c.sem('p8_v'), nv)
    r = c.v.tensor_single_scalar(out=oh[:], in_=lr[:], scalar=mrow[0:1, 0:1], op=OP.is_equal)
    nv = c.tick(r, 'p8_v')
    c.t.wait_ge(c.sem('p8_v'), nv)
    r = c.t.matmul(out=pC[0:128, 520:528], lhsT=onrt[:], rhs=oh[:], start=True, stop=True)
    nt = c.tick(r, 'p8_t2')
    c.v.wait_ge(c.sem('p8_t2'), nt)
    r = c.v.tensor_copy(out=ohb[:], in_=pC[0:128, 520:528])
    nv = c.tick(r, 'p8_v')
    for e in range(NE):
        c.v.wait_ge(c.sem('p8_v'), c.n('p8_v'))
        if e == 0:
            r = c.v.tensor_scalar(out=wad_acc, in0=wad[:, 0:512], scalar1=ohb[:, 0:1],
                                  scalar2=None, op0=OP.mult)
        else:
            r = c.v.scalar_tensor_tensor(out=wad_acc, in0=wad[:, e * 512:(e + 1) * 512],
                                         scalar=ohb[:, e:e + 1], in1=wad_acc,
                                         op0=OP.mult, op1=OP.add)
        c.tick(r, 'p8_v')
        c.v.wait_ge(c.sem('p8_v'), c.n('p8_v'))
        if e == 0:
            r = c.v.tensor_scalar(out=wup_acc, in0=wau[0:64, 0:1024], scalar1=ohb[0:64, 0:1],
                                  scalar2=None, op0=OP.mult)
        else:
            r = c.v.scalar_tensor_tensor(out=wup_acc, in0=wau[0:64, e * 1024:(e + 1) * 1024],
                                         scalar=ohb[0:64, e:e + 1], in1=wup_acc,
                                         op0=OP.mult, op1=OP.add)
        c.tick(r, 'p8_v')
        c.sy.wait_ge(c.sem('p8_v'), c.n('p8_v'))
        c.sy.wait_ge(c.sem('p8_ld'), c.n('p8_ld'))
        r = c.sy.dma_start(out=row0[0:1, 0:64], in_=bad_d.ap()[e:e + 1, :])
        nl = c.tick(r, 'p8_ld', 16)
        c.v.wait_ge(c.sem('p8_ld'), nl)
        if e == 0:
            r = c.v.tensor_single_scalar(out=bad_acc[:], in_=row0[0:1, 0:64],
                                         scalar=oh[0:1, 0:1], op=OP.mult)
        else:
            r = c.v.scalar_tensor_tensor(out=bad_acc[:], in0=row0[0:1, 0:64],
                                         scalar=oh[0:1, e:e + 1], in1=bad_acc[:],
                                         op0=OP.mult, op1=OP.add)
        c.tick(r, 'p8_v')
        c.sy.wait_ge(c.sem('p8_v'), c.n('p8_v'))
        c.sy.wait_ge(c.sem('p8_ld'), c.n('p8_ld'))
        r = c.sy.dma_start(out=row0, in_=bau_d.ap()[e:e + 1, :])
        nl = c.tick(r, 'p8_ld', 16)
        c.v.wait_ge(c.sem('p8_ld'), nl)
        if e == 0:
            r = c.v.tensor_single_scalar(out=bau_acc, in_=row0,
                                         scalar=oh[0:1, 0:1], op=OP.mult)
        else:
            r = c.v.scalar_tensor_tensor(out=bau_acc, in0=row0,
                                         scalar=oh[0:1, e:e + 1], in1=bau_acc,
                                         op0=OP.mult, op1=OP.add)
        c.tick(r, 'p8_v')
    c.v.wait_ge(c.sem('p8_v'), c.n('p8_v'))
    r = c.v.tensor_copy(out=wadb, in_=wad_acc); c.tick(r, 'p8_v')
    r = c.v.tensor_copy(out=wupb, in_=wup_acc); c.tick(r, 'p8_v')
    r = c.v.tensor_copy(out=badb[:], in_=bad_acc[:]); c.tick(r, 'p8_v')
    r = c.v.tensor_copy(out=baub, in_=bau_acc); nvsel = c.tick(r, 'p8_v')

    fa_hist = {}; fld_hist = {}; flx_hist = {}; fout_hist = {}; fnt_hist = {}
    c.sy.wait_ge(c.sem('p8_ld'), c.n('p8_ld'))

    def fstage_a(ct):
        q = ct % 2
        if ct == 0:
            c.sy.wait_ge(c.sem('p8_v'), nvsel)
        if ct >= 1:
            c.sy.wait_ge(c.sem('p8_t'), fnt_hist[ct - 1])
        if ct >= 2:
            c.sy.wait_ge(c.sem('p8_v'), fout_hist[ct - 2])
        r = c.sy.dma_start(out=xfm.rearrange('p (a w) -> p a w', a=8),
                           in_=xn_fm.ap().rearrange('(a p) s -> p a s', p=128)[:, :, ct * 128:(ct + 1) * 128])
        nld = c.tick(r, 'p8_ld', 16)
        r = c.sy.dma_start(out=xtok2[q], in_=x3.ap().rearrange('(t p) d -> t p d', p=128)[ct])
        flx_hist[ct] = c.tick(r, 'p8_lx%d' % q, 16)
        fld_hist[ct] = nld
        base = pA[:, 0:1024] if q == 0 else pA[:, 1024:2048]
        ada = pC[:, 0:64] if q == 0 else pC[:, 512:576]
        c.t.wait_ge(c.sem('p8_ld'), nld)
        if ct == 0:
            c.t.wait_ge(c.sem('p8_v'), nvsel)
        if ct >= 2:
            c.t.wait_ge(c.sem('p8_v'), fout_hist[ct - 2])
        for half in range(2):
            for a in range(8):
                r = c.t.matmul(out=base[:, half * 512:half * 512 + 512],
                               lhsT=xfm[:, a * 128:(a + 1) * 128],
                               rhs=wbase[:, a * 1024 + half * 512: a * 1024 + half * 512 + 512],
                               start=(a == 0), stop=False)
                c.tick(r, 'p8_t')
            r = c.t.matmul(out=base[:, half * 512:half * 512 + 512],
                           lhsT=onb[:], rhs=bbb[0:1, half * 512:(half + 1) * 512],
                           start=False, stop=True)
            c.tick(r, 'p8_t')
        if ct >= 2:
            c.t.wait_ge(c.sem('p8_a'), fa_hist[ct - 2])
        for a in range(8):
            r = c.t.matmul(out=ada, lhsT=xfm[:, a * 128:(a + 1) * 128],
                           rhs=wadb[:, a * 64:(a + 1) * 64], start=(a == 0), stop=False)
            c.tick(r, 'p8_t')
        r = c.t.matmul(out=ada, lhsT=onb[:], rhs=badb[:], start=False, stop=True)
        nt = c.tick(r, 'p8_t')
        fnt_hist[ct] = nt
        c.s.wait_ge(c.sem('p8_t'), nt)
        r = c.s.activation(out=aB[:, q * 64:(q + 1) * 64], in_=ada, func=AF.Relu)
        fa_hist[ct] = c.tick(r, 'p8_a')

    def fstage_b(ct):
        q = ct % 2
        base = pA[:, 0:1024] if q == 0 else pA[:, 1024:2048]
        c.t.wait_ge(c.sem('p8_a'), fa_hist[ct])
        pv = pC[:, 64 + q * 512:64 + q * 512 + 64].bitcast(BF16)[0:64, 0:128]
        r = c.t.transpose(out=pv, in_=aB[:, q * 64:(q + 1) * 64], identity=idt[:])
        nt = c.tick(r, 'p8_t')
        c.v.wait_ge(c.sem('p8_t'), nt)
        r = c.v.tensor_copy(out=af[0:64, q * 128:(q + 1) * 128], in_=pv)
        nv2 = c.tick(r, 'p8_ev')
        c.t.wait_ge(c.sem('p8_ev'), nv2)
        if ct >= 1:
            c.t.wait_ge(c.sem('p8_v'), fout_hist[ct - 1])
        for half in range(2):
            r = c.t.matmul(out=pB[:, half * 512:(half + 1) * 512], lhsT=af[0:64, q * 128:(q + 1) * 128],
                           rhs=wupb[0:64, half * 512:(half + 1) * 512], start=True, stop=False)
            c.tick(r, 'p8_t')
            r = c.t.matmul(out=pB[:, half * 512:(half + 1) * 512], lhsT=onb[:],
                           rhs=baub[0:1, half * 512:(half + 1) * 512], start=False, stop=True)
            nt = c.tick(r, 'p8_t')
        c.v.wait_ge(c.sem('p8_t'), nt)
        c.v.wait_ge(c.sem('p8_lx%d' % q), flx_hist[ct])
        if ct >= 2:
            c.v.wait_ge(c.sem('p8_g%d' % q), 16 * (ct // 2))
        r = c.v.tensor_copy(out=diff2[q], in_=base)
        nv2 = c.tick(r, 'p8_v')
        c.v.wait_ge(c.sem('p8_v'), nv2)
        r = c.v.tensor_tensor(out=diff2[q], in0=diff2[q], in1=pB[:, 0:1024], op=OP.add)
        nv2 = c.tick(r, 'p8_v')
        c.v.wait_ge(c.sem('p8_v'), nv2)
        r = c.v.tensor_tensor(out=out2[q], in0=xtok2[q], in1=diff2[q], op=OP.add)
        fout_hist[ct] = nx = c.tick(r, 'p8_v')
        c.g.wait_ge(c.sem('p8_v'), nx)
        r = c.g.dma_start(out=out_d.ap().rearrange('(t p) d -> t p d', p=128)[ct],
                          in_=out2[q])
        c.tick(r, 'p8_g%d' % q, 16)

    for ct in range(ntile + 1):
        if ct < ntile:
            fstage_a(ct)
        if ct >= 1:
            fstage_b(ct - 1)
    c.g.wait_ge(c.sem('p8_g0'), c.n('p8_g0'))
    c.g.wait_ge(c.sem('p8_g1'), c.n('p8_g1'))


PHASES = {'w0': phase_w0, 'w1': phase_w1, 'p0': phase_p0, 'p1': phase_p1,
          'p2': phase_p2, 'p3': phase_p3, 'p4a': phase_p4a, 'p4b': phase_p4b,
          'p5': phase_p5, 'p7': phase_p7, 'p8': phase_p8}



def build_program(phase_names, seq, ext_in, ext_out, n_cores=NCORES):
    nc = bass.Bass('TRN2', target_bir_lowering=False)
    c = Ctx(nc, seq, ext_in, ext_out, n_cores)
    c.prologue()
    for i, pn in enumerate(phase_names):
        PHASES[pn](c)
        if i != len(phase_names) - 1:
            c.barrier()
    c.replay()
    return nc, c


ALL_PHASES = ['w0', 'w1', 'p0', 'p1', 'p2', 'p3', 'p4a', 'p4b', 'p5', 'p7', 'p8']
REAL_INPUTS = ['L0_attn_Wq', 'L0_attn_Wk', 'L0_attn_Wv', 'L0_attn_Wo', 'L0_attn_norm_const',
               'L0_ffn_w_gate', 'L0_ffn_w_up', 'L0_ffn_w_down',
               'L1_attn_Wq', 'L1_attn_Wk', 'L1_attn_Wv', 'L1_attn_Wfc',
               'L1_ffn_W_base', 'L1_ffn_b_base', 'L1_ffn_Wd1', 'L1_ffn_bd1',
               'L1_ffn_Wd2', 'L1_ffn_bd2', 'L1_ffn_Wa_down', 'L1_ffn_ba_down',
               'L1_ffn_Wa_up', 'L1_ffn_ba_up']


def kernel(**inputs):
    seq = 4096
    x = np.ascontiguousarray(inputs['L0_attn_x'], dtype=np.float32)
    nb = x.shape[0]
    weights = {}
    for nm in REAL_INPUTS:
        v = np.ascontiguousarray(inputs[nm], dtype=np.float32)
        if nm == 'L0_attn_norm_const':
            v = v.reshape(1, NH)
        elif nm == 'L1_ffn_b_base':
            v = v.reshape(1, D)
        weights[nm] = v
    nc, c = build_program(ALL_PHASES, seq, set(REAL_INPUTS + ['x0']), {'out'}, NCORES)
    in_maps = []
    for i in range(NCORES):
        d = dict(weights)
        d['x0'] = x[i]
        in_maps.append(d)
    res = run_bass_kernel_spmd(nc, in_maps, core_ids=list(range(NCORES)))
    out = np.stack([np.asarray(res.results[i]['out'], dtype=np.float32)
                    for i in range(NCORES)])
    return out


# version 12
# speedup vs baseline: 1.6390x; 1.0100x over previous
import sys
import numpy as np
import ml_dtypes

import concourse.bass as bass
import concourse.mybir as mybir
from concourse.alu_op_type import AluOpType
from concourse.bass_utils import run_bass_kernel_spmd

F32 = mybir.dt.float32
BF16 = mybir.dt.bfloat16
I32 = mybir.dt.int32
I16 = mybir.dt.int16
U16 = mybir.dt.uint16
AF = mybir.ActivationFunctionType
AX = mybir.AxisListType
OP = AluOpType

D = 1024
FF = 3072
NH = 16
HD = 64
NE = 8
EH = 256
RK = 64
NCORES = 8

ENGINES = ('sync', 'gpsimd', 'vector', 'scalar', 'tensor')


class RecInst:
    __slots__ = ('incs',)
    def __init__(self):
        self.incs = []
    def then_inc(self, sem, n=1):
        self.incs.append((sem, n))
        return self


class Rec:
    _CTRL = frozenset(('wait_ge', 'sem_inc', 'nop'))
    def __init__(self, q):
        self.__dict__['_q'] = q
        self.__dict__['last'] = None
    def __getattr__(self, name):
        q = self._q
        ctrl = name in Rec._CTRL
        rec = self
        def call(*a, **k):
            ri = RecInst()
            q.append((name, a, k, ri))
            if not ctrl:
                rec.__dict__['last'] = (name, ri)
            return ri
        return call


class Ctx:
    def __init__(self, nc, seq, ext_in, ext_out, n_cores=NCORES):
        self.nc = nc
        self.seq = seq
        self.n_cores = n_cores
        self.ntile = seq // 128
        self.nqg = seq // 512
        self.q = {e: [] for e in ENGINES}
        self.rec = {e: Rec(self.q[e]) for e in ENGINES}
        self.cnt = {}
        self.aux = {}
        self._sems = {}
        self._seminfo = {}
        self._clock = 0
        self._semidx = 0
        self.T = {}
        self.P = {}
        self.Dr = {}
        self.ext_in = ext_in
        self.ext_out = ext_out
        self.consts = {}
        self._nbar = 0

    @property
    def sy(self): return self.rec['sync']
    @property
    def g(self): return self.rec['gpsimd']
    @property
    def v(self): return self.rec['vector']
    @property
    def s(self): return self.rec['scalar']
    @property
    def t(self): return self.rec['tensor']


    def sem(self, name):
        info = self._seminfo.setdefault(name, dict(first=None, last=None, dma=False))
        self._use(name)
        return ('SEM', name)

    def _use(self, name):
        self._clock += 1
        info = self._seminfo[name]
        if info['first'] is None:
            info['first'] = self._clock
        info['last'] = self._clock

    def tick(self, ri, name, n=1):
        self.sem(name)
        if n >= 16:
            self._seminfo[name]['dma'] = True
        ri.then_inc(('SEM', name), n)
        self.cnt[name] = self.cnt.get(name, 0) + n
        return self.cnt[name]

    def n(self, name):
        self.sem(name)
        return self.cnt.get(name, 0)

    def _resolve_sems(self):
        names = sorted(self._seminfo, key=lambda nm: self._seminfo[nm]['first'])
        phys = []
        n2h, n2off = {}, {}
        for nm in names:
            info = self._seminfo[nm]
            total = self.cnt.get(nm, 0)
            if info['dma']:
                h = self.nc.semaphore(f'sd_{nm}').__enter__()
                n2h[nm], n2off[nm] = h, 0
                continue
            slot = None
            for p in phys:
                if p['last'] < info['first']:
                    slot = p
                    break
            if slot is None:
                slot = dict(handle=self.nc.semaphore(f'sp{len(phys)}').__enter__(),
                            last=-1, total=0)
                phys.append(slot)
            n2h[nm] = slot['handle']
            n2off[nm] = slot['total']
            slot['total'] += total
            slot['last'] = info['last']
        return n2h, n2off

    def barrier(self, engines=ENGINES):
        self._nbar += 1
        snap = getattr(self, '_bar_snap', {})
        diff = [(nm, v) for nm, v in self.cnt.items()
                if v > 0 and snap.get(nm, -1) != v]
        self._bar_snap = dict(self.cnt)
        for e in engines:
            for nm, v in diff:
                self.sem(nm)
                self.rec[e].wait_ge(('SEM', nm), v)

    VIEW_MAP = {
        'F0':      ('SC0', 0, 2048, 'f32'),
        'F1':      ('SC0', 2048, 4096, 'f32'),
        'F2':      ('SC1', 0, 2048, 'f32'),
        'F3':      ('SC1', 2048, 4096, 'f32'),
        'I0':      ('SC1', 4096, 6144, 'i32'),
        'p3_i16':  ('SC0', 4096, 5120, 'i16'),
        'p3_u16':  ('SC0', 5120, 6144, 'u16'),
        'p4_i16b': ('SC0', 6144, 7168, 'i16'),
        'p4b_x':   ('SC0', 7168, 8192, 'f32'),
        'B3':      ('SC0', 4096, 8192, 'bf16'),
    }
    _VDT = {'f32': F32, 'i32': I32, 'i16': I16, 'u16': U16, 'bf16': BF16}

    def base_tile(self, name):
        if name not in self.T:
            self.T[name] = self.nc.alloc_sbuf_tensor(f't_{name}', [128, 8192 if name == 'SC0' else 6144], F32)
        return self.T[name]

    def tile(self, name, shape, dtype):
        vm = self.VIEW_MAP.get(name)
        if vm is not None:
            base, c0, c1, dts = vm
            bt = self.base_tile(base)
            ap = bt[:, c0:c1]
            tgt = self._VDT[dts]
            if tgt != F32:
                ap = ap.bitcast(tgt)
            return ap
        if name not in self.T:
            self.T[name] = self.nc.alloc_sbuf_tensor(f't_{name}', list(shape), dtype)
        return self.T[name]

    def psum(self, name, shape, dtype=F32):
        if name not in self.P:
            self.P[name] = self.nc.alloc_psum_tensor(f'p_{name}', list(shape), dtype)
        return self.P[name]

    def dram(self, name, shape, dtype):
        if name not in self.Dr:
            if name in self.ext_in:
                kind = 'ExternalInput'
            elif name in self.ext_out:
                kind = 'ExternalOutput'
            else:
                kind = 'Internal'
            self.Dr[name] = self.nc.dram_tensor(name, list(shape), dtype, kind=kind)
        return self.Dr[name]

    def const(self, name, arr):
        if name not in self.consts:
            self.consts[name] = self.nc.inline_tensor(np.ascontiguousarray(arr), f'c_{name}')
        return self.consts[name]

    KCONST_VALS = (0.5, 1e-5, 1.0, -0.5)

    def kbias(self, val):
        t = self.tile('kconst', [128, 8], F32)
        i = self.KCONST_VALS.index(val)
        return t[:, i:i + 1]

    def prologue(self):
        t = self.tile('kconst', [128, 8], F32)
        arr = np.zeros((128, 8), np.float32)
        for i, v in enumerate(self.KCONST_VALS):
            arr[:, i] = v
        h = self.const('kconst', arr)
        r = self.sy.dma_start(out=t[:], in_=h.ap())
        self.tick(r, 'kc', 16)
        for e in ENGINES:
            self.rec[e].wait_ge(self.sem('kc'), 16)

    def replay(self):
        n2h, n2off = self._resolve_sems()

        def xl(v):
            if isinstance(v, tuple) and len(v) == 2 and v[0] == 'SEM':
                return n2h[v[1]]
            return v

        with self.nc.Block() as block:
            for ename in ENGINES:
                qq = self.q[ename]
                def mk(qq):
                    def body(eng):
                        for name, a, k, ri in qq:
                            if name == 'wait_ge':
                                semref, val = a[0], a[1]
                                nm = semref[1]
                                eng.wait_ge(n2h[nm], val + n2off[nm])
                                continue
                            a = tuple(xl(x) for x in a)
                            k = {kk: xl(vv) for kk, vv in k.items()}
                            r = getattr(eng, name)(*a, **k)
                            for semref, n in ri.incs:
                                r.then_inc(xl(semref), n)
                    return body
                getattr(block, ename)(mk(qq))



def tile_b1(c):
    return c.tile('B1', [128, max(c.ntile * 1040, 32768)], BF16)


def ident_bf16(c):
    return c.const('id_bf16', np.eye(128, dtype=ml_dtypes.bfloat16))

def ident_f32(c):
    return c.const('id_f32', np.eye(128, dtype=np.float32))

def ones_row(c):
    return c.const('ones_row', np.ones((1, 128), dtype=np.float32))


def qdq_round(c, pre, u_tile, i32_tile, cfp_tile, m_tile, out_tile, src_ap, rs_ap,
              vsem, psem, asem, clip_lo=None, clip_hi=None, out_scale=None,
              out_slice=None, extra_u_bias=0.5):
    v, s, g = c.v, c.s, c.g
    r = v.tensor_scalar(out=u_tile, in0=src_ap, scalar1=rs_ap, scalar2=extra_u_bias,
                        op0=OP.mult, op1=OP.add)
    nu = c.tick(r, vsem)
    s.wait_ge(c.sem(vsem), nu)
    r = s.activation(out=i32_tile, in_=u_tile, func=AF.Identity)
    na = c.tick(r, asem)
    g.wait_ge(c.sem(asem), na)
    r = g.tensor_copy(out=cfp_tile, in_=i32_tile)
    np_ = c.tick(r, psem)
    v.wait_ge(c.sem(psem), np_)
    r = v.tensor_tensor(out=m_tile, in0=cfp_tile, in1=u_tile, op=OP.is_gt)
    nu = c.tick(r, vsem)
    v.wait_ge(c.sem(vsem), nu)
    tgt = out_tile if out_slice is None else out_slice
    if clip_lo is None and out_scale is None:
        r = v.scalar_tensor_tensor(out=tgt, in0=m_tile, scalar=-1.0, in1=cfp_tile,
                                   op0=OP.mult, op1=OP.add)
        nu = c.tick(r, vsem)
        return nu
    r = v.scalar_tensor_tensor(out=m_tile, in0=m_tile, scalar=-1.0, in1=cfp_tile,
                               op0=OP.mult, op1=OP.add)
    nu = c.tick(r, vsem)
    v.wait_ge(c.sem(vsem), nu)
    if clip_lo is not None and out_scale is None:
        r = v.tensor_scalar(out=tgt, in0=m_tile, scalar1=clip_lo, scalar2=clip_hi,
                            op0=OP.max, op1=OP.min)
        nu = c.tick(r, vsem)
        return nu
    if clip_lo is not None:
        r = v.tensor_scalar(out=m_tile, in0=m_tile, scalar1=clip_lo, scalar2=clip_hi,
                            op0=OP.max, op1=OP.min)
        nu = c.tick(r, vsem)
        v.wait_ge(c.sem(vsem), nu)
    r = v.tensor_scalar(out=tgt, in0=m_tile, scalar1=out_scale, scalar2=None,
                        op0=OP.mult)
    nu = c.tick(r, vsem)
    return nu



def phase_w0(c):
    nc = c.nc
    names = [('L0_attn_Wq', 'wT_l0q'), ('L0_attn_Wk', 'wT_l0k'),
             ('L0_attn_Wv', 'wT_l0v'), ('L0_attn_Wo', 'wT_l0o'),
             ('L1_attn_Wq', 'wT_l1q'), ('L1_attn_Wk', 'wT_l1k'),
             ('L1_attn_Wv', 'wT_l1v'), ('L1_attn_Wfc', 'wT_l1fc')]
    b0 = c.tile('B0', [128, 32768], BF16)
    wbf = b0[:, 0:8192]
    ev = b0[:, 8192:16384]
    svb = c.tile('w0_svb', [128, 16], F32)
    sv_row = c.tile('w0_svr', [1, 16], F32)
    scr1 = c.tile('w0_s1', [1, 16], F32)
    idb = ident_bf16(c)
    onr = ones_row(c)
    idt = c.tile('w0_id', [128, 128], BF16)
    pT = c.psum('pA', [128, 2048], F32)
    pB = c.psum('pB', [128, 1024], F32)
    nct = c.dram('L0_attn_norm_const', [1, 16], F32)

    seqc = c.const('seqval', np.full((1, 1), float(c.seq), np.float32))
    seqt = c.tile('w0_seq', [1, 1], F32)
    lnt = c.tile('w0_ln', [1, 1], F32)
    onrt = c.tile('w0_onr', [1, 128], F32)
    r = c.sy.dma_start(out=idt[:], in_=idb.ap()); c.tick(r, 'w0_ld', 16)
    r = c.sy.dma_start(out=sv_row[:], in_=nct.ap()); c.tick(r, 'w0_ld', 16)
    r = c.sy.dma_start(out=seqt[:], in_=seqc.ap()); c.tick(r, 'w0_ld', 16)
    r = c.sy.dma_start(out=onrt[:], in_=onr.ap()); c.tick(r, 'w0_ld', 16)
    c.s.wait_ge(c.sem('w0_ld'), 64)
    r = c.s.activation(out=scr1[:], in_=sv_row[:], func=AF.Sigmoid)
    c.tick(r, 'w0_a')
    r = c.s.activation(out=lnt[:], in_=seqt[:], func=AF.Ln)
    c.tick(r, 'w0_a')
    c.v.wait_ge(c.sem('w0_a'), 2)
    r = c.v.tensor_scalar(out=scr1[:], in0=scr1[:], scalar1=lnt[0:1, 0:1], scalar2=-1.0,
                          op0=OP.mult, op1=OP.mult)
    c.tick(r, 'w0_v')
    c.s.wait_ge(c.sem('w0_v'), 1)
    r = c.s.activation(out=sv_row[:], in_=scr1[:], func=AF.Exp)
    c.tick(r, 'w0_a')
    c.t.wait_ge(c.sem('w0_a'), 3)
    c.t.wait_ge(c.sem('w0_ld'), 64)
    r = c.t.matmul(out=pB[0:128, 0:16], lhsT=onrt[:], rhs=sv_row[:], start=True, stop=True)
    c.tick(r, 'w0_t')
    c.v.wait_ge(c.sem('w0_t'), 1)
    r = c.v.tensor_copy(out=svb[:], in_=pB[0:128, 0:16])
    c.tick(r, 'w0_v')

    for wi, (src, dst) in enumerate(names):
        wd = c.dram(src, [D, D], F32)
        wt = c.dram(dst, [D, D], BF16)
        r = c.g.dma_start(out=wbf.rearrange('p (a d) -> p a d', a=8), in_=wd.ap().rearrange('(a p) d -> p a d', p=128))
        c.tick(r, 'w0_gld', 16)
        c.v.wait_ge(c.sem('w0_st'), 16 * wi)
        c.t.wait_ge(c.sem('w0_gld'), c.n('w0_gld'))
        if src == 'L0_attn_Wv':
            c.v.wait_ge(c.sem('w0_gld'), c.n('w0_gld'))
            for a in range(8):
                for hh in range(2):
                    h = 2 * a + hh
                    r = c.v.tensor_scalar(
                        out=wbf[hh * 64:(hh + 1) * 64, a * 1024:(a + 1) * 1024],
                        in0=wbf[hh * 64:(hh + 1) * 64, a * 1024:(a + 1) * 1024],
                        scalar1=svb[hh * 64:(hh + 1) * 64, h:h + 1], scalar2=None,
                        op0=OP.mult)
                    c.tick(r, 'w0_v')
            c.t.wait_ge(c.sem('w0_v'), c.n('w0_v'))
        for a in range(8):
            for b in range(8):
                g_idx = wi * 64 + a * 8 + b
                bank = g_idx % 2
                pv_view = pT[:, bank * 1024:bank * 1024 + 128].bitcast(BF16)[:, 0:128]
                if g_idx >= 2:
                    c.t.wait_ge(c.sem('w0_ev'), c.n('w0_ev') - 1)
                r = c.t.transpose(out=pv_view, in_=wbf[:, a * 1024 + b * 128: a * 1024 + b * 128 + 128],
                                  identity=idt[:])
                nt = c.tick(r, 'w0_t')
                c.v.wait_ge(c.sem('w0_t'), nt)
                r = c.v.tensor_copy(out=ev[:, b * 1024 + a * 128: b * 1024 + a * 128 + 128],
                                    in_=pv_view)
                c.tick(r, 'w0_ev')
        c.sy.wait_ge(c.sem('w0_ev'), c.n('w0_ev'))
        r = c.sy.dma_start(out=wt.ap().rearrange('(b p) o -> p b o', p=128), in_=ev.rearrange('p (b o) -> p b o', b=8))
        c.tick(r, 'w0_st', 16)
        c.g.wait_ge(c.sem('w0_t'), c.n('w0_t'))
    c.sy.wait_ge(c.sem('w0_st'), c.n('w0_st'))



def phase_w1(c):
    f0 = c.tile('F0', [128, 2048], F32)
    f1 = c.tile('F1', [128, 2048], F32)
    f2 = c.tile('F2', [128, 2048], F32)
    f3 = c.tile('F3', [128, 2048], F32)
    i0 = c.tile('I0', [128, 2048], I32)
    b0 = c.tile('B0', [128, 32768], BF16)
    dq = b0[:, 0:8192]
    ev = b0[:, 8192:16384]
    idt = c.tile('w0_id', [128, 128], BF16)
    onrt = c.tile('w0_onr', [1, 128], F32)
    bm = c.tile('w1_bm', [128, 96], F32)
    sblk = c.tile('w1_s', [128, 96], F32)
    rsblk = c.tile('w1_rs', [128, 96], F32)
    sc7 = c.tile('w1_7', [1, 1], F32)
    r7 = c.tile('w1_r7', [1, 1], F32)
    r7b = c.tile('w1_r7b', [128, 1], F32)
    pT = c.psum('pA', [128, 2048], F32)
    pB = c.psum('pB', [128, 1024], F32)

    c7 = c.const('c7', np.full((1, 1), 7.0, np.float32))
    r = c.sy.dma_start(out=sc7[:], in_=c7.ap()); c.tick(r, 'w1_ld', 16)
    r = c.sy.dma_start(out=idt[:], in_=ident_bf16(c).ap()); c.tick(r, 'w1_ld', 16)
    r = c.sy.dma_start(out=onrt[:], in_=ones_row(c).ap()); c.tick(r, 'w1_ld', 16)
    c.v.wait_ge(c.sem('w1_ld'), 48)
    r = c.v.reciprocal(out=r7[:], in_=sc7[:]); c.tick(r, 'w1_v')
    c.t.wait_ge(c.sem('w1_v'), c.n('w1_v'))
    r = c.t.matmul(out=pB[0:128, 0:1], lhsT=onrt[:], rhs=r7[:], start=True, stop=True)
    c.tick(r, 'w1_t')
    c.v.wait_ge(c.sem('w1_t'), c.n('w1_t'))
    r = c.v.tensor_copy(out=r7b[:], in_=pB[0:128, 0:1]); c.tick(r, 'w1_v')

    specs = [('L0_ffn_w_up', 'wT_up', FF, D), ('L0_ffn_w_gate', 'wT_gate', FF, D),
             ('L0_ffn_w_down', 'wT_down', D, FF)]
    sc0b = c.base_tile('SC0')
    wslot = (f0[:, 0:2048], sc0b[:, 4096:6144])
    blocks = []
    for src_, dst, O, I in specs:
        wd = c.dram(src_, [O, I], F32)
        wt = c.dram(dst, [I, O], BF16)
        ics = [(0, 2048), (2048, 1024)] if I == 3072 else [(0, 1024)]
        for ti in range(O // 128):
            for ci, (i_off, i_w) in enumerate(ics):
                blocks.append((wd, wt, ti, i_off, i_w, ci == len(ics) - 1, I))
    nmw = {}

    def load(k):
        wd, wt, ti, i_off, i_w, last, I = blocks[k]
        s = k % 2
        if k >= 2:
            c.sy.wait_ge(c.sem('w1_v'), nmw[k - 2])
        r = c.sy.dma_start(out=wslot[s][:, 0:i_w],
                           in_=wd.ap()[ti * 128:(ti + 1) * 128, i_off:i_off + i_w])
        return c.tick(r, 'w1_l%d' % s, 16)

    nlds = {}
    nlds[0] = load(0)
    for k in range(len(blocks)):
        if k + 1 < len(blocks):
            nlds[k + 1] = load(k + 1)
        wd, wt, ti, i_off, i_w, last, I = blocks[k]
        s = k % 2
        nb = i_w // 32
        wa = wslot[s][:, 0:i_w]
        c.v.wait_ge(c.sem('w1_l%d' % s), nlds[k])
        r = c.v.reduce_max(out=bm[:, 0:nb], in_=wa.rearrange('p (b k) -> p b k', k=32),
                           axis=AX.X, apply_absolute_value=True)
        nv = c.tick(r, 'w1_v')
        c.v.wait_ge(c.sem('w1_v'), nv)
        r = c.v.tensor_scalar(out=sblk[:, 0:nb], in0=bm[:, 0:nb], scalar1=r7b[:, 0:1],
                              scalar2=1e-12, op0=OP.mult, op1=OP.max)
        nv = c.tick(r, 'w1_v')
        c.v.wait_ge(c.sem('w1_v'), nv)
        r = c.v.reciprocal(out=rsblk[:, 0:nb], in_=sblk[:, 0:nb])
        nv = c.tick(r, 'w1_v')
        c.v.wait_ge(c.sem('w1_v'), nv)
        c.v.wait_ge(c.sem('w1_a'), c.n('w1_a'))
        r = c.v.tensor_tensor(out=f1[:, 0:i_w].rearrange('p (b k) -> p b k', k=32),
                              in0=wa.rearrange('p (b k) -> p b k', k=32),
                              in1=rsblk[:, 0:nb].unsqueeze(2).broadcast_to([128, nb, 32]),
                              op=OP.mult)
        nmw[k] = nv = c.tick(r, 'w1_v')
        c.s.wait_ge(c.sem('w1_v'), nv)
        r = c.s.activation(out=i0[:, 0:i_w], in_=f1[:, 0:i_w], func=AF.Identity, bias=c.kbias(0.5))
        na = c.tick(r, 'w1_a')
        c.s.wait_ge(c.sem('w1_a'), na)
        r = c.s.activation(out=f2[:, 0:i_w], in_=i0[:, 0:i_w], func=AF.Identity)
        na = c.tick(r, 'w1_a')
        c.v.wait_ge(c.sem('w1_a'), na)
        r = c.v.scalar_tensor_tensor(out=f3[:, 0:i_w], in0=f2[:, 0:i_w], scalar=-0.5,
                                     in1=f1[:, 0:i_w], op0=OP.add, op1=OP.is_gt)
        nv = c.tick(r, 'w1_v')
        c.v.wait_ge(c.sem('w1_v'), nv)
        r = c.v.scalar_tensor_tensor(out=f1[:, 0:i_w], in0=f3[:, 0:i_w], scalar=-1.0,
                                     in1=f2[:, 0:i_w], op0=OP.mult, op1=OP.add)
        nv = c.tick(r, 'w1_v')
        c.v.wait_ge(c.sem('w1_v'), nv)
        r = c.v.tensor_tensor(out=dq[:, i_off:i_off + i_w].rearrange('p (b k) -> p b k', k=32),
                              in0=f1[:, 0:i_w].rearrange('p (b k) -> p b k', k=32),
                              in1=sblk[:, 0:nb].unsqueeze(2).broadcast_to([128, nb, 32]),
                              op=OP.mult)
        c.tick(r, 'w1_v')
        if last:
            c.t.wait_ge(c.sem('w1_v'), c.n('w1_v'))
            for b in range(I // 128):
                g_idx = c.aux.get('w1_tt', 0)
                c.aux['w1_tt'] = g_idx + 1
                bank = g_idx % 2
                pv = pT[:, bank * 1024:bank * 1024 + 128].bitcast(BF16)[:, 0:128]
                if g_idx >= 2:
                    c.t.wait_ge(c.sem('w1_ev'), c.n('w1_ev') - 1)
                r = c.t.transpose(out=pv, in_=dq[:, b * 128:(b + 1) * 128], identity=idt[:])
                nt = c.tick(r, 'w1_t')
                c.v.wait_ge(c.sem('w1_t'), nt)
                r = c.v.tensor_copy(out=ev[:, b * 128:(b + 1) * 128], in_=pv)
                c.tick(r, 'w1_ev')
            c.g.wait_ge(c.sem('w1_ev'), c.n('w1_ev'))
            r = c.g.dma_start(out=wt.ap().rearrange('(b p) o -> p b o', p=128)[:, :, ti * 128:(ti + 1) * 128],
                              in_=ev[:, 0:I].rearrange('p (b o) -> p b o', o=128))
            nst = c.tick(r, 'w1_st', 16)
            c.v.wait_ge(c.sem('w1_st'), nst)
    c.g.wait_ge(c.sem('w1_st'), c.n('w1_st'))



def layernorm_tiles(c, pre, x_dram, xn_tok_dram, xn_fm_dram, amax_tile=None):
    f0 = c.tile('F0', [128, 2048], F32)
    f1 = c.tile('F1', [128, 2048], F32)
    f2 = c.tile('F2', [128, 2048], F32)
    st = c.tile('p0_st', [128, 16], F32)
    xnb = c.tile('B3', None, None)[:, 0:4096]
    stg = c.tile('B2', [128, 4096], BF16)
    idt = c.tile('w0_id', [128, 128], BF16)
    pT = c.psum('pA', [128, 2048], F32)
    ld, st_s, vs, as_, ts_, evs, gs = (pre + x for x in ('_ld', '_st', '_v', '_a', '_t', '_ev', '_g'))

    r = c.sy.dma_start(out=idt[:], in_=ident_bf16(c).ap()); c.tick(r, ld, 16)
    if amax_tile is not None:
        r = c.v.memset(amax_tile, 0.0); c.tick(r, vs)
    H = dict(nld={}, nsub={}, nsq={}, nsqrt={}, nmul={}, nbf={}, ntr={}, nst={})

    def stage_a(i):
        s = i % 2
        xa = f0[:, s * 1024:(s + 1) * 1024]
        if i >= 2:
            c.sy.wait_ge(c.sem(vs), H['nsub'][i - 2])
        r = c.sy.dma_start(out=xa, in_=x_dram.ap().rearrange('(t p) d -> t p d', p=128)[i])
        H['nld'][i] = nld = c.tick(r, ld + str(s), 16)
        c.v.wait_ge(c.sem(ld + str(s)), nld)
        r = c.v.reduce_sum(out=st[:, s * 8:s * 8 + 1], in_=xa, axis=AX.X); nv = c.tick(r, vs)
        c.v.wait_ge(c.sem(vs), nv)
        r = c.v.tensor_scalar(out=st[:, s * 8 + 1:s * 8 + 2], in0=st[:, s * 8:s * 8 + 1],
                              scalar1=1.0 / 1024, scalar2=None, op0=OP.mult); nv = c.tick(r, vs)
        c.v.wait_ge(c.sem(vs), nv)
        if i >= 2:
            c.v.wait_ge(c.sem(as_), H['nsq'][i - 2])
        r = c.v.tensor_scalar(out=f1[:, s * 1024:(s + 1) * 1024], in0=xa,
                              scalar1=st[:, s * 8 + 1:s * 8 + 2], scalar2=None,
                              op0=OP.subtract)
        H['nsub'][i] = c.tick(r, vs)
        c.s.wait_ge(c.sem(vs), H['nsub'][i])
        if xn_tok_dram is not None and i >= 2:
            c.s.wait_ge(c.sem(st_s + str(s)), H['nst'][i - 2])
        r = c.s.activation(out=f2[:, s * 1024:(s + 1) * 1024], in_=f1[:, s * 1024:(s + 1) * 1024],
                           func=AF.Square, accum_out=st[:, s * 8 + 2:s * 8 + 3])
        H['nsq'][i] = na = c.tick(r, as_)
        c.s.wait_ge(c.sem(as_), na)
        r = c.s.activation(out=st[:, s * 8 + 3:s * 8 + 4], in_=st[:, s * 8 + 2:s * 8 + 3],
                           func=AF.Sqrt, scale=1.0 / 1024, bias=c.kbias(1e-5))
        H['nsqrt'][i] = c.tick(r, as_)

    def stage_b(i):
        s = i % 2
        c.v.wait_ge(c.sem(as_), H['nsqrt'][i])
        r = c.v.reciprocal(out=st[:, s * 8 + 4:s * 8 + 5], in_=st[:, s * 8 + 3:s * 8 + 4])
        nv = c.tick(r, vs)
        c.v.wait_ge(c.sem(vs), nv)
        r = c.v.tensor_scalar(out=f2[:, s * 1024:(s + 1) * 1024],
                              in0=f1[:, s * 1024:(s + 1) * 1024],
                              scalar1=st[:, s * 8 + 4:s * 8 + 5], scalar2=None, op0=OP.mult)
        H['nmul'][i] = nv = c.tick(r, vs)
        if xn_tok_dram is not None:
            c.sy.wait_ge(c.sem(vs), nv)
            r = c.sy.dma_start(out=xn_tok_dram.ap().rearrange('(t p) d -> t p d', p=128)[i],
                               in_=f2[:, s * 1024:(s + 1) * 1024])
            H['nst'][i] = c.tick(r, st_s + str(s), 16)
        if amax_tile is not None:
            c.v.wait_ge(c.sem(vs), nv)
            r = c.v.reduce_max(out=st[:, s * 8 + 5:s * 8 + 6], in_=f2[:, s * 1024:(s + 1) * 1024],
                               axis=AX.X, apply_absolute_value=True); nv = c.tick(r, vs)
            c.v.wait_ge(c.sem(vs), nv)
            r = c.v.tensor_tensor(out=amax_tile, in0=amax_tile, in1=st[:, s * 8 + 5:s * 8 + 6],
                                  op=OP.max)
            nv = c.tick(r, vs)
        c.v.wait_ge(c.sem(vs), nv)
        if i >= 2:
            c.v.wait_ge(c.sem(ts_), H['ntr'][i - 2])
        r = c.v.tensor_copy(out=xnb[:, s * 1024:(s + 1) * 1024],
                            in_=f2[:, s * 1024:(s + 1) * 1024])
        H['nbf'][i] = nv = c.tick(r, vs)
        c.t.wait_ge(c.sem(vs), nv)
        if i == 0:
            c.t.wait_ge(c.sem(ld), 16)
        col = (i % 4) * 128
        stg3 = stg.rearrange('p (b s) -> p b s', s=512)
        for half in range(2):
            g_idx = c.aux.get(pre + '_tt', 0)
            c.aux[pre + '_tt'] = g_idx + 1
            bank = g_idx % 2
            if g_idx >= 2:
                c.t.wait_ge(c.sem(evs), c.n(evs) - 1)
            for b4 in range(4):
                b = half * 4 + b4
                pv = pT[:, bank * 1024 + b4 * 64: bank * 1024 + b4 * 64 + 64].bitcast(BF16)
                r = c.t.transpose(out=pv, in_=xnb[:, s * 1024 + b * 128: s * 1024 + (b + 1) * 128],
                                  identity=idt[:])
                H['ntr'][i] = nt = c.tick(r, ts_)
            c.v.wait_ge(c.sem(ts_), nt)
            if i % 4 == 0 and half == 0:
                c.v.wait_ge(c.sem(gs), c.n(gs))
            r = c.v.tensor_copy(
                out=stg3[:, half * 4:half * 4 + 4, col:col + 128],
                in_=pT[:, bank * 1024: bank * 1024 + 256].bitcast(BF16).rearrange('p (b s) -> p b s', s=128))
            c.tick(r, evs)
        if i % 4 == 3:
            c.g.wait_ge(c.sem(evs), c.n(evs))
            j = i // 4
            r = c.g.dma_start(
                out=xn_fm_dram.ap().rearrange('(a p) s -> p a s', p=128)[:, :, j * 512:(j + 1) * 512],
                in_=stg[:, 0:4096].rearrange('p (a s) -> p a s', s=512))
            c.tick(r, gs, 16)

    for i in range(c.ntile + 1):
        if i < c.ntile:
            stage_a(i)
        if i >= 1:
            stage_b(i - 1)
    c.g.wait_ge(c.sem(gs), c.n(gs))
    if xn_tok_dram is not None:
        c.sy.wait_ge(c.sem(st_s + '0'), c.n(st_s + '0'))
        c.sy.wait_ge(c.sem(st_s + '1'), c.n(st_s + '1'))


def global_absmax_to_scale(c, pre, amax_tile, qmax, s_out, rs_out):
    onrt = c.tile('w0_onr', [1, 128], F32)
    idf = c.tile('p0_idf', [128, 128], F32)
    rowm = c.tile('p0_rowm', [1, 128], F32)
    sc = c.tile('p0_sc', [1, 8], F32)
    qm = c.tile('p0_qm', [1, 1], F32)
    rqm = c.tile('p0_rqm', [1, 1], F32)
    pB = c.psum('pB', [128, 1024], F32)
    cc_in = c.dram(pre + '_ccin', [1, 8], F32)
    cc_out = c.dram(pre + '_ccout', [1, 8], F32)

    r = c.sy.dma_start(out=idf[:], in_=ident_f32(c).ap()); c.tick(r, pre + '_ld2', 16)
    qmc = c.const('qmax_%d' % int(qmax), np.full((1, 1), float(qmax), np.float32))
    r = c.sy.dma_start(out=qm[:], in_=qmc.ap()); c.tick(r, pre + '_ld2', 16)
    r = c.sy.dma_start(out=onrt[:], in_=ones_row(c).ap()); c.tick(r, pre + '_ld2', 16)
    c.t.wait_ge(c.sem(pre + '_ld2'), 48)
    r = c.t.matmul(out=pB[0:1, 0:128], lhsT=amax_tile, rhs=idf[:], start=True, stop=True)
    nt = c.tick(r, pre + '_t2')
    c.v.wait_ge(c.sem(pre + '_t2'), nt)
    r = c.v.reduce_max(out=rowm[0:1, 0:1], in_=pB[0:1, 0:128], axis=AX.X)
    nv = c.tick(r, pre + '_v2')
    c.v.wait_ge(c.sem(pre + '_v2'), nv)
    r = c.v.tensor_copy(out=sc[:], in_=rowm[0:1, 0:1].broadcast_to([1, 8]))
    nv = c.tick(r, pre + '_v2')
    c.g.wait_ge(c.sem(pre + '_v2'), nv)
    r = c.g.dma_start(out=cc_in.ap(), in_=sc[:]); c.tick(r, pre + '_g2', 16)
    c.g.wait_ge(c.sem(pre + '_g2'), c.n(pre + '_g2'))
    r = c.g.collective_compute('AllReduce', OP.max,
                               replica_groups=[list(range(c.n_cores))],
                               ins=[cc_in.ap()], outs=[cc_out.ap()])
    c.tick(r, pre + '_cc')
    c._seminfo[pre + '_cc']['dma'] = True
    c.g.wait_ge(c.sem(pre + '_cc'), 1)
    r = c.g.dma_start(out=sc[:], in_=cc_out.ap()); c.tick(r, pre + '_g2', 16)
    c.v.wait_ge(c.sem(pre + '_g2'), c.n(pre + '_g2'))
    r = c.v.reciprocal(out=rqm[:], in_=qm[:]); nv = c.tick(r, pre + '_v2')
    c.v.wait_ge(c.sem(pre + '_v2'), nv)
    r = c.v.tensor_scalar(out=sc[0:1, 1:2], in0=sc[0:1, 0:1], scalar1=rqm[0:1, 0:1],
                          scalar2=1e-12, op0=OP.mult, op1=OP.max)
    nv = c.tick(r, pre + '_v2')
    c.v.wait_ge(c.sem(pre + '_v2'), nv)
    r = c.v.reciprocal(out=sc[0:1, 2:3], in_=sc[0:1, 1:2]); nv = c.tick(r, pre + '_v2')
    c.t.wait_ge(c.sem(pre + '_v2'), nv)
    r = c.t.matmul(out=pB[0:128, 8:10], lhsT=onrt[:], rhs=sc[0:1, 1:3], start=True, stop=True)
    nt = c.tick(r, pre + '_t2')
    c.v.wait_ge(c.sem(pre + '_t2'), nt)
    r = c.v.tensor_copy(out=s_out, in_=pB[0:128, 8:9]); c.tick(r, pre + '_v2')
    r = c.v.tensor_copy(out=rs_out, in_=pB[0:128, 9:10]); nv = c.tick(r, pre + '_v2')
    return nv


def phase_p0(c):
    x_dram = c.dram('x0', [c.seq, D], F32)
    xn_tok = c.dram('xn_tok_l0', [c.seq, D], F32)
    xn_fm = c.dram('xn_fm_l0', [D, c.seq], BF16)
    xq_fm = c.dram('xq_fm', [D, c.seq], BF16)
    amax = c.tile('p0_amax', [128, 1], F32)
    layernorm_tiles(c, 'p0', x_dram, xn_tok, xn_fm, amax_tile=amax[:])
    c.barrier()
    s1b = c.tile('p0_s1b', [128, 1], F32)
    rs1b = c.tile('p0_rs1b', [128, 1], F32)
    global_absmax_to_scale(c, 'p0c', amax[:], 32767.0, s1b[:], rs1b[:])
    c.barrier()
    f0 = c.tile('F0', [128, 2048], F32)
    f1 = c.tile('F1', [128, 2048], F32)
    f2 = c.tile('F2', [128, 2048], F32)
    f3 = c.tile('F3', [128, 2048], F32)
    i0 = c.tile('I0', [128, 2048], I32)
    xnb = c.tile('B3', None, None)[:, 0:4096]
    stg = c.tile('B2', [128, 4096], BF16)
    idt = c.tile('w0_id', [128, 128], BF16)
    pT = c.psum('pA', [128, 2048], F32)
    Hq = dict(nld={}, nts={}, na={}, np={}, nstt={}, nsc={}, ntr={})

    def qstage_a(i):
        s = i % 2
        if i >= 2:
            c.sy.wait_ge(c.sem('q0_v'), Hq['nts'][i - 2])
        r = c.sy.dma_start(out=f0[:, s * 1024:(s + 1) * 1024],
                           in_=xn_tok.ap().rearrange('(t p) d -> t p d', p=128)[i])
        Hq['nld'][i] = nld = c.tick(r, 'q0_l%d' % s, 16)
        c.v.wait_ge(c.sem('q0_l%d' % s), nld)
        if i >= 2:
            c.v.wait_ge(c.sem('q0_a'), Hq['na'][i - 2])
        r = c.v.tensor_scalar(out=f1[:, s * 1024:(s + 1) * 1024],
                              in0=f0[:, s * 1024:(s + 1) * 1024], scalar1=rs1b[:, 0:1],
                              scalar2=None, op0=OP.mult)
        Hq['nts'][i] = nv = c.tick(r, 'q0_v')
        c.s.wait_ge(c.sem('q0_v'), nv)
        if i >= 2:
            c.s.wait_ge(c.sem('q0_p'), Hq['np'][i - 2])
        r = c.s.activation(out=i0[:, s * 1024:(s + 1) * 1024],
                           in_=f1[:, s * 1024:(s + 1) * 1024], func=AF.Identity, bias=c.kbias(0.5))
        Hq['na'][i] = na = c.tick(r, 'q0_a')
        c.g.wait_ge(c.sem('q0_a'), na)
        if i >= 2:
            c.g.wait_ge(c.sem('q0_v'), Hq['nstt'][i - 2])
        r = c.g.tensor_copy(out=f2[:, s * 1024:(s + 1) * 1024], in_=i0[:, s * 1024:(s + 1) * 1024])
        Hq['np'][i] = c.tick(r, 'q0_p')

    def qstage_b(i):
        s = i % 2
        c.v.wait_ge(c.sem('q0_p'), Hq['np'][i])
        r = c.v.scalar_tensor_tensor(out=f3[:, s * 1024:(s + 1) * 1024],
                                     in0=f2[:, s * 1024:(s + 1) * 1024], scalar=-0.5,
                                     in1=f1[:, s * 1024:(s + 1) * 1024], op0=OP.add, op1=OP.is_gt)
        nv = c.tick(r, 'q0_v')
        c.v.wait_ge(c.sem('q0_v'), nv)
        r = c.v.scalar_tensor_tensor(out=f1[:, s * 1024:(s + 1) * 1024],
                                     in0=f3[:, s * 1024:(s + 1) * 1024], scalar=-1.0,
                                     in1=f2[:, s * 1024:(s + 1) * 1024], op0=OP.mult, op1=OP.add)
        Hq['nstt'][i] = nv = c.tick(r, 'q0_v')
        c.v.wait_ge(c.sem('q0_v'), nv)
        if i >= 2:
            c.v.wait_ge(c.sem('q0_t'), Hq['ntr'][i - 2])
        r = c.v.tensor_scalar(out=xnb[:, s * 1024:(s + 1) * 1024],
                              in0=f1[:, s * 1024:(s + 1) * 1024], scalar1=s1b[:, 0:1],
                              scalar2=None, op0=OP.mult)
        Hq['nsc'][i] = nv = c.tick(r, 'q0_v')
        c.t.wait_ge(c.sem('q0_v'), nv)
        col = (i % 4) * 128
        stg3 = stg.rearrange('p (b s) -> p b s', s=512)
        for half in range(2):
            g_idx = c.aux.get('q0_tt', 0)
            c.aux['q0_tt'] = g_idx + 1
            bank = g_idx % 2
            if g_idx >= 2:
                c.t.wait_ge(c.sem('q0_ev'), c.n('q0_ev') - 1)
            for b4 in range(4):
                b = half * 4 + b4
                pv = pT[:, bank * 1024 + b4 * 64: bank * 1024 + b4 * 64 + 64].bitcast(BF16)
                r = c.t.transpose(out=pv, in_=xnb[:, s * 1024 + b * 128: s * 1024 + (b + 1) * 128],
                                  identity=idt[:])
                Hq['ntr'][i] = nt = c.tick(r, 'q0_t')
            c.v.wait_ge(c.sem('q0_t'), nt)
            if i % 4 == 0 and half == 0:
                c.v.wait_ge(c.sem('q0_g'), c.n('q0_g'))
            r = c.v.tensor_copy(
                out=stg3[:, half * 4:half * 4 + 4, col:col + 128],
                in_=pT[:, bank * 1024: bank * 1024 + 256].bitcast(BF16).rearrange('p (b s) -> p b s', s=128))
            c.tick(r, 'q0_ev')
        if i % 4 == 3:
            c.g.wait_ge(c.sem('q0_ev'), c.n('q0_ev'))
            j = i // 4
            r = c.g.dma_start(
                out=xq_fm.ap().rearrange('(a p) s -> p a s', p=128)[:, :, j * 512:(j + 1) * 512],
                in_=stg[:, 0:4096].rearrange('p (a s) -> p a s', s=512))
            c.tick(r, 'q0_g', 16)

    for i in range(c.ntile + 1):
        if i < c.ntile:
            qstage_a(i)
        if i >= 1:
            qstage_b(i - 1)
    c.g.wait_ge(c.sem('q0_g'), c.n('q0_g'))



def _proj_tok(c, pre, xfm, w_sb, a_cols, out_ps, nmm_sem):
    n = 0
    for half in range(2):
        for a in range(8):
            r = c.t.matmul(out=out_ps[:, half * 512:(half + 1) * 512],
                           lhsT=xfm[:, a * 128:(a + 1) * 128],
                           rhs=w_sb[:, a_cols + a * 1024 + half * 512: a_cols + a * 1024 + half * 512 + 512],
                           start=(a == 0), stop=(a == 7))
            n = c.tick(r, nmm_sem)
    return n


def _l2norm_tok(c, pre, src_f32, dst_bf, st, vsem, asem):
    r = c.v.reduce_sum(out=st[:, 0:16], in_=src_f32.rearrange('p (h e) -> p h e', e=HD),
                       axis=AX.X, apply_absolute_value=False)
    return r


def phase_p1(c):
    seq = c.seq
    x0 = c.dram('x0', [seq, D], F32)
    x1 = c.dram('x1', [seq, D], F32)
    xn_fm = c.dram('xn_fm_l0', [D, seq], BF16)
    wk = c.dram('wT_l0k', [D, D], BF16)
    wv = c.dram('wT_l0v', [D, D], BF16)
    wq = c.dram('wT_l0q', [D, D], BF16)
    wo = c.dram('wT_l0o', [D, D], BF16)

    w_sb = c.tile('B0', [128, 32768], BF16)
    sc1_ = c.base_tile('SC1')
    f3bf = sc1_[:, 2048:4096].bitcast(BF16)
    i0bf = sc1_[:, 4096:6144].bitcast(BF16)
    xfm = f3bf[:, 0:1024]
    KB = f3bf[:, 1024:2048]
    VB = f3bf[:, 2048:3072]
    afm = f3bf[:, 3072:4096]
    kvb = i0bf[:, 0:1024]
    f0 = c.tile('F0', [128, 2048], F32)
    f1 = c.tile('F1', [128, 2048], F32)
    f2 = c.tile('F2', [128, 2048], F32)
    st = c.tile('p0_st', [128, 8], F32)
    stq = c.tile('p1_stq', [128, 48], F32)
    idt = c.tile('w0_id', [128, 128], BF16)
    pA = c.psum('pA', [128, 2048], F32)
    pB = c.psum('pB', [128, 1024], F32)
    pC = c.psum('pC', [128, 1024], F32)

    r = c.sy.dma_start(out=idt[:], in_=ident_bf16(c).ap()); c.tick(r, 'p1_w', 16)
    r = c.sy.dma_start(out=w_sb[:, 0:8192].rearrange('p (a o) -> p a o', a=8),
                       in_=wk.ap().rearrange('(a p) o -> p a o', p=128))
    c.tick(r, 'p1_w', 16)
    r = c.sy.dma_start(out=w_sb[:, 8192:16384].rearrange('p (a o) -> p a o', a=8),
                       in_=wv.ap().rearrange('(a p) o -> p a o', p=128))
    c.tick(r, 'p1_w', 16)
    c.t.wait_ge(c.sem('p1_w'), 48)

    def sq_norm_scale(src_f32, dst_bf, col0):
        ss, no, rn = (stq[:, col0:col0 + 16], stq[:, col0 + 16:col0 + 32],
                      stq[:, col0 + 32:col0 + 48])
        c.s.wait_ge(c.sem('p1_v'), c.n('p1_v'))
        c.s.wait_ge(c.sem('p1_g'), c.n('p1_g'))
        r = c.s.activation(out=f1[:, 0:1024], in_=src_f32, func=AF.Square)
        na = c.tick(r, 'p1_a')
        c.v.wait_ge(c.sem('p1_a'), na)
        r = c.v.reduce_sum(out=ss, in_=f1[:, 0:1024].rearrange('p (h e) -> p h e', e=HD),
                           axis=AX.X)
        nv = c.tick(r, 'p1_v')
        c.s.wait_ge(c.sem('p1_v'), nv)
        r = c.s.activation(out=no, in_=ss, func=AF.Sqrt)
        na = c.tick(r, 'p1_a')
        c.v.wait_ge(c.sem('p1_a'), na)
        r = c.v.tensor_scalar(out=no, in0=no, scalar1=1e-12, scalar2=None, op0=OP.max)
        nv = c.tick(r, 'p1_v')
        c.v.wait_ge(c.sem('p1_v'), nv)
        r = c.v.reciprocal(out=rn, in_=no)
        nv = c.tick(r, 'p1_v')
        c.v.wait_ge(c.sem('p1_v'), nv)
        r = c.v.tensor_tensor(out=dst_bf.rearrange('p (h e) -> p h e', e=HD),
                              in0=src_f32.rearrange('p (h e) -> p h e', e=HD),
                              in1=rn.unsqueeze(2).broadcast_to([128, NH, HD]), op=OP.mult)
        nv = c.tick(r, 'p1_v')
        return nv

    for i in range(c.ntile):
        c.sy.wait_ge(c.sem('p1_t'), c.n('p1_t'))
        r = c.sy.dma_start(out=xfm.rearrange('p (a w) -> p a w', a=8),
                           in_=xn_fm.ap().rearrange('(a p) s -> p a s', p=128)[:, :, i * 128:(i + 1) * 128])
        nld = c.tick(r, 'p1_ld', 16)
        c.t.wait_ge(c.sem('p1_ld'), nld)
        if i > 0:
            c.t.wait_ge(c.sem('p1_v'), c.n('p1_v'))
            c.t.wait_ge(c.sem('p1_a'), c.n('p1_a'))
        _proj_tok(c, 'p1', xfm, w_sb, 0, pA[:, 0:1024], 'p1_t')
        nmmv = _proj_tok(c, 'p1', xfm, w_sb, 8192, pA[:, 1024:2048], 'p1_t')
        c.v.wait_ge(c.sem('p1_t'), nmmv)
        r = c.v.tensor_copy(out=f0[:, 0:1024], in_=pA[:, 0:1024])
        c.tick(r, 'p1_v')
        c.s.wait_ge(c.sem('p1_t'), nmmv)
        r = c.s.activation(out=VB, in_=pA[:, 1024:2048], func=AF.Identity)
        nva = c.tick(r, 'p1_a')
        c.v.wait_ge(c.sem('p1_v'), c.n('p1_v'))
        nkn = sq_norm_scale(f0[:, 0:1024], KB, 0)
        c.t.wait_ge(c.sem('p1_v'), nkn)
        c.t.wait_ge(c.sem('p1_a'), nva)
        for h in range(NH):
            par = h % 2
            r = c.t.matmul(out=pB[par * 64:par * 64 + 64,
                                  par * 512 + (h // 2) * 64: par * 512 + (h // 2) * 64 + 64],
                           lhsT=KB[:, h * 64:(h + 1) * 64], rhs=VB[:, h * 64:(h + 1) * 64],
                           start=(i == 0 and h in (0, 1)),
                           stop=(i == c.ntile - 1 and h in (14, 15)),
                           tile_position=(0, par * 64))
            c.tick(r, 'p1_t')

    c.v.wait_ge(c.sem('p1_t'), c.n('p1_t'))
    r = c.v.tensor_copy(out=kvb[0:64, 0:512], in_=pB[0:64, 0:512])
    c.tick(r, 'p1_v')
    r = c.v.tensor_copy(out=kvb[64:128, 512:1024], in_=pB[64:128, 512:1024])
    nkv = c.tick(r, 'p1_v')
    c.sy.wait_ge(c.sem('p1_t'), c.n('p1_t'))
    r = c.sy.dma_start(out=w_sb[:, 0:8192].rearrange('p (a o) -> p a o', a=8),
                       in_=wq.ap().rearrange('(a p) o -> p a o', p=128))
    c.tick(r, 'p1_w', 16)
    r = c.sy.dma_start(out=w_sb[:, 8192:16384].rearrange('p (a o) -> p a o', a=8),
                       in_=wo.ap().rearrange('(a p) o -> p a o', p=128))
    nld2 = c.tick(r, 'p1_w', 16)
    c.t.wait_ge(c.sem('p1_w'), nld2)
    c.t.wait_ge(c.sem('p1_v'), nkv)

    for i in range(c.ntile):
        c.sy.wait_ge(c.sem('p1_t'), c.n('p1_t'))
        r = c.sy.dma_start(out=xfm.rearrange('p (a w) -> p a w', a=8),
                           in_=xn_fm.ap().rearrange('(a p) s -> p a s', p=128)[:, :, i * 128:(i + 1) * 128])
        nld = c.tick(r, 'p1_ld', 16)
        c.sy.wait_ge(c.sem('p1_v'), c.n('p1_v'))
        r = c.sy.dma_start(out=f2[:, 0:1024],
                           in_=x0.ap().rearrange('(t p) d -> t p d', p=128)[i])
        nldx = c.tick(r, 'p1_ld', 16)
        c.t.wait_ge(c.sem('p1_ld'), nldx)
        if i > 0:
            c.t.wait_ge(c.sem('p1_v'), c.n('p1_v'))
            c.t.wait_ge(c.sem('p1_a'), c.n('p1_a'))
        nmm = _proj_tok(c, 'p1', xfm, w_sb, 0, pA[:, 0:1024], 'p1_t')
        c.v.wait_ge(c.sem('p1_t'), nmm)
        r = c.v.tensor_copy(out=f0[:, 0:1024], in_=pA[:, 0:1024])
        c.tick(r, 'p1_v')
        c.v.wait_ge(c.sem('p1_v'), c.n('p1_v'))
        nqn = sq_norm_scale(f0[:, 0:1024], KB, 0)
        c.t.wait_ge(c.sem('p1_v'), nqn)
        for b in range(8):
            g_idx = c.aux.get('p1_tt', 0); c.aux['p1_tt'] = g_idx + 1
            bank = g_idx % 2
            pv = pC[:, bank * 512:bank * 512 + 128].bitcast(BF16)[:, 0:128]
            if g_idx >= 2:
                c.t.wait_ge(c.sem('p1_ev'), c.n('p1_ev') - 1)
            r = c.t.transpose(out=pv, in_=KB[:, b * 128:(b + 1) * 128], identity=idt[:])
            nt = c.tick(r, 'p1_t')
            c.v.wait_ge(c.sem('p1_t'), nt)
            r = c.v.tensor_copy(out=afm[:, b * 128:(b + 1) * 128], in_=pv)
            c.tick(r, 'p1_ev')
        c.t.wait_ge(c.sem('p1_ev'), c.n('p1_ev'))
        for h in range(NH):
            par = h % 2
            lhs = kvb[par * 64:par * 64 + 64,
                      par * 512 + (h // 2) * 64: par * 512 + (h // 2) * 64 + 64]
            rhs = afm[(h % 2) * 64:(h % 2) * 64 + 64, (h // 2) * 128:(h // 2) * 128 + 128]
            r = c.t.matmul(out=pA[(h % 2) * 64:(h % 2) * 64 + 64,
                                  1024 + (h // 2) * 128:1024 + (h // 2) * 128 + 128],
                           lhsT=lhs, rhs=rhs, start=True, stop=True,
                           tile_position=((h % 2) * 64, (h % 2) * 64))
            natt = c.tick(r, 'p1_t')
        c.v.wait_ge(c.sem('p1_t'), natt)
        r = c.v.tensor_copy(out=afm, in_=pA[:, 1024:2048])
        nev = c.tick(r, 'p1_v')
        c.t.wait_ge(c.sem('p1_v'), nev)
        for half in range(2):
            for a in range(8):
                r = c.t.matmul(out=pB[:, half * 512:(half + 1) * 512],
                               lhsT=afm[:, a * 128:(a + 1) * 128],
                               rhs=w_sb[:, 8192 + a * 1024 + half * 512: 8192 + a * 1024 + half * 512 + 512],
                               start=(a == 0), stop=(a == 7))
                no = c.tick(r, 'p1_t')
        c.v.wait_ge(c.sem('p1_t'), no)
        c.v.wait_ge(c.sem('p1_a'), c.n('p1_a'))
        r = c.v.tensor_tensor(out=f1[:, 0:1024], in0=f2[:, 0:1024], in1=pB[:, 0:1024],
                              op=OP.add)
        nx1 = c.tick(r, 'p1_v')
        c.g.wait_ge(c.sem('p1_v'), nx1)
        r = c.g.dma_start(out=x1.ap().rearrange('(t p) d -> t p d', p=128)[i],
                          in_=f1[:, 0:1024])
        c.tick(r, 'p1_g', 16)
        c.v.wait_ge(c.sem('p1_g'), c.n('p1_g'))
    c.g.wait_ge(c.sem('p1_g'), c.n('p1_g'))



def _row_to_scal(c, pre, vals_and_dsts):
    pass


def _coll_max(c, pre, src_row_ap, n_items, dst_row_ap):
    cc_in = c.dram(pre + '_ccin', [1, 8], F32)
    cc_out = c.dram(pre + '_ccout', [1, 8], F32)
    r = c.g.dma_start(out=cc_in.ap(), in_=src_row_ap)
    c.tick(r, pre + '_g', 16)
    c.g.wait_ge(c.sem(pre + '_g'), c.n(pre + '_g'))
    r = c.g.collective_compute('AllReduce', OP.max,
                               replica_groups=[list(range(c.n_cores))],
                               ins=[cc_in.ap()], outs=[cc_out.ap()])
    c.tick(r, pre + '_cc')
    c._seminfo[pre + '_cc']['dma'] = True
    c.g.wait_ge(c.sem(pre + '_cc'), 1)
    r = c.g.dma_start(out=dst_row_ap, in_=cc_out.ap())
    return c.tick(r, pre + '_g', 16)


def _xpart_max(c, pre, amax_col_ap, dst_row_ap):
    idf = c.tile('p0_idf', [128, 128], F32)
    pB = c.psum('pB', [128, 1024], F32)
    if not c.aux.get(pre + '_idf'):
        c.aux[pre + '_idf'] = True
        r = c.sy.dma_start(out=idf[:], in_=ident_f32(c).ap()); c.tick(r, pre + '_ld', 16)
    c.t.wait_ge(c.sem(pre + '_ld'), c.n(pre + '_ld'))
    c.t.wait_ge(c.sem(pre + '_v'), c.n(pre + '_v'))
    r = c.t.matmul(out=pB[0:1, 0:128], lhsT=amax_col_ap, rhs=idf[:], start=True, stop=True)
    nt = c.tick(r, pre + '_t')
    c.v.wait_ge(c.sem(pre + '_t'), nt)
    r = c.v.reduce_max(out=dst_row_ap, in_=pB[0:1, 0:128], axis=AX.X)
    return c.tick(r, pre + '_v')


def _bcast_cols(c, pre, src_row_ap, n, dst_tile_ap, psum_off=512):
    onrt = c.tile('w0_onr', [1, 128], F32)
    pB = c.psum('pB', [128, 1024], F32)
    c.t.wait_ge(c.sem(pre + '_v'), c.n(pre + '_v'))
    r = c.t.matmul(out=pB[0:128, psum_off:psum_off + n], lhsT=onrt[:], rhs=src_row_ap,
                   start=True, stop=True)
    nt = c.tick(r, pre + '_t')
    c.v.wait_ge(c.sem(pre + '_t'), nt)
    r = c.v.tensor_copy(out=dst_tile_ap, in_=pB[0:128, psum_off:psum_off + n])
    return c.tick(r, pre + '_v')


def phase_p2(c):
    seq = c.seq
    ncc = seq // 512
    xq_fm = c.dram('xq_fm', [D, seq], BF16)
    scalA = c.dram('scalA', [1, 8], F32)
    w_sb = c.tile('B0', [128, 32768], BF16)
    xqs = c.tile('B2', [128, 4096], BF16)
    am = c.tile('p2_am', [128, 2], F32)
    rc = c.tile('p2_rc', [128, 1], F32)
    row = c.tile('p2_row', [1, 8], F32)
    pA = c.psum('pA', [128, 2048], F32)

    r = c.v.memset(am[:], 0.0); c.tick(r, 'p2_v')
    r = c.v.memset(row[:], 0.0); c.tick(r, 'p2_v')
    nred = {}
    for wi, wname in enumerate(['wT_up', 'wT_gate']):
        wd = c.dram(wname, [D, FF], BF16)
        c.sy.wait_ge(c.sem('p2_t'), c.n('p2_t'))
        r = c.sy.dma_start(out=w_sb[:, 0:8 * FF].rearrange('p (a o) -> p a o', a=8),
                           in_=wd.ap().rearrange('(a p) o -> p a o', p=128))
        nw = c.tick(r, 'p2_w', 16)
        for cc in range(ncc):
            c.sy.wait_ge(c.sem('p2_t'), c.n('p2_t'))
            r = c.sy.dma_start(out=xqs[:].rearrange('p (a s) -> p a s', a=8),
                               in_=xq_fm.ap().rearrange('(a p) s -> p a s', p=128)[:, :, cc * 512:(cc + 1) * 512])
            nld = c.tick(r, 'p2_ld', 16)
            c.t.wait_ge(c.sem('p2_ld'), nld)
            c.t.wait_ge(c.sem('p2_w'), nw)
            for t in range(FF // 128):
                blk = (wi * ncc + cc) * (FF // 128) + t
                bank = blk % 2
                if blk >= 2:
                    c.t.wait_ge(c.sem('p2_v'), nred[blk - 2])
                for a in range(8):
                    r = c.t.matmul(out=pA[:, bank * 512:(bank + 1) * 512],
                                   lhsT=w_sb[:, a * FF + t * 128: a * FF + t * 128 + 128],
                                   rhs=xqs[:, a * 512:(a + 1) * 512],
                                   start=(a == 0), stop=(a == 7))
                    nmm = c.tick(r, 'p2_t')
                c.v.wait_ge(c.sem('p2_t'), nmm)
                c.v.wait_ge(c.sem('p2_v'), c.n('p2_v'))
                r = c.v.reduce_max(out=rc[:], in_=pA[:, bank * 512:(bank + 1) * 512],
                                   axis=AX.X, apply_absolute_value=True)
                nv = c.tick(r, 'p2_v')
                c.v.wait_ge(c.sem('p2_v'), nv)
                r = c.v.tensor_tensor(out=am[:, wi:wi + 1], in0=am[:, wi:wi + 1],
                                      in1=rc[:], op=OP.max)
                nred[blk] = c.tick(r, 'p2_v')
    c.t.wait_ge(c.sem('p2_v'), c.n('p2_v'))
    _xpart_max(c, 'p2x', am[:, 0:1], row[0:1, 0:1])
    _xpart_max(c, 'p2x', am[:, 1:2], row[0:1, 1:2])
    nv = c.n('p2x_v')
    c.g.wait_ge(c.sem('p2x_v'), nv)
    ng = _coll_max(c, 'p2c', row[:], 2, row[:])
    c.g.wait_ge(c.sem('p2c_g'), ng)
    r = c.g.dma_start(out=scalA.ap(), in_=row[:])
    c.tick(r, 'p2c_g', 16)
    c.g.wait_ge(c.sem('p2c_g'), c.n('p2c_g'))


def _ffn_scales(c, pre, upto):
    sc = c.tile(pre + '_sc', [1, 16], F32)
    qm = c.tile('p0_qm', [1, 1], F32)
    rqm = c.tile('p0_rqm', [1, 1], F32)
    out = {}
    scalA = c.dram('scalA', [1, 8], F32)
    r = c.sy.dma_start(out=sc[0:1, 0:8], in_=scalA.ap()); c.tick(r, pre + '_sld', 16)
    qmc = c.const('qmax_32767', np.full((1, 1), 32767.0, np.float32))
    r = c.sy.dma_start(out=qm[:], in_=qmc.ap()); c.tick(r, pre + '_sld', 16)
    onrt = c.tile('w0_onr', [1, 128], F32)
    r = c.sy.dma_start(out=onrt[:], in_=ones_row(c).ap()); c.tick(r, pre + '_sld', 16)
    if upto >= 2:
        scalB = c.dram('scalB', [1, 8], F32)
        r = c.sy.dma_start(out=sc[0:1, 8:10], in_=scalB.ap()[0:1, 0:2]); c.tick(r, pre + '_sld', 16)
    if upto >= 3:
        scalC = c.dram('scalC', [1, 8], F32)
        r = c.sy.dma_start(out=sc[0:1, 10:12], in_=scalC.ap()[0:1, 0:2]); c.tick(r, pre + '_sld', 16)
    c.v.wait_ge(c.sem(pre + '_sld'), c.n(pre + '_sld'))
    r = c.v.reciprocal(out=rqm[:], in_=qm[:]); nv = c.tick(r, pre + '_v')
    c.v.wait_ge(c.sem(pre + '_v'), nv)
    W_ = sc
    def vop(fn):
        nonlocal nv
        r = fn()
        nv = c.tick(r, pre + '_v')
        c.v.wait_ge(c.sem(pre + '_v'), nv)
    vop(lambda: c.v.tensor_scalar(out=W_[0:1, 12:13], in0=W_[0:1, 0:1], scalar1=rqm[0:1, 0:1],
                                  scalar2=1e-12, op0=OP.mult, op1=OP.max))
    vop(lambda: c.v.reciprocal(out=W_[0:1, 13:14], in_=W_[0:1, 12:13]))
    vop(lambda: c.v.tensor_scalar(out=W_[0:1, 14:15], in0=W_[0:1, 1:2], scalar1=rqm[0:1, 0:1],
                                  scalar2=1e-12, op0=OP.mult, op1=OP.max))
    vop(lambda: c.v.reciprocal(out=W_[0:1, 15:16], in_=W_[0:1, 14:15]))
    tiles = {}
    def bc(name, src_ap):
        t = c.tile(pre + '_' + name, [128, 1], F32)
        c.t.wait_ge(c.sem(pre + '_v'), c.n(pre + '_v'))
        _bcast_cols(c, pre + 'b', src_ap, 1, t[:])
        tiles[name] = t
    if upto == 1:
        bc('rs2b', W_[0:1, 13:14])
        bc('rs3b', W_[0:1, 15:16])
        bc('s3b', W_[0:1, 14:15])
        return tiles
    vop(lambda: c.v.tensor_scalar(out=W_[0:1, 2:3], in0=W_[0:1, 14:15], scalar1=1.0 / 65536.0,
                                  scalar2=None, op0=OP.mult))
    vop(lambda: c.v.tensor_tensor(out=W_[0:1, 3:4], in0=W_[0:1, 8:9], in1=W_[0:1, 2:3], op=OP.mult))
    vop(lambda: c.v.tensor_scalar(out=W_[0:1, 4:5], in0=W_[0:1, 3:4], scalar1=rqm[0:1, 0:1],
                                  scalar2=1e-12, op0=OP.mult, op1=OP.max))
    vop(lambda: c.v.reciprocal(out=W_[0:1, 5:6], in_=W_[0:1, 4:5]))
    vop(lambda: c.v.tensor_tensor(out=W_[0:1, 6:7], in0=W_[0:1, 2:3], in1=W_[0:1, 5:6], op=OP.mult))
    if upto == 2:
        bc('comb4b', W_[0:1, 6:7])
        return tiles
    vop(lambda: c.v.tensor_tensor(out=W_[0:1, 7:8], in0=W_[0:1, 4:5], in1=W_[0:1, 12:13], op=OP.mult))
    vop(lambda: c.v.tensor_tensor(out=W_[0:1, 3:4], in0=W_[0:1, 10:11], in1=W_[0:1, 7:8], op=OP.mult))
    vop(lambda: c.v.tensor_scalar(out=W_[0:1, 9:10], in0=W_[0:1, 3:4], scalar1=rqm[0:1, 0:1],
                                  scalar2=1e-12, op0=OP.mult, op1=OP.max))
    vop(lambda: c.v.reciprocal(out=W_[0:1, 11:12], in_=W_[0:1, 9:10]))
    vop(lambda: c.v.tensor_tensor(out=W_[0:1, 2:3], in0=W_[0:1, 7:8], in1=W_[0:1, 11:12], op=OP.mult))
    bc('comb5b', W_[0:1, 2:3])
    bc('s5b', W_[0:1, 9:10])
    return tiles


def _qdq_int_chain(c, pre, src_ap, rs_ap, f_t, f_i32, f_c, f_m, out_ap, width,
                   clip_lo, clip_hi, u_has_half=False):
    vs, as_, ps = pre + '_v', pre + '_a', pre + '_p'
    if not u_has_half:
        r = c.v.tensor_scalar(out=f_t, in0=src_ap, scalar1=rs_ap, scalar2=0.5,
                              op0=OP.mult, op1=OP.add)
        nv = c.tick(r, vs)
        u_ap = f_t
    else:
        u_ap = src_ap
        nv = c.n(vs)
    c.s.wait_ge(c.sem(vs), nv)
    r = c.s.activation(out=f_i32, in_=u_ap, func=AF.Identity)
    na = c.tick(r, as_)
    c.g.wait_ge(c.sem(as_), na)
    r = c.g.tensor_copy(out=f_c, in_=f_i32)
    npl = c.tick(r, ps)
    c.v.wait_ge(c.sem(ps), npl)
    r = c.v.tensor_tensor(out=f_m, in0=f_c, in1=u_ap, op=OP.is_gt)
    nv = c.tick(r, vs)
    c.v.wait_ge(c.sem(vs), nv)
    r = c.v.scalar_tensor_tensor(out=f_m, in0=f_m, scalar=-1.0, in1=f_c,
                                 op0=OP.mult, op1=OP.add)
    nv = c.tick(r, vs)
    c.v.wait_ge(c.sem(vs), nv)
    r = c.v.tensor_scalar(out=out_ap, in0=f_m, scalar1=float(clip_lo), scalar2=float(clip_hi),
                          op0=OP.max, op1=OP.min)
    return c.tick(r, vs)


def phase_p3(c):
    seq = c.seq
    ncc = seq // 512
    NT = FF // 128
    xq_fm = c.dram('xq_fm', [D, seq], BF16)
    G1Q = c.dram('G1Q', [FF, seq], I16)
    Q2 = c.dram('Q2', [FF, seq], U16)
    UP1Q = c.dram('UP1Q', [FF, seq], I16)
    scalB = c.dram('scalB', [1, 8], F32)
    wu_d = c.dram('wT_up', [D, FF], BF16)
    wg_d = c.dram('wT_gate', [D, FF], BF16)
    wup = c.tile('B0', [128, 32768], BF16)
    wgt = tile_b1(c)
    xqs = c.tile('B2', [128, 4096], BF16)
    sc0 = c.base_tile('SC0')
    ut = (sc0[:, 0:1024], sc0[:, 1024:2048])
    mt = (sc0[:, 2048:3072], sc0[:, 3072:4096])
    g1 = (sc0[:, 4096:5120], sc0[:, 5120:6144])
    q2s = (sc0[:, 6144:7168], sc0[:, 7168:8192])
    sc1 = c.base_tile('SC1')
    ct_ = (sc1[:, 0:1024], sc1[:, 1024:2048])
    ia = (sc1[:, 2048:3072].bitcast(I32), sc1[:, 3072:4096].bitcast(I32))
    sg = (sc1[:, 4096:5120], sc1[:, 5120:6144])
    rc2 = c.tile('p4_rc', [128, 2], F32)
    am = c.tile('p3_am', [128, 1], F32)
    row = c.tile('p2_row', [1, 8], F32)
    pA = c.psum('pA', [128, 2048], F32)
    pB = c.psum('pB', [128, 1024], F32)
    pC = c.psum('pC', [128, 1024], F32)
    sc = _ffn_scales(c, 'p3s', 1)
    c.t.wait_ge(c.sem('p3sb_v'), c.n('p3sb_v'))
    r = c.v.memset(am[:], 0.0); c.tick(r, 'p3_v')

    r = c.sy.dma_start(out=wup[:, 0:8 * FF].rearrange('p (a o) -> p a o', a=8),
                       in_=wu_d.ap().rearrange('(a p) o -> p a o', p=128))
    c.tick(r, 'p3_w', 16)
    r = c.sy.dma_start(out=wgt[:, 0:8 * FF].rearrange('p (a o) -> p a o', a=8),
                       in_=wg_d.ap().rearrange('(a p) o -> p a o', p=128))
    nw = c.tick(r, 'p3_w', 16)

    G1Q_r = G1Q.ap().rearrange('(T p) s -> p T s', p=128)
    Q2_r = Q2.ap().rearrange('(T p) s -> p T s', p=128)
    UP1Q_r = UP1Q.ap().rearrange('(T p) s -> p T s', p=128)

    H = dict(ngmm={}, numm={}, ncastA={}, nuA={}, ncA={}, nsttA={}, nsig={},
             ncastB={}, nuB={}, ncB={}, nmin={}, ncastC={}, nuC={}, ncC={},
             nsttC={}, nam={}, nsttB={})

    def emit_pair(cc, k0):
        bb = (cc * (NT // 2) + k0, cc * (NT // 2) + k0 + 1)
        cs = slice(cc * 512, (cc + 1) * 512)
        for b in bb:
            s = b % 2
            kk = (b % (NT // 2)) * 2
            if b >= 2:
                c.t.wait_ge(c.sem('p3_a'), H['nuA'][b - 2])
            for tq in range(2):
                for a in range(8):
                    r = c.t.matmul(out=pA[:, s * 1024 + tq * 512: s * 1024 + tq * 512 + 512],
                                   lhsT=wgt[:, a * FF + (kk + tq) * 128: a * FF + (kk + tq) * 128 + 128],
                                   rhs=xqs[:, a * 512:(a + 1) * 512],
                                   start=(a == 0), stop=(a == 7))
                    H['ngmm'][b] = c.tick(r, 'p3_t')
        for b in bb:
            s = b % 2
            kk = (b % (NT // 2)) * 2
            po = pB if s == 0 else pC
            if b >= 2:
                c.t.wait_ge(c.sem('p3_a'), H['nuC'][b - 2])
            for tq in range(2):
                for a in range(8):
                    r = c.t.matmul(out=po[:, tq * 512: tq * 512 + 512],
                                   lhsT=wup[:, a * FF + (kk + tq) * 128: a * FF + (kk + tq) * 128 + 128],
                                   rhs=xqs[:, a * 512:(a + 1) * 512],
                                   start=(a == 0), stop=(a == 7))
                    H['numm'][b] = c.tick(r, 'p3_t')
        for b in bb:
            s = b % 2
            c.s.wait_ge(c.sem('p3_t'), H['ngmm'][b])
            if b >= 2:
                c.s.wait_ge(c.sem('p3_a'), H['ncC'][b - 2])
                c.s.wait_ge(c.sem('p3_v'), H['nsttC'][b - 2])
            r = c.s.activation(out=ia[s], in_=pA[:, s * 1024:s * 1024 + 1024],
                               func=AF.Identity, scale=sc['rs3b'][:, 0:1], bias=c.kbias(0.5))
            H['ncastA'][b] = c.tick(r, 'p3_a')
            r = c.s.activation(out=ut[s], in_=pA[:, s * 1024:s * 1024 + 1024],
                               func=AF.Identity, scale=sc['rs3b'][:, 0:1], bias=c.kbias(0.5))
            H['nuA'][b] = c.tick(r, 'p3_a')
        for b in bb:
            s = b % 2
            c.s.wait_ge(c.sem('p3_a'), H['ncastA'][b])
            if b >= 2:
                c.s.wait_ge(c.sem('p3_v'), H['nsttC'][b - 2])
            r = c.s.activation(out=ct_[s], in_=ia[s], func=AF.Identity)
            H['ncA'][b] = c.tick(r, 'p3_a')
        for b in bb:
            s = b % 2
            c.v.wait_ge(c.sem('p3_a'), H['ncA'][b])
            c.v.wait_ge(c.sem('p3_a'), H['nuA'][b])
            c.v.wait_ge(c.sem('p3_st'), c.n('p3_st'))
            if b >= 2:
                c.v.wait_ge(c.sem('p3_a'), H['nsig'][b - 2])
            r = c.v.tensor_tensor(out=mt[s], in0=ct_[s], in1=ut[s], op=OP.is_gt)
            nv = c.tick(r, 'p3_v')
            c.v.wait_ge(c.sem('p3_v'), nv)
            r = c.v.scalar_tensor_tensor(out=g1[s], in0=mt[s], scalar=-1.0, in1=ct_[s],
                                         op0=OP.mult, op1=OP.add)
            H['nsttA'][b] = c.tick(r, 'p3_v')
        for b in bb:
            s = b % 2
            kk = (b % (NT // 2)) * 2
            c.g.wait_ge(c.sem('p3_v'), H['nsttA'][b])
            r = c.g.dma_start(out=G1Q_r[:, kk:kk + 2, cs],
                              in_=g1[s].rearrange('p (T s) -> p T s', T=2))
            c.tick(r, 'p3_st', 16)
        for b in bb:
            s = b % 2
            c.s.wait_ge(c.sem('p3_v'), H['nsttA'][b])
            r = c.s.activation(out=sg[s], in_=g1[s], func=AF.Sigmoid, scale=sc['s3b'][:, 0:1])
            H['nsig'][b] = c.tick(r, 'p3_a')
        for b in bb:
            s = b % 2
            c.s.wait_ge(c.sem('p3_a'), H['nsig'][b])
            c.s.wait_ge(c.sem('p3_a'), H['ncA'][b])
            r = c.s.activation(out=ia[s], in_=sg[s], func=AF.Identity,
                               scale=65536.0, bias=c.kbias(0.5))
            H['ncastB'][b] = c.tick(r, 'p3_a')
            c.s.wait_ge(c.sem('p3_v'), H['nsttA'][b])
            r = c.s.activation(out=ut[s], in_=sg[s], func=AF.Identity,
                               scale=65536.0, bias=c.kbias(0.5))
            H['nuB'][b] = c.tick(r, 'p3_a')
        for b in bb:
            s = b % 2
            c.s.wait_ge(c.sem('p3_a'), H['ncastB'][b])
            c.s.wait_ge(c.sem('p3_v'), H['nsttA'][b])
            r = c.s.activation(out=ct_[s], in_=ia[s], func=AF.Identity)
            H['ncB'][b] = c.tick(r, 'p3_a')
        for b in bb:
            s = b % 2
            c.v.wait_ge(c.sem('p3_a'), H['ncB'][b])
            c.v.wait_ge(c.sem('p3_a'), H['nuB'][b])
            r = c.v.tensor_tensor(out=mt[s], in0=ct_[s], in1=ut[s], op=OP.is_gt)
            nv = c.tick(r, 'p3_v')
            c.v.wait_ge(c.sem('p3_v'), nv)
            r = c.v.scalar_tensor_tensor(out=q2s[s], in0=mt[s], scalar=-1.0, in1=ct_[s],
                                         op0=OP.mult, op1=OP.add)
            H['nsttB'][b] = nv = c.tick(r, 'p3_v')
            c.v.wait_ge(c.sem('p3_v'), nv)
            r = c.v.tensor_scalar(out=q2s[s], in0=q2s[s], scalar1=65535.0, scalar2=None,
                                  op0=OP.min)
            H['nmin'][b] = c.tick(r, 'p3_v')
        for b in bb:
            s = b % 2
            kk = (b % (NT // 2)) * 2
            c.g.wait_ge(c.sem('p3_v'), H['nmin'][b])
            r = c.g.dma_start(out=Q2_r[:, kk:kk + 2, cs],
                              in_=q2s[s].rearrange('p (T s) -> p T s', T=2))
            c.tick(r, 'p3_st', 16)
        for b in bb:
            s = b % 2
            r = c.v.tensor_tensor(out=mt[s], in0=g1[s], in1=q2s[s], op=OP.mult)
            nv = c.tick(r, 'p3_v')
            c.v.wait_ge(c.sem('p3_v'), nv)
            r = c.v.reduce_max(out=rc2[:, s:s + 1], in_=mt[s], axis=AX.X,
                               apply_absolute_value=True)
            nv = c.tick(r, 'p3_v')
            c.v.wait_ge(c.sem('p3_v'), nv)
            r = c.v.tensor_tensor(out=am[:], in0=am[:], in1=rc2[:, s:s + 1], op=OP.max)
            H['nam'][b] = c.tick(r, 'p3_v')
        for b in bb:
            s = b % 2
            po = pB if s == 0 else pC
            c.s.wait_ge(c.sem('p3_t'), H['numm'][b])
            c.s.wait_ge(c.sem('p3_a'), H['ncB'][b])
            r = c.s.activation(out=ia[s], in_=po[:, 0:1024],
                               func=AF.Identity, scale=sc['rs2b'][:, 0:1], bias=c.kbias(0.5))
            H['ncastC'][b] = c.tick(r, 'p3_a')
            c.s.wait_ge(c.sem('p3_v'), H['nmin'][b])
            r = c.s.activation(out=ut[s], in_=po[:, 0:1024],
                               func=AF.Identity, scale=sc['rs2b'][:, 0:1], bias=c.kbias(0.5))
            H['nuC'][b] = c.tick(r, 'p3_a')
        for b in bb:
            s = b % 2
            c.s.wait_ge(c.sem('p3_a'), H['ncastC'][b])
            c.s.wait_ge(c.sem('p3_v'), H['nmin'][b])
            r = c.s.activation(out=ct_[s], in_=ia[s], func=AF.Identity)
            H['ncC'][b] = c.tick(r, 'p3_a')
        for b in bb:
            s = b % 2
            c.v.wait_ge(c.sem('p3_a'), H['ncC'][b])
            c.v.wait_ge(c.sem('p3_a'), H['nuC'][b])
            r = c.v.tensor_tensor(out=mt[s], in0=ct_[s], in1=ut[s], op=OP.is_gt)
            nv = c.tick(r, 'p3_v')
            c.v.wait_ge(c.sem('p3_v'), nv)
            r = c.v.scalar_tensor_tensor(out=mt[s], in0=mt[s], scalar=-1.0, in1=ct_[s],
                                         op0=OP.mult, op1=OP.add)
            H['nsttC'][b] = c.tick(r, 'p3_v')
        for b in bb:
            s = b % 2
            kk = (b % (NT // 2)) * 2
            c.g.wait_ge(c.sem('p3_v'), H['nsttC'][b])
            r = c.g.dma_start(out=UP1Q_r[:, kk:kk + 2, cs],
                              in_=mt[s].rearrange('p (T s) -> p T s', T=2))
            c.tick(r, 'p3_st', 16)

    for cc in range(ncc):
        c.sy.wait_ge(c.sem('p3_t'), c.n('p3_t'))
        r = c.sy.dma_start(out=xqs[:].rearrange('p (a s) -> p a s', a=8),
                           in_=xq_fm.ap().rearrange('(a p) s -> p a s', p=128)[:, :, cc * 512:(cc + 1) * 512])
        nld = c.tick(r, 'p3_ld', 16)
        c.t.wait_ge(c.sem('p3_ld'), nld)
        if cc == 0:
            c.t.wait_ge(c.sem('p3_w'), nw)
        for k in range(NT // 4):
            emit_pair(cc, 2 * k)
    c.t.wait_ge(c.sem('p3_v'), c.n('p3_v'))
    _xpart_max(c, 'p3x', am[:], row[0:1, 0:1])
    c.g.wait_ge(c.sem('p3_st'), c.n('p3_st'))
    c.g.wait_ge(c.sem('p3x_v'), c.n('p3x_v'))
    ng = _coll_max(c, 'p3c', row[:], 1, row[:])
    c.g.wait_ge(c.sem('p3c_g'), ng)
    r = c.g.dma_start(out=scalB.ap(), in_=row[:])
    c.tick(r, 'p3c_g', 16)
    c.g.wait_ge(c.sem('p3c_g'), c.n('p3c_g'))


def phase_p4a(c):
    seq = c.seq
    W2 = min(1024, seq)
    ncol = seq // W2
    NT = FF // 128
    G1Q = c.dram('G1Q', [FF, seq], I16)
    Q2 = c.dram('Q2', [FF, seq], U16)
    UP1Q = c.dram('UP1Q', [FF, seq], I16)
    G2Q = c.dram('G2Q', [FF, seq], I16)
    scalC = c.dram('scalC', [1, 8], F32)
    sc0 = c.base_tile('SC0')
    fg = (sc0[:, 0:W2], sc0[:, 1024:1024 + W2])
    fq = (sc0[:, 2048:2048 + W2], sc0[:, 3072:3072 + W2])
    fu = (sc0[:, 4096:4096 + W2], sc0[:, 5120:5120 + W2])
    ut = (sc0[:, 6144:6144 + W2], sc0[:, 7168:7168 + W2])
    sc1 = c.base_tile('SC1')
    ct_ = (sc1[:, 0:W2], sc1[:, 1024:1024 + W2])
    ia = (sc1[:, 2048:2048 + W2].bitcast(I32), sc1[:, 3072:3072 + W2].bitcast(I32))
    mt = (sc1[:, 4096:4096 + W2], sc1[:, 5120:5120 + W2])
    rc = c.tile('p4_rc', [128, 2], F32)
    am = c.tile('p3_am', [128, 1], F32)
    row = c.tile('p2_row', [1, 8], F32)
    sc = _ffn_scales(c, 'p4s', 2)
    r = c.v.memset(am[:], 0.0); c.tick(r, 'p4_v')

    H = dict(nld={}, nm4={}, ncast={}, nu={}, nc={}, nstt={}, nm5={}, nrm={})

    def emit_pair(b0):
        for b in (b0, b0 + 1):
            s = b % 2
            t = b // ncol
            cs = slice((b % ncol) * W2, (b % ncol + 1) * W2)
            if b >= 2:
                c.g.wait_ge(c.sem('p4_v'), H['nrm'][b - 2])
                c.g.wait_ge(c.sem('p4_st'), c.n('p4_st'))
            r = c.g.dma_start(out=fg[s], in_=G1Q.ap()[t * 128:(t + 1) * 128, cs])
            c.tick(r, 'p4_l%d' % s, 16)
            r = c.g.dma_start(out=fq[s], in_=Q2.ap()[t * 128:(t + 1) * 128, cs])
            c.tick(r, 'p4_l%d' % s, 16)
            r = c.g.dma_start(out=fu[s], in_=UP1Q.ap()[t * 128:(t + 1) * 128, cs])
            H['nld'][b] = c.tick(r, 'p4_l%d' % s, 16)
        for b in (b0, b0 + 1):
            s = b % 2
            c.v.wait_ge(c.sem('p4_l%d' % s), H['nld'][b])
            r = c.v.tensor_tensor(out=fg[s], in0=fg[s], in1=fq[s], op=OP.mult)
            H['nm4'][b] = c.tick(r, 'p4_v')
        for b in (b0, b0 + 1):
            s = b % 2
            c.s.wait_ge(c.sem('p4_v'), H['nm4'][b])
            if b >= 2:
                pass
                c.s.wait_ge(c.sem('p4_v'), H['nstt'][b - 2])
            r = c.s.activation(out=ia[s], in_=fg[s], func=AF.Identity,
                               scale=sc['comb4b'][:, 0:1], bias=c.kbias(0.5))
            H['ncast'][b] = c.tick(r, 'p4_a')
            r = c.s.activation(out=ut[s], in_=fg[s], func=AF.Identity,
                               scale=sc['comb4b'][:, 0:1], bias=c.kbias(0.5))
            H['nu'][b] = c.tick(r, 'p4_a')
        for b in (b0, b0 + 1):
            s = b % 2
            c.s.wait_ge(c.sem('p4_a'), H['nu'][b])
            if b >= 2:
                c.s.wait_ge(c.sem('p4_v'), H['nstt'][b - 2])
            r = c.s.activation(out=ct_[s], in_=ia[s], func=AF.Identity)
            H['nc'][b] = c.tick(r, 'p4_a')
        for b in (b0, b0 + 1):
            s = b % 2
            t = b // ncol
            cs = slice((b % ncol) * W2, (b % ncol + 1) * W2)
            c.v.wait_ge(c.sem('p4_a'), H['nc'][b])
            c.v.wait_ge(c.sem('p4_a'), H['nu'][b])
            c.v.wait_ge(c.sem('p4_st'), c.n('p4_st'))
            r = c.v.tensor_tensor(out=mt[s], in0=ct_[s], in1=ut[s], op=OP.is_gt)
            nv = c.tick(r, 'p4_v')
            c.v.wait_ge(c.sem('p4_v'), nv)
            r = c.v.scalar_tensor_tensor(out=fq[s], in0=mt[s], scalar=-1.0, in1=ct_[s],
                                         op0=OP.mult, op1=OP.add)
            H['nstt'][b] = nv = c.tick(r, 'p4_v')
            c.g.wait_ge(c.sem('p4_v'), nv)
            r = c.g.dma_start(out=G2Q.ap()[t * 128:(t + 1) * 128, cs], in_=fq[s])
            c.tick(r, 'p4_st', 16)
            c.v.wait_ge(c.sem('p4_v'), nv)
            r = c.v.tensor_tensor(out=fu[s], in0=fq[s], in1=fu[s], op=OP.mult)
            H['nm5'][b] = nv = c.tick(r, 'p4_v')
            c.v.wait_ge(c.sem('p4_v'), nv)
            r = c.v.reduce_max(out=rc[:, s:s + 1], in_=fu[s], axis=AX.X,
                               apply_absolute_value=True)
            nv = c.tick(r, 'p4_v')
            c.v.wait_ge(c.sem('p4_v'), nv)
            r = c.v.tensor_tensor(out=am[:], in0=am[:], in1=rc[:, s:s + 1], op=OP.max)
            H['nrm'][b] = c.tick(r, 'p4_v')

    nblk = NT * ncol
    for k in range(nblk // 2):
        emit_pair(2 * k)
    c.t.wait_ge(c.sem('p4_v'), c.n('p4_v'))
    _xpart_max(c, 'p4x', am[:], row[0:1, 0:1])
    c.g.wait_ge(c.sem('p4_st'), c.n('p4_st'))
    c.g.wait_ge(c.sem('p4x_v'), c.n('p4x_v'))
    ng = _coll_max(c, 'p4c', row[:], 1, row[:])
    c.g.wait_ge(c.sem('p4c_g'), ng)
    r = c.g.dma_start(out=scalC.ap(), in_=row[:])
    c.tick(r, 'p4c_g', 16)
    c.g.wait_ge(c.sem('p4c_g'), c.n('p4c_g'))


def phase_p4b(c):
    seq = c.seq
    ncc = seq // 512
    NT = FF // 128
    G2Q = c.dram('G2Q', [FF, seq], I16)
    UP1Q = c.dram('UP1Q', [FF, seq], I16)
    x1 = c.dram('x1', [seq, D], F32)
    x2 = c.dram('x2', [seq, D], F32)
    wdn = c.dram('wT_down', [FF, D], BF16)
    oin = c.tile('B0', [128, 32768], BF16)
    wdt = tile_b1(c)
    sc0 = c.base_tile('SC0')
    fg = (sc0[:, 0:1024], sc0[:, 1024:2048])
    fu = (sc0[:, 2048:3072], sc0[:, 3072:4096])
    ut = (sc0[:, 4096:5120], sc0[:, 5120:6144])
    q5 = (sc0[:, 6144:7168], sc0[:, 7168:8192])
    sc1 = c.base_tile('SC1')
    ct_ = (sc1[:, 0:1024], sc1[:, 1024:2048])
    ia = (sc1[:, 2048:3072].bitcast(I32), sc1[:, 3072:4096].bitcast(I32))
    mt = (sc1[:, 4096:5120], sc1[:, 5120:6144])
    b2f = c.tile('B2', [128, 4096], BF16)
    xres = (b2f[:, 0:2048].bitcast(F32), b2f[:, 2048:4096].bitcast(F32))
    pA = c.psum('pA', [128, 2048], F32)
    sc = _ffn_scales(c, 'p4t', 3)
    r = c.sy.dma_start(out=wdt[:, 0:24 * 1024].rearrange('p (a o) -> p a o', a=24),
                       in_=wdn.ap().rearrange('(a p) o -> p a o', p=128))
    nwd = c.tick(r, 'p4b_w', 16)
    G2Q_r = G2Q.ap().rearrange('(T p) s -> p T s', p=128)
    UP1Q_r = UP1Q.ap().rearrange('(T p) s -> p T s', p=128)

    H = dict(nld={}, nm={}, ncast={}, nu={}, nc={}, nstt={}, nsc={})
    nmm_cc = {}
    nx_cc = {}

    def chain_pair(cc, k0):
        bb = (cc * (NT // 2) + k0, cc * (NT // 2) + k0 + 1)
        cs = slice(cc * 512, (cc + 1) * 512)
        for b in bb:
            s = b % 2
            kk = (b % (NT // 2)) * 2
            if b >= 2:
                c.g.wait_ge(c.sem('p4b_v'), H['nm'][b - 2])
                c.g.wait_ge(c.sem('p4b_v'), H['nsc'][b - 2])
            r = c.g.dma_start(out=fg[s].rearrange('p (T s) -> p T s', T=2),
                              in_=G2Q_r[:, kk:kk + 2, cs])
            c.tick(r, 'p4b_l%d' % s, 16)
            r = c.g.dma_start(out=fu[s].rearrange('p (T s) -> p T s', T=2),
                              in_=UP1Q_r[:, kk:kk + 2, cs])
            H['nld'][b] = c.tick(r, 'p4b_l%d' % s, 16)
        for b in bb:
            s = b % 2
            c.v.wait_ge(c.sem('p4b_l%d' % s), H['nld'][b])
            r = c.v.tensor_tensor(out=fg[s], in0=fg[s], in1=fu[s], op=OP.mult)
            H['nm'][b] = c.tick(r, 'p4b_v')
        for b in bb:
            s = b % 2
            c.s.wait_ge(c.sem('p4b_v'), H['nm'][b])
            if b >= 2:
                pass
                c.s.wait_ge(c.sem('p4b_v'), H['nstt'][b - 2])
            r = c.s.activation(out=ia[s], in_=fg[s], func=AF.Identity,
                               scale=sc['comb5b'][:, 0:1], bias=c.kbias(0.5))
            H['ncast'][b] = c.tick(r, 'p4b_a')
            r = c.s.activation(out=ut[s], in_=fg[s], func=AF.Identity,
                               scale=sc['comb5b'][:, 0:1], bias=c.kbias(0.5))
            H['nu'][b] = c.tick(r, 'p4b_a')
        for b in bb:
            s = b % 2
            c.s.wait_ge(c.sem('p4b_a'), H['nu'][b])
            if b >= 2:
                c.s.wait_ge(c.sem('p4b_v'), H['nstt'][b - 2])
            r = c.s.activation(out=ct_[s], in_=ia[s], func=AF.Identity)
            H['nc'][b] = c.tick(r, 'p4b_a')
        for b in bb:
            s = b % 2
            kk = (b % (NT // 2)) * 2
            c.v.wait_ge(c.sem('p4b_a'), H['nc'][b])
            c.v.wait_ge(c.sem('p4b_a'), H['nu'][b])
            if b >= 2:
                c.v.wait_ge(c.sem('p4b_v'), H['nsc'][b - 2])
            r = c.v.tensor_tensor(out=mt[s], in0=ct_[s], in1=ut[s], op=OP.is_gt)
            nv = c.tick(r, 'p4b_v')
            c.v.wait_ge(c.sem('p4b_v'), nv)
            r = c.v.scalar_tensor_tensor(out=q5[s], in0=mt[s], scalar=-1.0, in1=ct_[s],
                                         op0=OP.mult, op1=OP.add)
            H['nstt'][b] = nv = c.tick(r, 'p4b_v')
            c.v.wait_ge(c.sem('p4b_v'), nv)
            if cc >= 1 and kk == 0:
                c.v.wait_ge(c.sem('p4b_t'), nmm_cc[cc - 1])
            r = c.v.tensor_scalar(out=oin[:, kk * 512:(kk + 2) * 512], in0=q5[s],
                                  scalar1=sc['s5b'][:, 0:1], scalar2=None, op0=OP.mult)
            H['nsc'][b] = c.tick(r, 'p4b_v')

    def gemm(cc):
        c.t.wait_ge(c.sem('p4b_w'), nwd)
        c.t.wait_ge(c.sem('p4b_v'), H['nsc'][cc * (NT // 2) + NT // 2 - 1])
        for sub in range(4):
            bank = sub % 2
            i_tile = cc * 4 + sub
            if i_tile >= 2:
                c.t.wait_ge(c.sem('p4b_x'), nx_cc[i_tile - 2])
            for a in range(24):
                r = c.t.matmul(out=pA[:, bank * 1024:bank * 1024 + 512],
                               lhsT=oin[:, a * 512 + sub * 128: a * 512 + sub * 128 + 128],
                               rhs=wdt[:, a * 1024: a * 1024 + 512],
                               start=(a == 0), stop=(a == 23))
                c.tick(r, 'p4b_t')
                r = c.t.matmul(out=pA[:, bank * 1024 + 512:bank * 1024 + 1024],
                               lhsT=oin[:, a * 512 + sub * 128: a * 512 + sub * 128 + 128],
                               rhs=wdt[:, a * 1024 + 512: a * 1024 + 1024],
                               start=(a == 0), stop=(a == 23))
                nmm = c.tick(r, 'p4b_t')
            par = i_tile % 2
            if i_tile >= 1:
                c.sy.wait_ge(c.sem('p4b_x'), c.n('p4b_x'))
            if i_tile >= 2:
                c.sy.wait_ge(c.sem('p4b_g%d' % par), 16 * (i_tile // 2))
            r = c.sy.dma_start(out=xres[par], in_=x1.ap().rearrange('(t p) d -> t p d', p=128)[i_tile])
            nldx = c.tick(r, 'p4b_lx', 16)
            c.v.wait_ge(c.sem('p4b_lx'), nldx)
            c.v.wait_ge(c.sem('p4b_t'), nmm)
            r = c.v.tensor_tensor(out=xres[par], in0=xres[par],
                                  in1=pA[:, bank * 1024:bank * 1024 + 1024], op=OP.add)
            nx_cc[i_tile] = nx = c.tick(r, 'p4b_x')
            c.g.wait_ge(c.sem('p4b_x'), nx)
            r = c.g.dma_start(out=x2.ap().rearrange('(t p) d -> t p d', p=128)[i_tile],
                              in_=xres[par])
            c.tick(r, 'p4b_g%d' % par, 16)
        nmm_cc[cc] = nmm

    for cc in range(ncc):
        for k in range(NT // 4):
            chain_pair(cc, 2 * k)
        gemm(cc)
    c.g.wait_ge(c.sem('p4b_g0'), c.n('p4b_g0'))
    c.g.wait_ge(c.sem('p4b_g1'), c.n('p4b_g1'))



def phase_p5(c):
    x2 = c.dram('x2', [c.seq, D], F32)
    xn_tok = c.dram('xn_tok_l1', [c.seq, D], F32)
    xn_fm = c.dram('xn_fm_l1', [D, c.seq], BF16)
    layernorm_tiles(c, 'p5', x2, xn_tok, xn_fm, amax_tile=None)



def phase_p7(c):
    seq = c.seq
    ntile = c.ntile
    x2 = c.dram('x2', [seq, D], F32)
    x3 = c.dram('x3', [seq, D], F32)
    xn_fm = c.dram('xn_fm_l1', [D, seq], BF16)
    wq_d = c.dram('wT_l1q', [D, D], BF16)
    wk_d = c.dram('wT_l1k', [D, D], BF16)
    wv_d = c.dram('wT_l1v', [D, D], BF16)
    wfc_d = c.dram('wT_l1fc', [D, D], BF16)

    wall = c.tile('B0', [128, 32768], BF16)
    b1 = tile_b1(c)
    Q4 = b1[:, 0:4096]
    K4 = b1[:, 4096:8192]
    abf = (b1[:, 12288:13312], b1[:, 13312:14336])
    afm = b1[:, 14336:15360]
    xqs = c.tile('B2', [128, 4096], BF16)
    sc0 = c.base_tile('SC0')
    Pt = sc0[:, 0:2048]
    OUT = sc0[:, 2048:3072]
    OUTB = sc0[:, 3072:4096]
    S = sc0[:, 6144:6400]
    MX = sc0[:, 6400:6416]
    SM = sc0[:, 6416:6432]
    RC = sc0[:, 6432:6448]
    xres = (sc0[:, 4096:5120], sc0[:, 5120:6144])
    sc1 = c.base_tile('SC1')
    V4 = sc1[:, 0:4096]
    TMP = sc1[:, 4096:5120]
    TMPP = sc1[:, 5120:6144]
    idb = c.tile('p7_idb', [128, 128], BF16)
    pA = c.psum('pA', [128, 2048], F32)
    pB = c.psum('pB', [128, 1024], F32)
    pC = c.psum('pC', [128, 1024], F32)
    NPV = 11

    for wi, wd in enumerate((wq_d, wk_d, wv_d, wfc_d)):
        r = c.sy.dma_start(out=wall[:, wi * 8192:(wi + 1) * 8192].rearrange('p (a o) -> p a o', a=8),
                           in_=wd.ap().rearrange('(a p) o -> p a o', p=128))
        nwl = c.tick(r, 'p7_wl', 16)
    r = c.sy.dma_start(out=idb[:], in_=ident_bf16(c).ap())
    nwl = c.tick(r, 'p7_wl', 16)

    def proj(i):
        g, j = divmod(i, 4)
        if j == 0:
            c.sy.wait_ge(c.sem('p7_t'), c.n('p7_t'))
            r = c.sy.dma_start(out=xqs[:].rearrange('p (a s) -> p a s', a=8),
                               in_=xn_fm.ap().rearrange('(a p) s -> p a s', p=128)[:, :, g * 512:(g + 1) * 512])
            nld = c.tick(r, 'p7_ld', 16)
            c.t.wait_ge(c.sem('p7_ld'), nld)
            if i == 0:
                c.t.wait_ge(c.sem('p7_wl'), nwl)
        outs = ((pA[:, 0:1024], 0, Q4), (pA[:, 1024:2048], 8192, K4), (pB[:, 0:1024], 16384, None))
        c.t.wait_ge(c.sem('p7_ev'), c.n('p7_ev'))
        for po, woff, dst4 in outs:
            for half in range(2):
                for a in range(8):
                    r = c.t.matmul(out=po[:, half * 512:(half + 1) * 512],
                                   lhsT=xqs[:, a * 512 + j * 128: a * 512 + j * 128 + 128],
                                   rhs=wall[:, woff + a * 1024 + half * 512: woff + a * 1024 + half * 512 + 512],
                                   start=(a == 0), stop=(a == 7))
                    nt = c.tick(r, 'p7_t')
            c.s.wait_ge(c.sem('p7_t'), nt)
            if dst4 is None:
                r = c.s.activation(out=V4[:, j * 1024:(j + 1) * 1024], in_=po, func=AF.Identity)
            else:
                r = c.s.activation(out=dst4[:, j * 1024:(j + 1) * 1024], in_=po, func=AF.Identity)
            c.tick(r, 'p7_ev')

    def compute(i):
        j = i % 4
        q3 = Q4[:, j * 1024:(j + 1) * 1024].rearrange('p (h d) -> p h d', d=HD)
        S3 = S.rearrange('p (h t) -> p h t', t=NH)
        P3 = Pt.rearrange('p (h d) -> p h d', d=HD)
        O3 = OUT.rearrange('p (h d) -> p h d', d=HD)
        OB3 = OUTB.rearrange('p (h d) -> p h d', d=HD)
        nev = c.n('p7_ev')
        c.v.wait_ge(c.sem('p7_ev'), nev)
        c.g.wait_ge(c.sem('p7_ev'), nev)
        q4 = q3.unsqueeze(2).broadcast_to([128, NH, 2, HD])
        P4 = Pt.rearrange('p (h t d) -> p h t d', t=2, d=HD)
        for g2 in range(NH // 2):
            k4 = K4[:, j * 1024 + g2 * 2 * HD: j * 1024 + (g2 + 1) * 2 * HD] \
                .rearrange('p (t d) -> p t d', d=HD).unsqueeze(1).broadcast_to([128, NH, 2, HD])
            r = c.v.tensor_tensor(out=P4, in0=q4, in1=k4, op=OP.mult)
            nv = c.tick(r, 'p7_v')
            c.v.wait_ge(c.sem('p7_v'), nv)
            r = c.v.reduce_sum(out=S3[:, :, g2 * 2:(g2 + 1) * 2].unsqueeze(3), in_=P4, axis=AX.X)
            nv = c.tick(r, 'p7_v')
            c.v.wait_ge(c.sem('p7_v'), nv)
        r = c.v.reduce_max(out=MX.unsqueeze(2), in_=S3, axis=AX.X)
        nv = c.tick(r, 'p7_v')
        c.v.wait_ge(c.sem('p7_v'), nv)
        r = c.v.tensor_tensor(out=S3, in0=S3, in1=MX.unsqueeze(2).broadcast_to([128, NH, NH]),
                              op=OP.subtract)
        nv = c.tick(r, 'p7_v')
        c.s.wait_ge(c.sem('p7_v'), nv)
        r = c.s.activation(out=S, in_=S, func=AF.Exp, scale=0.125)
        ne = c.tick(r, 'p7_e')
        c.v.wait_ge(c.sem('p7_e'), ne)
        r = c.v.reduce_sum(out=SM.unsqueeze(2), in_=S3, axis=AX.X)
        nv = c.tick(r, 'p7_v')
        c.v.wait_ge(c.sem('p7_v'), nv)
        r = c.v.reciprocal(out=RC, in_=SM)
        nv = c.tick(r, 'p7_v')
        c.v.wait_ge(c.sem('p7_v'), nv)
        nexp = nv
        for t in range(NPV):
            e3 = S3[:, :, t:t + 1].broadcast_to([128, NH, HD])
            v3 = V4[:, j * 1024 + t * HD: j * 1024 + (t + 1) * HD].unsqueeze(1).broadcast_to([128, NH, HD])
            if t == 0:
                r = c.v.tensor_tensor(out=O3, in0=e3, in1=v3, op=OP.mult)
            else:
                r = c.v.tensor_tensor(out=TMP.rearrange('p (h d) -> p h d', d=HD), in0=e3, in1=v3, op=OP.mult)
            nv = c.tick(r, 'p7_v')
            c.v.wait_ge(c.sem('p7_v'), nv)
            if t > 0:
                r = c.v.tensor_tensor(out=OUT, in0=OUT, in1=TMP, op=OP.add)
                nv = c.tick(r, 'p7_v')
                c.v.wait_ge(c.sem('p7_v'), nv)
        c.g.wait_ge(c.sem('p7_v'), nexp)
        c.g.wait_ge(c.sem('p7_x'), c.n('p7_x'))
        for t in range(NPV, NH):
            e3 = S3[:, :, t:t + 1].broadcast_to([128, NH, HD])
            v3 = V4[:, j * 1024 + t * HD: j * 1024 + (t + 1) * HD].unsqueeze(1).broadcast_to([128, NH, HD])
            if t == NPV:
                r = c.g.tensor_tensor(out=OB3, in0=e3, in1=v3, op=OP.mult)
            else:
                r = c.g.tensor_tensor(out=TMPP.rearrange('p (h d) -> p h d', d=HD), in0=e3, in1=v3, op=OP.mult)
            np_ = c.tick(r, 'p7_p')
            c.g.wait_ge(c.sem('p7_p'), np_)
            if t > NPV:
                r = c.g.tensor_tensor(out=OUTB, in0=OUTB, in1=TMPP, op=OP.add)
                np_ = c.tick(r, 'p7_p')
                c.g.wait_ge(c.sem('p7_p'), np_)
        c.v.wait_ge(c.sem('p7_p'), c.n('p7_p'))
        r = c.v.tensor_tensor(out=O3, in0=O3, in1=OB3, op=OP.add)
        nx2 = c.tick(r, 'p7_x')
        c.v.wait_ge(c.sem('p7_x'), nx2)
        if i >= 2:
            c.v.wait_ge(c.sem('p7_tt'), 8 * (i - 1))
        a3 = abf[i % 2].rearrange('p (h d) -> p h d', d=HD)
        r = c.v.tensor_tensor(out=a3, in0=O3,
                              in1=RC.unsqueeze(2).broadcast_to([128, NH, HD]), op=OP.mult)
        ndv = c.tick(r, 'p7_v')
        c.t.wait_ge(c.sem('p7_v'), ndv)
        c.t.wait_ge(c.sem('p7_ev'), c.n('p7_ev'))
        for b in range(8):
            r = c.t.transpose(out=pB[:, b * 64:(b + 1) * 64].bitcast(BF16),
                              in_=abf[i % 2][:, b * 128:(b + 1) * 128], identity=idb[:])
            ntt = c.tick(r, 'p7_tt')
        c.v.wait_ge(c.sem('p7_tt'), ntt)
        c.v.wait_ge(c.sem('p7_t3'), c.n('p7_t3'))
        r = c.v.tensor_copy(out=afm, in_=pB[:, 0:512].bitcast(BF16))
        nea = c.tick(r, 'p7_ev')
        c.t.wait_ge(c.sem('p7_ev'), nea)
        if i >= 1:
            c.t.wait_ge(c.sem('p7_x'), c.n('p7_x'))
        for half in range(2):
            for a in range(8):
                r = c.t.matmul(out=pC[:, half * 512:(half + 1) * 512],
                               lhsT=afm[:, a * 128:(a + 1) * 128],
                               rhs=wall[:, 24576 + a * 1024 + half * 512: 24576 + a * 1024 + half * 512 + 512],
                               start=(a == 0), stop=(a == 7))
                nt3 = c.tick(r, 'p7_t3')
        if i >= 1:
            c.sy.wait_ge(c.sem('p7_x'), c.n('p7_x'))
        if i >= 2:
            c.sy.wait_ge(c.sem('p7_g%d' % (i % 2)), 16 * (i // 2))
        r = c.sy.dma_start(out=xres[i % 2], in_=x2.ap().rearrange('(t p) d -> t p d', p=128)[i])
        nlx = c.tick(r, 'p7_lx', 16)
        c.v.wait_ge(c.sem('p7_lx'), nlx)
        c.v.wait_ge(c.sem('p7_t3'), nt3)
        r = c.v.tensor_tensor(out=xres[i % 2], in0=xres[i % 2], in1=pC[:, 0:1024], op=OP.add)
        nx = c.tick(r, 'p7_x')
        c.g.wait_ge(c.sem('p7_x'), nx)
        r = c.g.dma_start(out=x3.ap().rearrange('(t p) d -> t p d', p=128)[i], in_=xres[i % 2])
        c.tick(r, 'p7_g%d' % (i % 2), 16)

    proj(0)
    for i in range(ntile):
        if i + 1 < ntile:
            proj(i + 1)
        compute(i)
    c.g.wait_ge(c.sem('p7_g0'), c.n('p7_g0'))
    c.g.wait_ge(c.sem('p7_g1'), c.n('p7_g1'))




def ones_row_bf(c):
    return c.const('ones_row_bf', np.ones((1, 128), dtype=ml_dtypes.bfloat16))


def phase_p8(c):
    seq = c.seq
    ntile = c.ntile
    x3 = c.dram('x3', [seq, D], F32)
    out_d = c.dram('out', [seq, D], F32)
    xn_fm = c.dram('xn_fm_l1', [D, seq], BF16)
    xn_tok = c.dram('xn_tok_l1', [seq, D], F32)
    wb_d = c.dram('L1_ffn_W_base', [D, D], F32)
    bb_d = c.dram('L1_ffn_b_base', [1, D], F32)
    wd1_d = c.dram('L1_ffn_Wd1', [NE, D, EH], F32)
    bd1_d = c.dram('L1_ffn_bd1', [NE, EH], F32)
    wd2_d = c.dram('L1_ffn_Wd2', [NE, EH, D], F32)
    bd2_d = c.dram('L1_ffn_bd2', [NE, D], F32)
    wad_d = c.dram('L1_ffn_Wa_down', [NE, D, RK], F32)
    bad_d = c.dram('L1_ffn_ba_down', [NE, RK], F32)
    wau_d = c.dram('L1_ffn_Wa_up', [NE, RK, D], F32)
    bau_d = c.dram('L1_ffn_ba_up', [NE, D], F32)

    b0 = c.tile('B0', [128, 32768], BF16)
    b1 = tile_b1(c)
    wbase = b0[:, 0:8192]
    wd1 = b0[:, 8192:24576]
    wad = b0[:, 24576:28672]
    bd1r = b0[0:1, 28672:30720]
    wd2 = b1[:, 0:16384]
    wau = b1[0:64, 16384:24576]
    bd2r = b1[0:1, 24576:32768]
    xfm = c.tile('B2', [128, 4096], BF16)[:, 0:1024]
    f1 = c.tile('F1', [128, 2048], F32)
    xtok2 = (f1[:, 0:1024], f1[:, 1024:2048])
    f0 = c.tile('F0', [128, 2048], F32)
    diff2 = (f0[:, 0:1024], f0[:, 1024:2048])
    sc1 = c.base_tile('SC1')
    sq_out = sc1[:, 0:1024]
    wad_acc = sc1[:, 1024:1536]
    wup_acc = sc1[0:64, 2048:3072]
    out2 = (sc1[:, 0:1024], sc1[:, 1024:2048])
    r1 = sc1[:, 4096:4608].bitcast(BF16)[0:1, 0:1024]
    wadb = sc1[:, 4608:4864].bitcast(BF16)[:, 0:512]
    wupb = sc1[:, 4864:5376].bitcast(BF16)[0:64, 0:1024]
    acc2 = sc1[:, 5376:5378]
    hB = c.tile('p8_h', [128, 512], BF16)
    hf = c.tile('p8_hf', [128, 512], BF16)
    aB = c.tile('p8_a', [128, 128], BF16)
    af = c.tile('p8_af', [64, 256], BF16)
    ls = c.tile('p8_ls', [128, 8], F32)
    lc = c.tile('p8_lc', [8, 1], F32)
    lr = c.tile('p8_lr', [1, 8], F32)
    mrow = c.tile('p8_m', [1, 1], F32)
    oh = c.tile('p8_oh', [1, 8], F32)
    ohb = c.tile('p8_ohb', [128, 8], F32)
    bad_acc = c.tile('p8_bad', [1, 64], F32)
    badb = c.tile('p8_badb', [1, 64], BF16)
    bau_acc = sc1[:, 3072:4096][0:1, 0:1024]
    baub = c.tile('B2', [128, 4096], BF16)[:, 1024:2048][0:1, 0:1024]
    bbb = c.tile('p8_bbb', [1, 1024], BF16)
    row0 = c.tile('B2', [128, 4096], BF16)[:, 2048:4096].bitcast(F32)[0:1, 0:1024]
    onb = c.tile('p8_onb', [1, 128], BF16)
    idt = c.tile('w0_id', [128, 128], BF16)
    idf = c.tile('p0_idf', [128, 128], F32)
    pA = c.psum('pA', [128, 2048], F32)
    pB = c.psum('pB', [128, 1024], F32)
    pC = c.psum('pC', [128, 1024], F32)

    r = c.g.dma_start(out=wbase.rearrange('p (a o) -> p a o', a=8),
                      in_=wb_d.ap().rearrange('(a p) o -> p a o', p=128))
    c.tick(r, 'p8_wg', 16)
    r = c.g.dma_start(out=wd1.rearrange('p (e a h) -> p e a h', e=NE, a=8),
                      in_=wd1_d.ap().rearrange('e (a p) h -> p e a h', p=128))
    c.tick(r, 'p8_wg', 16)
    r = c.g.dma_start(out=wad.rearrange('p (e a k) -> p e a k', e=NE, a=8),
                      in_=wad_d.ap().rearrange('e (a p) k -> p e a k', p=128))
    c.tick(r, 'p8_wg', 16)
    r = c.g.dma_start(out=wd2.rearrange('p (e hc o) -> p e hc o', e=NE, hc=2),
                      in_=wd2_d.ap().rearrange('e (hc p) o -> p e hc o', p=128))
    c.tick(r, 'p8_wg', 16)
    r = c.g.dma_start(out=wau.rearrange('p (e o) -> p e o', e=NE),
                      in_=wau_d.ap().rearrange('e k o -> k e o'))
    c.tick(r, 'p8_wg', 16)
    r = c.g.dma_start(out=bbb[:], in_=bb_d.ap())
    c.tick(r, 'p8_wg', 16)
    for e in range(NE):
        r = c.g.dma_start(out=bd1r[0:1, e * 256:(e + 1) * 256], in_=bd1_d.ap()[e:e + 1, :])
        c.tick(r, 'p8_wg', 16)
        r = c.g.dma_start(out=bd2r[0:1, e * 1024:(e + 1) * 1024], in_=bd2_d.ap()[e:e + 1, :])
        c.tick(r, 'p8_wg', 16)
    r = c.sy.dma_start(out=onb[:], in_=ones_row_bf(c).ap()); c.tick(r, 'p8_ld', 16)
    r = c.sy.dma_start(out=idt[:], in_=ident_bf16(c).ap()); c.tick(r, 'p8_ld', 16)
    r = c.sy.dma_start(out=idf[:], in_=ident_f32(c).ap()); c.tick(r, 'p8_ld', 16)
    onrt = c.tile('w0_onr', [1, 128], F32)
    r = c.sy.dma_start(out=onrt[:], in_=ones_row(c).ap()); c.tick(r, 'p8_ld', 16)
    r = c.v.memset(ls[:], 0.0); c.tick(r, 'p8_v')
    nwg = c.n('p8_wg')

    total = ntile * NE
    na_hist = {}
    nvd_hist = {}
    nld_ct = {}
    nlx_ct = {}
    nsq_hist = {}
    nwd1_ct = {}

    def stage_a(it):
        ct, e = divmod(it, NE)
        q = it % 2
        if e == 0:
            if ct >= 1:
                c.sy.wait_ge(c.sem('p8_t'), nwd1_ct[ct - 1])
            if ct >= 2:
                c.sy.wait_ge(c.sem('p8_v'), nvd_hist[(ct - 2) * NE + NE - 1])
            r = c.sy.dma_start(out=xfm.rearrange('p (a w) -> p a w', a=8),
                               in_=xn_fm.ap().rearrange('(a p) s -> p a s', p=128)[:, :, ct * 128:(ct + 1) * 128])
            nld_ct[ct] = c.tick(r, 'p8_ld', 16)
            r = c.sy.dma_start(out=xtok2[ct % 2], in_=xn_tok.ap().rearrange('(t p) d -> t p d', p=128)[ct])
            nlx_ct[ct] = c.tick(r, 'p8_lx%d' % (ct % 2), 16)
        off = 0 if q == 0 else 512
        c.t.wait_ge(c.sem('p8_ld'), nld_ct[ct])
        if it == 0:
            c.t.wait_ge(c.sem('p8_wg'), nwg)
        if it >= 2:
            c.t.wait_ge(c.sem('p8_a'), na_hist[it - 2])
        for a in range(8):
            r = c.t.matmul(out=pA[:, off:off + 256], lhsT=xfm[:, a * 128:(a + 1) * 128],
                           rhs=wd1[:, (e * 8 + a) * 256:(e * 8 + a + 1) * 256],
                           start=(a == 0), stop=False)
            c.tick(r, 'p8_t')
        r = c.t.matmul(out=pA[:, off:off + 256], lhsT=onb[:], rhs=bd1r[0:1, e * 256:(e + 1) * 256],
                       start=False, stop=True)
        nt = c.tick(r, 'p8_t')
        if e == NE - 1:
            nwd1_ct[ct] = nt
        c.s.wait_ge(c.sem('p8_t'), nt)
        r = c.s.activation(out=hB[:, q * 256:(q + 1) * 256], in_=pA[:, off:off + 256], func=AF.Relu)
        na_hist[it] = c.tick(r, 'p8_a')

    def stage_b(it):
        ct, e = divmod(it, NE)
        q = it % 2
        c.t.wait_ge(c.sem('p8_a'), na_hist[it])
        for b in range(2):
            gx = c.aux.get('p8_tt', 0); c.aux['p8_tt'] = gx + 1
            slot = gx % 2
            pv = pC[:, slot * 512:slot * 512 + 128].bitcast(BF16)[:, 0:128]
            if gx >= 2:
                c.t.wait_ge(c.sem('p8_ev'), c.n('p8_ev') - 1)
            r = c.t.transpose(out=pv, in_=hB[:, q * 256 + b * 128:q * 256 + (b + 1) * 128], identity=idt[:])
            ntt = c.tick(r, 'p8_t')
            c.v.wait_ge(c.sem('p8_t'), ntt)
            r = c.v.tensor_copy(out=hf[:, q * 256 + b * 128:q * 256 + (b + 1) * 128], in_=pv)
            c.tick(r, 'p8_ev')
        rec = pB[:, 0:1024] if q == 0 else pA[:, 1024:2048]
        c.t.wait_ge(c.sem('p8_ev'), c.n('p8_ev'))
        if it >= 2:
            c.t.wait_ge(c.sem('p8_v'), nvd_hist[it - 2])
        for half in range(2):
            for hc in range(2):
                r = c.t.matmul(out=rec[:, half * 512:half * 512 + 512],
                               lhsT=hf[:, q * 256 + hc * 128:q * 256 + (hc + 1) * 128],
                               rhs=wd2[:, (e * 2 + hc) * 1024 + half * 512:(e * 2 + hc) * 1024 + half * 512 + 512],
                               start=(hc == 0), stop=False)
                c.tick(r, 'p8_t')
            r = c.t.matmul(out=rec[:, half * 512:half * 512 + 512], lhsT=onb[:],
                           rhs=bd2r[0:1, e * 1024 + half * 512:e * 1024 + half * 512 + 512],
                           start=False, stop=True)
            nt = c.tick(r, 'p8_t')
        c.v.wait_ge(c.sem('p8_t'), nt)
        c.v.wait_ge(c.sem('p8_lx%d' % (ct % 2)), nlx_ct[ct])
        if it >= 2:
            c.v.wait_ge(c.sem('p8_a'), nsq_hist[it - 2])
        r = c.v.tensor_tensor(out=diff2[q], in0=rec, in1=xtok2[ct % 2], op=OP.subtract)
        nvd_hist[it] = nv = c.tick(r, 'p8_v')
        c.s.wait_ge(c.sem('p8_v'), nv)
        r = c.s.activation(out=sq_out, in_=diff2[q], func=AF.Square, accum_out=acc2[:, q:q + 1])
        nsq_hist[it] = na = c.tick(r, 'p8_a')
        c.v.wait_ge(c.sem('p8_a'), na)
        r = c.v.tensor_tensor(out=ls[:, e:e + 1], in0=ls[:, e:e + 1], in1=acc2[:, q:q + 1], op=OP.add)
        c.tick(r, 'p8_v')

    for it in range(total + 1):
        if it < total:
            stage_a(it)
        if it >= 1:
            stage_b(it - 1)

    c.t.wait_ge(c.sem('p8_v'), c.n('p8_v'))
    c.t.wait_ge(c.sem('p8_ld'), c.n('p8_ld'))
    r = c.t.matmul(out=pC[0:8, 0:128], lhsT=ls[:], rhs=idf[:], start=True, stop=True)
    nt = c.tick(r, 'p8_t2')
    c.v.wait_ge(c.sem('p8_t2'), nt)
    r = c.v.reduce_sum(out=lc[:], in_=pC[0:8, 0:128], axis=AX.X)
    nv = c.tick(r, 'p8_v')
    c.t.wait_ge(c.sem('p8_v'), nv)
    r = c.t.matmul(out=pC[0:1, 512:520], lhsT=lc[:], rhs=idf[0:8, 0:8], start=True, stop=True)
    nt = c.tick(r, 'p8_t2')
    c.v.wait_ge(c.sem('p8_t2'), nt)
    r = c.v.tensor_copy(out=lr[:], in_=pC[0:1, 512:520])
    nv = c.tick(r, 'p8_v')
    c.v.wait_ge(c.sem('p8_v'), nv)
    r = c.v.tensor_reduce(out=mrow[:], in_=lr[:], axis=AX.X, op=OP.min)
    nv = c.tick(r, 'p8_v')
    c.v.wait_ge(c.sem('p8_v'), nv)
    r = c.v.tensor_single_scalar(out=oh[:], in_=lr[:], scalar=mrow[0:1, 0:1], op=OP.is_equal)
    nv = c.tick(r, 'p8_v')
    c.t.wait_ge(c.sem('p8_v'), nv)
    r = c.t.matmul(out=pC[0:128, 520:528], lhsT=onrt[:], rhs=oh[:], start=True, stop=True)
    nt = c.tick(r, 'p8_t2')
    c.v.wait_ge(c.sem('p8_t2'), nt)
    r = c.v.tensor_copy(out=ohb[:], in_=pC[0:128, 520:528])
    nv = c.tick(r, 'p8_v')
    for e in range(NE):
        c.v.wait_ge(c.sem('p8_v'), c.n('p8_v'))
        if e == 0:
            r = c.v.tensor_scalar(out=wad_acc, in0=wad[:, 0:512], scalar1=ohb[:, 0:1],
                                  scalar2=None, op0=OP.mult)
        else:
            r = c.v.scalar_tensor_tensor(out=wad_acc, in0=wad[:, e * 512:(e + 1) * 512],
                                         scalar=ohb[:, e:e + 1], in1=wad_acc,
                                         op0=OP.mult, op1=OP.add)
        c.tick(r, 'p8_v')
        c.v.wait_ge(c.sem('p8_v'), c.n('p8_v'))
        if e == 0:
            r = c.v.tensor_scalar(out=wup_acc, in0=wau[0:64, 0:1024], scalar1=ohb[0:64, 0:1],
                                  scalar2=None, op0=OP.mult)
        else:
            r = c.v.scalar_tensor_tensor(out=wup_acc, in0=wau[0:64, e * 1024:(e + 1) * 1024],
                                         scalar=ohb[0:64, e:e + 1], in1=wup_acc,
                                         op0=OP.mult, op1=OP.add)
        c.tick(r, 'p8_v')
        c.sy.wait_ge(c.sem('p8_v'), c.n('p8_v'))
        c.sy.wait_ge(c.sem('p8_ld'), c.n('p8_ld'))
        r = c.sy.dma_start(out=row0[0:1, 0:64], in_=bad_d.ap()[e:e + 1, :])
        nl = c.tick(r, 'p8_ld', 16)
        c.v.wait_ge(c.sem('p8_ld'), nl)
        if e == 0:
            r = c.v.tensor_single_scalar(out=bad_acc[:], in_=row0[0:1, 0:64],
                                         scalar=oh[0:1, 0:1], op=OP.mult)
        else:
            r = c.v.scalar_tensor_tensor(out=bad_acc[:], in0=row0[0:1, 0:64],
                                         scalar=oh[0:1, e:e + 1], in1=bad_acc[:],
                                         op0=OP.mult, op1=OP.add)
        c.tick(r, 'p8_v')
        c.sy.wait_ge(c.sem('p8_v'), c.n('p8_v'))
        c.sy.wait_ge(c.sem('p8_ld'), c.n('p8_ld'))
        r = c.sy.dma_start(out=row0, in_=bau_d.ap()[e:e + 1, :])
        nl = c.tick(r, 'p8_ld', 16)
        c.v.wait_ge(c.sem('p8_ld'), nl)
        if e == 0:
            r = c.v.tensor_single_scalar(out=bau_acc, in_=row0,
                                         scalar=oh[0:1, 0:1], op=OP.mult)
        else:
            r = c.v.scalar_tensor_tensor(out=bau_acc, in0=row0,
                                         scalar=oh[0:1, e:e + 1], in1=bau_acc,
                                         op0=OP.mult, op1=OP.add)
        c.tick(r, 'p8_v')
    c.v.wait_ge(c.sem('p8_v'), c.n('p8_v'))
    r = c.v.tensor_copy(out=wadb, in_=wad_acc); c.tick(r, 'p8_v')
    r = c.v.tensor_copy(out=wupb, in_=wup_acc); c.tick(r, 'p8_v')
    r = c.v.tensor_copy(out=badb[:], in_=bad_acc[:]); c.tick(r, 'p8_v')
    r = c.v.tensor_copy(out=baub, in_=bau_acc); nvsel = c.tick(r, 'p8_v')

    fa_hist = {}; fld_hist = {}; flx_hist = {}; fout_hist = {}; fnt_hist = {}
    c.sy.wait_ge(c.sem('p8_ld'), c.n('p8_ld'))

    def fstage_a(ct):
        q = ct % 2
        if ct == 0:
            c.sy.wait_ge(c.sem('p8_v'), nvsel)
        if ct >= 1:
            c.sy.wait_ge(c.sem('p8_t'), fnt_hist[ct - 1])
        if ct >= 2:
            c.sy.wait_ge(c.sem('p8_v'), fout_hist[ct - 2])
        r = c.sy.dma_start(out=xfm.rearrange('p (a w) -> p a w', a=8),
                           in_=xn_fm.ap().rearrange('(a p) s -> p a s', p=128)[:, :, ct * 128:(ct + 1) * 128])
        nld = c.tick(r, 'p8_ld', 16)
        r = c.sy.dma_start(out=xtok2[q], in_=x3.ap().rearrange('(t p) d -> t p d', p=128)[ct])
        flx_hist[ct] = c.tick(r, 'p8_lx%d' % q, 16)
        fld_hist[ct] = nld
        base = pA[:, 0:1024] if q == 0 else pA[:, 1024:2048]
        ada = pC[:, 0:64] if q == 0 else pC[:, 512:576]
        c.t.wait_ge(c.sem('p8_ld'), nld)
        if ct == 0:
            c.t.wait_ge(c.sem('p8_v'), nvsel)
        if ct >= 2:
            c.t.wait_ge(c.sem('p8_v'), fout_hist[ct - 2])
        for half in range(2):
            for a in range(8):
                r = c.t.matmul(out=base[:, half * 512:half * 512 + 512],
                               lhsT=xfm[:, a * 128:(a + 1) * 128],
                               rhs=wbase[:, a * 1024 + half * 512: a * 1024 + half * 512 + 512],
                               start=(a == 0), stop=False)
                c.tick(r, 'p8_t')
            r = c.t.matmul(out=base[:, half * 512:half * 512 + 512],
                           lhsT=onb[:], rhs=bbb[0:1, half * 512:(half + 1) * 512],
                           start=False, stop=True)
            c.tick(r, 'p8_t')
        if ct >= 2:
            c.t.wait_ge(c.sem('p8_a'), fa_hist[ct - 2])
        for a in range(8):
            r = c.t.matmul(out=ada, lhsT=xfm[:, a * 128:(a + 1) * 128],
                           rhs=wadb[:, a * 64:(a + 1) * 64], start=(a == 0), stop=False)
            c.tick(r, 'p8_t')
        r = c.t.matmul(out=ada, lhsT=onb[:], rhs=badb[:], start=False, stop=True)
        nt = c.tick(r, 'p8_t')
        fnt_hist[ct] = nt
        c.s.wait_ge(c.sem('p8_t'), nt)
        r = c.s.activation(out=aB[:, q * 64:(q + 1) * 64], in_=ada, func=AF.Relu)
        fa_hist[ct] = c.tick(r, 'p8_a')

    def fstage_b(ct):
        q = ct % 2
        base = pA[:, 0:1024] if q == 0 else pA[:, 1024:2048]
        c.t.wait_ge(c.sem('p8_a'), fa_hist[ct])
        pv = pC[:, 64 + q * 512:64 + q * 512 + 64].bitcast(BF16)[0:64, 0:128]
        r = c.t.transpose(out=pv, in_=aB[:, q * 64:(q + 1) * 64], identity=idt[:])
        nt = c.tick(r, 'p8_t')
        c.v.wait_ge(c.sem('p8_t'), nt)
        r = c.v.tensor_copy(out=af[0:64, q * 128:(q + 1) * 128], in_=pv)
        nv2 = c.tick(r, 'p8_ev')
        c.t.wait_ge(c.sem('p8_ev'), nv2)
        if ct >= 1:
            c.t.wait_ge(c.sem('p8_v'), fout_hist[ct - 1])
        for half in range(2):
            r = c.t.matmul(out=pB[:, half * 512:(half + 1) * 512], lhsT=af[0:64, q * 128:(q + 1) * 128],
                           rhs=wupb[0:64, half * 512:(half + 1) * 512], start=True, stop=False)
            c.tick(r, 'p8_t')
            r = c.t.matmul(out=pB[:, half * 512:(half + 1) * 512], lhsT=onb[:],
                           rhs=baub[0:1, half * 512:(half + 1) * 512], start=False, stop=True)
            nt = c.tick(r, 'p8_t')
        c.v.wait_ge(c.sem('p8_t'), nt)
        c.v.wait_ge(c.sem('p8_lx%d' % q), flx_hist[ct])
        if ct >= 2:
            c.v.wait_ge(c.sem('p8_g%d' % q), 16 * (ct // 2))
        r = c.v.tensor_copy(out=diff2[q], in_=base)
        nv2 = c.tick(r, 'p8_v')
        c.v.wait_ge(c.sem('p8_v'), nv2)
        r = c.v.tensor_tensor(out=diff2[q], in0=diff2[q], in1=pB[:, 0:1024], op=OP.add)
        nv2 = c.tick(r, 'p8_v')
        c.v.wait_ge(c.sem('p8_v'), nv2)
        r = c.v.tensor_tensor(out=out2[q], in0=xtok2[q], in1=diff2[q], op=OP.add)
        fout_hist[ct] = nx = c.tick(r, 'p8_v')
        c.g.wait_ge(c.sem('p8_v'), nx)
        r = c.g.dma_start(out=out_d.ap().rearrange('(t p) d -> t p d', p=128)[ct],
                          in_=out2[q])
        c.tick(r, 'p8_g%d' % q, 16)

    for ct in range(ntile + 1):
        if ct < ntile:
            fstage_a(ct)
        if ct >= 1:
            fstage_b(ct - 1)
    c.g.wait_ge(c.sem('p8_g0'), c.n('p8_g0'))
    c.g.wait_ge(c.sem('p8_g1'), c.n('p8_g1'))


PHASES = {'w0': phase_w0, 'w1': phase_w1, 'p0': phase_p0, 'p1': phase_p1,
          'p2': phase_p2, 'p3': phase_p3, 'p4a': phase_p4a, 'p4b': phase_p4b,
          'p5': phase_p5, 'p7': phase_p7, 'p8': phase_p8}



def build_program(phase_names, seq, ext_in, ext_out, n_cores=NCORES):
    nc = bass.Bass('TRN2', target_bir_lowering=False)
    c = Ctx(nc, seq, ext_in, ext_out, n_cores)
    c.prologue()
    for i, pn in enumerate(phase_names):
        PHASES[pn](c)
        if i != len(phase_names) - 1:
            c.barrier()
    c.replay()
    return nc, c


ALL_PHASES = ['w0', 'w1', 'p0', 'p1', 'p2', 'p3', 'p4a', 'p4b', 'p5', 'p7', 'p8']
REAL_INPUTS = ['L0_attn_Wq', 'L0_attn_Wk', 'L0_attn_Wv', 'L0_attn_Wo', 'L0_attn_norm_const',
               'L0_ffn_w_gate', 'L0_ffn_w_up', 'L0_ffn_w_down',
               'L1_attn_Wq', 'L1_attn_Wk', 'L1_attn_Wv', 'L1_attn_Wfc',
               'L1_ffn_W_base', 'L1_ffn_b_base', 'L1_ffn_Wd1', 'L1_ffn_bd1',
               'L1_ffn_Wd2', 'L1_ffn_bd2', 'L1_ffn_Wa_down', 'L1_ffn_ba_down',
               'L1_ffn_Wa_up', 'L1_ffn_ba_up']


def kernel(**inputs):
    seq = 4096
    x = np.ascontiguousarray(inputs['L0_attn_x'], dtype=np.float32)
    nb = x.shape[0]
    weights = {}
    for nm in REAL_INPUTS:
        v = np.ascontiguousarray(inputs[nm], dtype=np.float32)
        if nm == 'L0_attn_norm_const':
            v = v.reshape(1, NH)
        elif nm == 'L1_ffn_b_base':
            v = v.reshape(1, D)
        weights[nm] = v
    nc, c = build_program(ALL_PHASES, seq, set(REAL_INPUTS + ['x0']), {'out'}, NCORES)
    in_maps = []
    for i in range(NCORES):
        d = dict(weights)
        d['x0'] = x[i]
        in_maps.append(d)
    res = run_bass_kernel_spmd(nc, in_maps, core_ids=list(range(NCORES)))
    out = np.stack([np.asarray(res.results[i]['out'], dtype=np.float32)
                    for i in range(NCORES)])
    return out
